# version 08
# speedup vs baseline: 1.6197x; 1.0530x over previous
import contextlib
import numpy as np
import concourse.bass as bass
import concourse.mybir as mybir
from concourse.bass_utils import run_bass_kernel_spmd

F32 = mybir.dt.float32
BF16 = mybir.dt.bfloat16
AF = mybir.ActivationFunctionType
ALU = mybir.AluOpType
AX = mybir.AxisListType

NCH = 8
NCORES = 8


ENGINES = ("tensor", "vector", "scalar", "gpsimd", "sync")


class _Op:
    __slots__ = ("idx", "engine", "emit", "is_dma", "stream", "seq", "deps",
                 "waits", "pre_drain", "needed", "ord", "tag", "cclock")

    def __init__(self, idx, engine, emit, is_dma, stream, tag):
        self.idx = idx
        self.engine = engine
        self.emit = emit
        self.is_dma = is_dma
        self.stream = stream
        self.seq = 0
        self.deps = []
        self.waits = []
        self.pre_drain = False
        self.needed = False
        self.ord = None
        self.tag = tag
        self.cclock = None


def _merge(dst, src):
    for k, v in src.items():
        if dst.get(k, -1) < v:
            dst[k] = v


class Prog:
    def __init__(self, nc, ctx):
        self.nc = nc
        self.ctx = ctx
        self.ops = []
        self.last_writer = {}
        self.readers = {}
        self.streams = {}
        self.stream_sems = {}
        self.engine_sems = {}
        self._emitted = False
        self.n_drains = 0
        self.n_waits = 0

    def _op(self, engine, emit, reads, writes, is_dma=False, stream=None, tag=""):
        assert not self._emitted
        idx = len(self.ops)
        op = _Op(idx, engine, emit, is_dma, stream, tag)
        if is_dma:
            if stream is None:
                ks = list(writes) + list(reads)
                assert ks, f"dma op needs reads/writes or stream ({tag})"
                stream = ks[0]
            op.stream = stream
            self.streams[stream] = self.streams.get(stream, 0) + 1
            op.seq = self.streams[stream]
        deps = set()
        for k in reads:
            w = self.last_writer.get(k)
            if w is not None:
                deps.add(w)
        for k in writes:
            w = self.last_writer.get(k)
            if w is not None:
                deps.add(w)
            for r in self.readers.get(k, ()):
                deps.add(r)
        op.deps = sorted(deps)
        for k in reads:
            self.readers.setdefault(k, []).append(idx)
        for k in writes:
            self.last_writer[k] = idx
            self.readers[k] = []
        self.ops.append(op)
        return idx

    def t(self, emit, reads=(), writes=(), tag=""):
        return self._op("tensor", emit, reads, writes, tag=tag)

    def v(self, emit, reads=(), writes=(), tag=""):
        return self._op("vector", emit, reads, writes, tag=tag)

    def s(self, emit, reads=(), writes=(), tag=""):
        return self._op("scalar", emit, reads, writes, tag=tag)

    def g(self, emit, reads=(), writes=(), tag=""):
        return self._op("gpsimd", emit, reads, writes, tag=tag)

    def dma(self, queue, emit, reads=(), writes=(), stream=None, tag=""):
        return self._op(queue, emit, reads, writes, is_dma=True, stream=stream, tag=tag)

    def wait_all(self, engine, keys, tag="final"):
        return self._op(engine, lambda e: e.nop(), keys, (), tag=tag)

    def full_barrier(self, tag="bar"):
        self._barn = getattr(self, "_barn", 0) + 1
        n = self._barn
        keys = list(self.last_writer.keys())
        stage = []
        for e in ENGINES:
            k = f"__bar{n}_{e}"
            self._op(e, lambda e_: e_.nop(), keys, [k], tag=f"{tag}{n}a")
            stage.append(k)
        for e in ENGINES:
            self._op(e, lambda e_: e_.nop(), stage, (), tag=f"{tag}{n}b")

    def _resolve(self):
        know = {e: {} for e in ENGINES}
        drainmark = {e: -1 for e in ENGINES}
        lastop = {e: -1 for e in ENGINES}
        sindex = {}
        created = {}
        for op in self.ops:
            if op.is_dma:
                sindex[(op.stream, op.seq)] = op
        for op in self.ops:
            E = op.engine
            my_know = know[E]
            sem_waits = []
            stream_waits = {}
            need_drain = False
            for d in op.deps:
                p = self.ops[d]
                if p.is_dma:
                    if my_know.get(("s", p.stream), 0) >= p.seq:
                        continue
                    sw = stream_waits.get(p.stream, 0)
                    stream_waits[p.stream] = max(sw, created[p.stream])
                elif p.engine == E:
                    if E == "tensor":
                        continue
                    if p.idx <= drainmark[E]:
                        continue
                    if my_know.get(("e", E), -1) >= p.idx:
                        continue
                    need_drain = True
                else:
                    if my_know.get(("e", p.engine), -1) >= p.idx:
                        continue
                    sem_waits.append(p)
            if need_drain:
                op.pre_drain = True
                self.n_drains += 1
                drainmark[E] = lastop[E]
            waits = []
            for s, seq in stream_waits.items():
                waits.append(("s", s, seq))
                my_know[("s", s)] = max(my_know.get(("s", s), 0), seq)
                dop = sindex.get((s, seq))
                if dop is not None and dop.cclock:
                    _merge(my_know, dop.cclock)
            for pr in sem_waits:
                pr.needed = True
                waits.append(("e", pr.engine, pr.idx))
                my_know[("e", pr.engine)] = max(my_know.get(("e", pr.engine), -1), pr.idx)
                if pr.cclock:
                    _merge(my_know, pr.cclock)
            op.waits = waits
            self.n_waits += len(waits)
            cc = dict(my_know)
            if op.is_dma:
                cc[("s", op.stream)] = max(cc.get(("s", op.stream), 0), op.seq)
                created[op.stream] = created.get(op.stream, 0) + 1
            else:
                cc[("e", E)] = op.idx
            op.cclock = cc
            lastop[E] = op.idx

        counters = {e: 0 for e in ENGINES}
        ordinals = {}
        for op in self.ops:
            if not op.is_dma and op.needed:
                counters[op.engine] += 1
                op.ord = counters[op.engine]
                ordinals[op.idx] = op.ord
        for op in self.ops:
            fixed = []
            for kind, name, val in op.waits:
                if kind == "e":
                    fixed.append(("e", self.ops[val].engine, ordinals[val]))
                else:
                    fixed.append(("s", name, 16 * val))
            op.waits = fixed
        for e in ENGINES:
            if counters[e] > 0:
                self.engine_sems[e] = self.ctx.enter_context(
                    self.nc.semaphore(f"pe_{e}"))
        for s in self.streams:
            self.stream_sems[s] = self.ctx.enter_context(
                self.nc.semaphore(f"st_{s}"[:30]))
        n_sems = len(self.engine_sems) + len(self.stream_sems)
        assert n_sems <= 90, f"too many semaphores: {n_sems}"

    def emit(self):
        self._resolve()
        self._emitted = True

        def section(engine_name):
            def body(eng):
                for op in self.ops:
                    if op.engine != engine_name:
                        continue
                    if op.pre_drain:
                        eng.drain()
                    for kind, name, val in op.waits:
                        if val <= 0:
                            continue
                        sem = (self.engine_sems[name] if kind == "e"
                               else self.stream_sems[name])
                        eng.wait_ge(sem, val)
                    inst = op.emit(eng)
                    if op.is_dma:
                        inst.then_inc(self.stream_sems[op.stream], 16)
                    elif op.needed:
                        inst.then_inc(self.engine_sems[op.engine], 1)
            return body

        with self.nc.Block() as block:
            for name in ENGINES:
                if not any(op.engine == name for op in self.ops):
                    continue
                getattr(block, name)(section(name))

    def stats(self):
        from collections import Counter
        c = Counter(op.engine for op in self.ops)
        d = Counter(op.engine for op in self.ops if op.is_dma)
        return (f"ops={len(self.ops)} per-engine={dict(c)} dma={dict(d)} "
                f"waits={self.n_waits} drains={self.n_drains} "
                f"streams={len(self.streams)}")



NFCONST = 32


class LayerCtx:
    def __init__(self, nc, T):
        self.nc = nc
        self.T = T
        self.ctx = contextlib.ExitStack()
        self.root_ctx = self.ctx
        self.p = Prog(nc, self.root_ctx)
        self._fconsts = {}
        self.ns = ""

    def sbuf(self, name, shape, dtype):
        return self.ctx.enter_context(
            self.nc.sbuf_tensor(self.ns + name, list(shape), dtype))

    def psum(self, name, shape, dtype):
        return self.ctx.enter_context(
            self.nc.psum_tensor(self.ns + name, list(shape), dtype))

    def cap(self, v):
        v = float(v)
        if v not in self._fconsts:
            i = len(self._fconsts)
            assert i < NFCONST
            self._fconsts[v] = i
            self.nc.const_aps.register(F32, v, self.fc[0:1, i:i + 1])
        return self._fconsts[v]

    def fconst_values(self):
        vals = np.zeros((1, NFCONST), np.float32)
        for v, i in self._fconsts.items():
            vals[0, i] = v
        return vals

    def setup_base(self, fconst_ap, n_f32_banks=6, n_bf16_banks=2):
        p, T = self.p, self.T
        self.pb = [self.psum(f"pb{i}", [128, 512], F32) for i in range(n_f32_banks)]
        self.pbt2 = [self.psum(f"pbt{i}", [128, 1024], BF16)
                     for i in range(n_bf16_banks)]
        self.pbt = self.pbt2[0] if self.pbt2 else None
        self.fc = self.sbuf("fc_pool", [1, NFCONST], F32)
        p.dma("sync", lambda e: e.dma_start(out=self.fc[:], in_=fconst_ap),
              writes=["fconst"], tag="ld_fconst")
        self.cst = self.sbuf("cst_sb", [128, 8], F32)
        self.ones_row = self.sbuf("ones_row_sb", [1, 128], F32)
        self.ones_row_bf = self.sbuf("ones_row_bf", [1, T], BF16)
        p.v(lambda e: e.memset(self.cst[:, 0:1], 1.0), writes=["cst"], tag="ms_cst")
        p.v(lambda e: e.memset(self.ones_row[:], 1.0), writes=["ones_row"], tag="ms_ones")
        p.v(lambda e: e.memset(self.ones_row_bf[:], 1.0), writes=["ones_row_bf"],
            tag="ms_onesbf")
        self.ones_col = self.cst[:, 0:1]

    def load_x(self, xin_ap):
        p, T = self.p, self.T
        self.x = self.sbuf("x_fm", [128, NCH * T], F32)
        for a in range(NCH):
            p.dma("sync", lambda e, a=a: e.dma_start(
                out=self.x[:, a * T:(a + 1) * T],
                in_=xin_ap[a * 128:(a + 1) * 128, :]),
                writes=[f"x:{a}"], stream="x_load", tag=f"load_x{a}")

    def store_x(self, xout_ap):
        p, T = self.p, self.T
        for a in range(NCH):
            p.dma("sync", lambda e, a=a: e.dma_start(
                out=xout_ap[a * 128:(a + 1) * 128, :],
                in_=self.x[:, a * T:(a + 1) * T]),
                reads=[f"x:{a}"], writes=[f"xout:{a}"], stream="x_store",
                tag=f"store_x{a}")
        p.wait_all("sync", [f"xout:{a}" for a in range(NCH)], tag="final_store")

    def layernorm(self):
        nc, p, T = self.nc, self.p, self.T
        x, pb = self.x, self.pb
        ntt = (T + 511) // 512
        assert ntt <= 4
        xn = self.sbuf("xn_bf", [128, NCH * T], BF16)
        x2 = self.sbuf("ln_x2", [128, 512], F32)
        stat2 = self.sbuf("ln_stats", [1, 2 * T], F32)
        stat_m = stat2[0:1, 0:T]
        stat_q = stat2[0:1, T:2 * T]
        tts = t_tiles(T)
        for a in range(NCH):
            for ti, (t0, tw) in enumerate(tts):
                p.s(lambda e, a=a, t0=t0, tw=tw: e.activation(
                    out=x2[:, 0:tw], in_=x[:, a * T + t0:a * T + t0 + tw],
                    func=AF.Square, bias=0.0, scale=1.0),
                    reads=[f"x:{a}"], writes=["ln_x2"], tag=f"ln_sq{a}")
                p.t(lambda e, a=a, t0=t0, tw=tw, ti=ti: e.matmul(
                    out=pb[ti][0:1, 0:tw], lhsT=self.ones_col,
                    rhs=x[:, a * T + t0:a * T + t0 + tw],
                    start=(a == 0), stop=(a == NCH - 1)),
                    reads=[f"x:{a}", "cst"], writes=[f"pb{ti}"], tag="ln_mm")
                p.t(lambda e, a=a, tw=tw, ti=ti: e.matmul(
                    out=pb[4 + ti][0:1, 0:tw], lhsT=self.ones_col,
                    rhs=x2[:, 0:tw],
                    start=(a == 0), stop=(a == NCH - 1)),
                    reads=["ln_x2", "cst"], writes=[f"pb{4 + ti}"], tag="ln_mq")
        for ti, (t0, tw) in enumerate(tts):
            p.s(lambda e, ti=ti, t0=t0, tw=tw: e.copy(
                out=stat_m[:, t0:t0 + tw], in_=pb[ti][0:1, 0:tw]),
                reads=[f"pb{ti}"], writes=["ln_m"], tag="ln_evm")
            p.s(lambda e, ti=ti, t0=t0, tw=tw: e.copy(
                out=stat_q[:, t0:t0 + tw], in_=pb[4 + ti][0:1, 0:tw]),
                reads=[f"pb{4 + ti}"], writes=["ln_q"], tag="ln_evq")
        self.cap(1.0 / 1024.0)
        self.cap(1e-5)
        self.cap(-0.5)
        for ti, (t0, tw) in enumerate(tts):
            p.s(lambda e, ti=ti, t0=t0, tw=tw: e.activation(
                out=pb[ti][0:1, 0:tw], in_=stat_m[:, t0:t0 + tw],
                func=AF.Square, bias=0.0, scale=1.0 / 1024.0),
                reads=["ln_m", "fconst"], writes=[f"pb{ti}"], tag="ln_m2")
            p.v(lambda e, ti=ti, t0=t0, tw=tw: e.scalar_tensor_tensor(
                out=stat_q[:, t0:t0 + tw], in0=stat_q[:, t0:t0 + tw],
                scalar=1.0 / 1024.0, in1=pb[ti][0:1, 0:tw],
                op0=ALU.mult, op1=ALU.subtract),
                reads=["ln_q", f"pb{ti}"], writes=["ln_q"], tag="ln_var")
        p.s(lambda e: e.activation(out=stat_q[:], in_=stat_q[:], func=AF.Ln,
                                   bias=1e-5, scale=1.0),
            reads=["ln_q", "fconst"], writes=["ln_q"], tag="ln_ln")
        p.s(lambda e: e.activation(out=stat_q[:], in_=stat_q[:], func=AF.Exp,
                                   bias=0.0, scale=-0.5),
            reads=["ln_q", "fconst"], writes=["ln_q"], tag="ln_exp")
        p.v(lambda e: e.scalar_tensor_tensor(
            out=stat_m[:], in0=stat_m[:], scalar=1.0 / 1024.0, in1=stat_q[:],
            op0=ALU.mult, op1=ALU.mult),
            reads=["ln_m", "ln_q"], writes=["ln_m"], tag="ln_musc")
        for ti, (t0, tw) in enumerate(tts):
            p.t(lambda e, ti=ti, t0=t0, tw=tw: e.matmul(
                out=pb[ti][:, 0:tw], lhsT=self.ones_row[0:1, 0:128],
                rhs=stat_q[:, t0:t0 + tw], start=True, stop=True),
                reads=["ln_q", "ones_row"], writes=[f"pb{ti}"], tag="ln_bcr")
            p.t(lambda e, ti=ti, t0=t0, tw=tw: e.matmul(
                out=pb[4 + ti][:, 0:tw], lhsT=self.ones_row[0:1, 0:128],
                rhs=stat_m[:, t0:t0 + tw], start=True, stop=True),
                reads=["ln_m", "ones_row"], writes=[f"pb{4 + ti}"], tag="ln_bcm")
        tmp = self.sbuf("ln_tmp", [128, 512], F32)
        self.xn_scratch_a = x2
        self.xn_scratch_b = tmp
        for a in range(NCH):
            for ti, (t0, tw) in enumerate(tts):
                p.v(lambda e, a=a, ti=ti, t0=t0, tw=tw: e.tensor_tensor(
                    out=tmp[:, 0:tw], in0=x[:, a * T + t0:a * T + t0 + tw],
                    in1=pb[ti][:, 0:tw], op=ALU.mult),
                    reads=[f"x:{a}", f"pb{ti}"], writes=["ln_tmp"], tag="ln_xr")
                p.v(lambda e, a=a, ti=ti, t0=t0, tw=tw: e.tensor_tensor(
                    out=xn[:, a * T + t0:a * T + t0 + tw], in0=tmp[:, 0:tw],
                    in1=pb[4 + ti][:, 0:tw], op=ALU.subtract),
                    reads=["ln_tmp", f"pb{4 + ti}"], writes=[f"xn:{a}"], tag="ln_xn")
        self.xn = xn
        return xn

    def layernorm_streaming(self, xin_ap):
        nc, p, T = self.nc, self.p, self.T
        pb = self.pb
        tts = t_tiles(T)
        xn = self.sbuf("xn_bf", [128, NCH * T], BF16)
        xs = [self.sbuf(f"ln_xs{i}", [128, 512], F32) for i in range(2)]
        x2 = self.sbuf("ln_x2", [128, 512], F32)
        stat2 = self.sbuf("ln_stats", [1, 2 * T], F32)
        stat_m = stat2[0:1, 0:T]
        stat_q = stat2[0:1, T:2 * T]
        self.cap(1.0 / 1024.0)
        self.cap(1e-5)
        self.cap(-0.5)
        n = 0
        for ti, (t0, tw) in enumerate(tts):
            for a in range(NCH):
                b = n % 2
                n += 1
                p.dma("sync", lambda e, a=a, t0=t0, tw=tw, b=b: e.dma_start(
                    out=xs[b][:, 0:tw],
                    in_=xin_ap[a * 128:(a + 1) * 128, t0:t0 + tw]),
                    writes=[f"ln_xs{b}"], tag="lns_ldx")
                p.s(lambda e, tw=tw, b=b: e.activation(
                    out=x2[:, 0:tw], in_=xs[b][:, 0:tw], func=AF.Square,
                    bias=0.0, scale=1.0),
                    reads=[f"ln_xs{b}"], writes=["ln_x2"], tag="lns_sq")
                p.t(lambda e, a=a, tw=tw, b=b: e.matmul(
                    out=pb[0][0:1, 0:tw], lhsT=self.ones_col,
                    rhs=xs[b][:, 0:tw],
                    start=(a == 0), stop=(a == NCH - 1)),
                    reads=[f"ln_xs{b}", "cst"], writes=["pb0"], tag="lns_mm")
                p.t(lambda e, a=a, tw=tw: e.matmul(
                    out=pb[1][0:1, 0:tw], lhsT=self.ones_col,
                    rhs=x2[:, 0:tw],
                    start=(a == 0), stop=(a == NCH - 1)),
                    reads=["ln_x2", "cst"], writes=["pb1"], tag="lns_mq")
            p.s(lambda e, ti=ti, t0=t0, tw=tw: e.copy(
                out=stat_m[:, t0:t0 + tw], in_=pb[0][0:1, 0:tw]),
                reads=["pb0"], writes=["ln_m"], tag="lns_evm")
            p.s(lambda e, ti=ti, t0=t0, tw=tw: e.copy(
                out=stat_q[:, t0:t0 + tw], in_=pb[1][0:1, 0:tw]),
                reads=["pb1"], writes=["ln_q"], tag="lns_evq")
            p.s(lambda e, t0=t0, tw=tw: e.activation(
                out=pb[0][0:1, 0:tw], in_=stat_m[:, t0:t0 + tw],
                func=AF.Square, bias=0.0, scale=1.0 / 1024.0),
                reads=["ln_m", "fconst"], writes=["pb0"], tag="lns_m2")
            p.v(lambda e, t0=t0, tw=tw: e.scalar_tensor_tensor(
                out=stat_q[:, t0:t0 + tw], in0=stat_q[:, t0:t0 + tw],
                scalar=1.0 / 1024.0, in1=pb[0][0:1, 0:tw],
                op0=ALU.mult, op1=ALU.subtract),
                reads=["ln_q", "pb0"], writes=["ln_q"], tag="lns_var")
        p.s(lambda e: e.activation(out=stat_q[:], in_=stat_q[:], func=AF.Ln,
                                   bias=1e-5, scale=1.0),
            reads=["ln_q", "fconst"], writes=["ln_q"], tag="lns_ln")
        p.s(lambda e: e.activation(out=stat_q[:], in_=stat_q[:], func=AF.Exp,
                                   bias=0.0, scale=-0.5),
            reads=["ln_q", "fconst"], writes=["ln_q"], tag="lns_exp")
        p.v(lambda e: e.scalar_tensor_tensor(
            out=stat_m[:], in0=stat_m[:], scalar=1.0 / 1024.0, in1=stat_q[:],
            op0=ALU.mult, op1=ALU.mult),
            reads=["ln_m", "ln_q"], writes=["ln_m"], tag="lns_musc")
        n = 0
        for ti, (t0, tw) in enumerate(tts):
            p.t(lambda e, t0=t0, tw=tw: e.matmul(
                out=pb[2][:, 0:tw], lhsT=self.ones_row[0:1, 0:128],
                rhs=stat_q[:, t0:t0 + tw], start=True, stop=True),
                reads=["ln_q", "ones_row"], writes=["pb2"], tag="lns_bcr")
            p.t(lambda e, t0=t0, tw=tw: e.matmul(
                out=pb[3][:, 0:tw], lhsT=self.ones_row[0:1, 0:128],
                rhs=stat_m[:, t0:t0 + tw], start=True, stop=True),
                reads=["ln_m", "ones_row"], writes=["pb3"], tag="lns_bcm")
            for a in range(NCH):
                b = n % 2
                n += 1
                p.dma("sync", lambda e, a=a, t0=t0, tw=tw, b=b: e.dma_start(
                    out=xs[b][:, 0:tw],
                    in_=xin_ap[a * 128:(a + 1) * 128, t0:t0 + tw]),
                    writes=[f"ln_xs{b}"], tag="lns_ldx2")
                p.v(lambda e, tw=tw, b=b: e.tensor_tensor(
                    out=x2[:, 0:tw], in0=xs[b][:, 0:tw],
                    in1=pb[2][:, 0:tw], op=ALU.mult),
                    reads=[f"ln_xs{b}", "pb2"], writes=["ln_x2"], tag="lns_xr")
                p.v(lambda e, a=a, t0=t0, tw=tw: e.tensor_tensor(
                    out=xn[:, a * T + t0:a * T + t0 + tw], in0=x2[:, 0:tw],
                    in1=pb[3][:, 0:tw], op=ALU.subtract),
                    reads=["ln_x2", "pb3"], writes=[f"xn:{a}"], tag="lns_xn")
        self.xn = xn
        self.xn_scratch_a = x2
        self.xn_scratch_b = xs[1]
        self.ln_stat2 = stat2
        self.ln_xs = xs
        return xn


def t_tiles(T, w=512):
    return [(i, min(w, T - i)) for i in range(0, T, w)]



def build_l1(L, aps, xin, xout):
    nc, p, T = L.nc, L.p, L.T
    p.dma("sync", lambda e: e.dma_start(out=xout, in_=xin),
          writes=[f"xout:{a}" for a in range(NCH)], stream="xout_init",
          tag="l1_xout_init")
    L.layernorm_streaming(xin)
    xn, pb = L.xn, L.pb
    tts = t_tiles(T)
    ntt = len(tts)

    prm = L.sbuf("l1_prm", [128, 9 * NCH], F32)
    for i, n in enumerate(["td", "tf", "tmk", "tmv", "tmr"]):
        p.dma("sync", lambda e, i=i, n=n: e.dma_start(
            out=prm[:, i * NCH:(i + 1) * NCH], in_=aps[n]),
            writes=[f"l1prm:{n}"], stream="l1_prm", tag=f"l1_prm_{n}")
    pk = ["l1prm:td", "l1prm:tf", "l1prm:tmk", "l1prm:tmv", "l1prm:tmr"]
    p.s(lambda e: e.activation(out=prm[:, 5 * NCH:6 * NCH], in_=prm[:, 0:NCH],
                               func=AF.Exp, bias=0.0, scale=1.0),
        reads=[pk[0]], writes=["l1prm:w0"], tag="l1_w0")
    p.v(lambda e: e.tensor_scalar(out=prm[:, 5 * NCH:6 * NCH],
                                  in0=prm[:, 5 * NCH:6 * NCH],
                                  scalar1=-1.0, scalar2=None, op0=ALU.mult),
        reads=["l1prm:w0"], writes=["l1prm:w"], tag="l1_w")
    for j in range(3):
        p.v(lambda e, j=j: e.tensor_scalar(
            out=prm[:, (6 + j) * NCH:(7 + j) * NCH],
            in0=prm[:, (2 + j) * NCH:(3 + j) * NCH],
            scalar1=-1.0, scalar2=1.0, op0=ALU.mult, op1=ALU.add),
            reads=[pk[2 + j]], writes=[f"l1prm:m1_{j}"], tag=f"l1_1m{j}")
    PK_ALL = ["l1prm:w"] + [f"l1prm:m1_{j}" for j in range(3)] + pk

    def col(i, a):
        return prm[:, i * NCH + a:i * NCH + a + 1]

    wbuf = L.sbuf("l1_wbuf", [128, NCH * 1024], BF16)
    wstg = [L.sbuf(f"l1_wstg{i}", [128, 1024], F32) for i in range(2)]

    def load_weight(name, tag):
        for a in range(NCH):
            b = a % 2
            p.dma("sync", lambda e, a=a, b=b: e.dma_start(
                out=wstg[b][:], in_=aps[name][a * 128:(a + 1) * 128, :]),
                writes=[f"l1_wstg{b}"], tag=f"ld_{tag}{a}")
            p.v(lambda e, a=a, b=b: e.tensor_copy(
                out=wbuf[:, a * 1024:(a + 1) * 1024], in_=wstg[b][:]),
                reads=[f"l1_wstg{b}"], writes=[f"wbuf:{a}"], tag=f"cv_{tag}{a}")

    WBUF_ALL = [f"wbuf:{a}" for a in range(NCH)]

    vr = L.sbuf("l1_vr", [128, NCH * 32], BF16)
    vr32 = L.sbuf("l1_vr32", [128, NCH * 32], F32)
    ct = L.sbuf("l1_ct", [32, 1024], BF16)
    ct32 = L.sbuf("l1_ct32", [32, 1024], F32)
    bcol = L.sbuf("l1_bcol", [128, NCH], F32)
    p.dma("sync", lambda e: e.dma_start(
        out=vr32[:].rearrange("p (a r) -> p a r", a=NCH),
        in_=aps["vr"].rearrange("(a p) r -> p a r", p=128)),
        writes=["l1_vr32"], tag="l1_ldvr")
    p.dma("sync", lambda e: e.dma_start(out=ct32[:], in_=aps["ct"]),
          writes=["l1_ct32"], tag="l1_ldct")
    p.dma("sync", lambda e: e.dma_start(out=bcol[:], in_=aps["b"]),
          writes=["l1_bcol"], tag="l1_ldb")
    p.v(lambda e: e.tensor_copy(out=vr[:], in_=vr32[:]), reads=["l1_vr32"],
        writes=["l1_vr"], tag="l1_cvvr")
    p.v(lambda e: e.tensor_copy(out=ct[:], in_=ct32[:]), reads=["l1_ct32"],
        writes=["l1_ct"], tag="l1_cvct")
    xr = L.sbuf("l1_xr", [32, T], BF16)
    ob1 = L.ln_xs[0]
    for ti, (t0, tw) in enumerate(tts):
        for a in range(NCH):
            p.t(lambda e, a=a, t0=t0, tw=tw: e.matmul(
                out=pb[0][0:32, 0:tw], lhsT=vr[:, a * 32:(a + 1) * 32],
                rhs=xn[:, a * T + t0:a * T + t0 + tw],
                start=(a == 0), stop=(a == NCH - 1)),
                reads=["l1_vr", f"xn:{a}"], writes=["pb0"], tag="l1_xr_mm")
        p.s(lambda e, t0=t0, tw=tw: e.copy(
            out=xr[:, t0:t0 + tw], in_=pb[0][0:32, 0:tw]),
            reads=["pb0"], writes=["l1_xr"], tag="l1_xr_ev")
    load_weight("wt", "ffnw")
    for oc in range(NCH):
        for ti, (t0, tw) in enumerate(tts):
            pbi = 1 + (ti % 2)
            for a in range(NCH):
                p.t(lambda e, a=a, oc=oc, t0=t0, tw=tw, pbi=pbi: e.matmul(
                    out=pb[pbi][:, 0:tw],
                    lhsT=wbuf[:, a * 1024 + oc * 128:a * 1024 + (oc + 1) * 128],
                    rhs=xn[:, a * T + t0:a * T + t0 + tw],
                    start=(a == 0), stop=False),
                    reads=[f"wbuf:{a}", f"xn:{a}"], writes=[f"pb{pbi}"],
                    tag="l1_ffn_mm")
            p.t(lambda e, oc=oc, t0=t0, tw=tw, pbi=pbi: e.matmul(
                out=pb[pbi][:, 0:tw], lhsT=ct[:, oc * 128:(oc + 1) * 128],
                rhs=xr[:, t0:t0 + tw], start=False, stop=True),
                reads=["l1_ct", "l1_xr"], writes=[f"pb{pbi}"], tag="l1_ffn_cr")
            p.v(lambda e, oc=oc, tw=tw, pbi=pbi: e.tensor_scalar(
                out=ob1[:, 0:tw], in0=pb[pbi][:, 0:tw],
                scalar1=bcol[:, oc:oc + 1], scalar2=None, op0=ALU.add),
                reads=[f"pb{pbi}", "l1_bcol"], writes=["l1_ob", "ln_xs0"],
                tag="l1_ffn_ev")
            p.dma("gpsimd", lambda e, oc=oc, t0=t0, tw=tw: e.dma_start(
                out=xout[oc * 128:(oc + 1) * 128, t0:t0 + tw],
                in_=ob1[:, 0:tw], accum_op=ALU.add),
                reads=["l1_ob", "ln_xs0"], writes=[f"xout:{oc}"],
                stream=f"l1_xo{oc % 4}", tag="l1_ffn_acc")

    kf = L.sbuf("l1_k", [128, NCH * 512], F32)
    vf = L.sbuf("l1_v", [128, NCH * 512], BF16)
    srf = L.sbuf("l1_sr", [128, NCH * 512], BF16)
    sm1 = L.sbuf("l1_sm1", [128, 512], BF16)
    sm2 = L.sbuf("l1_sm2", [128, 512], BF16)
    tA = L.sbuf("l1_tA", [128, 512], F32)
    tB = L.sbuf("l1_tB", [128, 512], F32)
    tC = L.sbuf("l1_tC", [128, 512], F32)
    tD = L.sbuf("l1_tD", [128, 512], F32)
    tE = L.sbuf("l1_tE", [128, 512], F32)
    tF = L.sbuf("l1_tF", [128, 512], F32)
    tG = L.xn_scratch_a
    tH = L.xn_scratch_b
    rw = L.sbuf("l1_rw", [128, NCH * 512], BF16)
    carry = L.sbuf("l1_carry", [128, 3 * NCH], F32)
    p.v(lambda e: e.memset(carry[:, 0:NCH], 0.0), writes=["l1_caa"], tag="l1_c0")
    p.v(lambda e: e.memset(carry[:, NCH:2 * NCH], 0.0), writes=["l1_cbb"], tag="l1_c1")
    p.v(lambda e: e.memset(carry[:, 2 * NCH:3 * NCH], -1e38), writes=["l1_cpp"],
        tag="l1_c2")

    projmap = [("k", 2, "wkt"), ("v", 3, "wvt"), ("r", 4, "wrt")]
    for ti, (t0, tw) in enumerate(tts):
        for pname, mi, wname in projmap:
            load_weight(wname, f"{wname}_{ti}")
            for dg in (0, 4):
                for a in range(NCH):
                    p.v(lambda e, a=a, mi=mi, t0=t0, tw=tw: e.tensor_scalar(
                        out=sm1[:, 0:tw], in0=xn[:, a * T + t0:a * T + t0 + tw],
                        scalar1=col(mi, a), scalar2=None, op0=ALU.mult),
                        reads=[f"xn:{a}", pk[mi]],
                        writes=["l1_sm1"], tag="l1_mx")
                    lo = max(t0 - 1, 0)
                    w2 = tw - 1 if t0 == 0 else tw
                    p.v(lambda e, a=a, mi=mi, lo=lo, w2=w2: e.tensor_scalar(
                        out=sm2[:, 0:w2], in0=xn[:, a * T + lo:a * T + lo + w2],
                        scalar1=col(mi + 4, a), scalar2=None, op0=ALU.mult),
                        reads=[f"xn:{a}", f"l1prm:m1_{mi - 2}"],
                        writes=["l1_sm2"], tag="l1_ms")
                    ob = 1 if t0 == 0 else 0
                    for d in range(dg, dg + 4):
                        p.t(lambda e, a=a, d=d, tw=tw: e.matmul(
                            out=pb[d % 4][:, 0:tw],
                            lhsT=wbuf[:, a * 1024 + d * 128:a * 1024 + (d + 1) * 128],
                            rhs=sm1[:, 0:tw], start=(a == 0), stop=False),
                            reads=["l1_sm1", f"wbuf:{a}"], writes=[f"pb{d % 4}"],
                            tag="l1_pj")
                        p.t(lambda e, a=a, d=d, ob=ob, w2=w2: e.matmul(
                            out=pb[d % 4][:, ob:ob + w2],
                            lhsT=wbuf[:, a * 1024 + d * 128:a * 1024 + (d + 1) * 128],
                            rhs=sm2[:, 0:w2], start=False, stop=(a == NCH - 1)),
                            reads=["l1_sm2", f"wbuf:{a}"], writes=[f"pb{d % 4}"],
                            tag="l1_pjs")
                for d in range(dg, dg + 4):
                    if pname == "k":
                        p.s(lambda e, d=d, tw=tw: e.copy(
                            out=kf[:, d * 512:d * 512 + tw], in_=pb[d % 4][:, 0:tw]),
                            reads=[f"pb{d % 4}"], writes=[f"l1_k:{d}"], tag="l1_evk")
                    elif pname == "v":
                        p.s(lambda e, d=d, tw=tw: e.copy(
                            out=vf[:, d * 512:d * 512 + tw], in_=pb[d % 4][:, 0:tw]),
                            reads=[f"pb{d % 4}"], writes=[f"l1_v:{d}"], tag="l1_evv")
                    else:
                        p.s(lambda e, d=d, tw=tw: e.activation(
                            out=srf[:, d * 512:d * 512 + tw], in_=pb[d % 4][:, 0:tw],
                            func=AF.Sigmoid, bias=0.0, scale=1.0),
                            reads=[f"pb{d % 4}"], writes=[f"l1_sr:{d}"], tag="l1_evr")

        for d in range(NCH):
            kc = kf[:, d * 512:d * 512 + tw]
            vc = vf[:, d * 512:d * 512 + tw]
            src = srf[:, d * 512:d * 512 + tw]
            caa = carry[:, d:d + 1]
            cbb = carry[:, NCH + d:NCH + d + 1]
            cpp = carry[:, 2 * NCH + d:2 * NCH + d + 1]
            rk, rv, rs = f"l1_k:{d}", f"l1_v:{d}", f"l1_sr:{d}"
            p.v(lambda e, d=d, tw=tw, kc=kc: e.tensor_scalar(
                out=tA[:, 0:tw], in0=kc, scalar1=0.0, scalar2=col(5, d),
                op0=ALU.mult, op1=ALU.add),
                reads=[rk, "l1prm:w"], writes=["l1_tA"], tag="l1_wbc")
            p.v(lambda e, tw=tw, kc=kc, cpp=cpp: e.tensor_tensor_scan(
                out=tB[:, 0:tw], data0=tA[:, 0:tw], data1=kc,
                initial=cpp, op0=ALU.add, op1=ALU.max),
                reads=["l1_tA", rk, "l1_cpp"], writes=["l1_tB"], tag="l1_scpp")
            p.v(lambda e, cpp=cpp: e.tensor_copy(out=tC[:, 0:1], in_=cpp),
                reads=["l1_cpp"], writes=["l1_tC"], tag="l1_ppp0")
            p.v(lambda e, tw=tw: e.tensor_copy(
                out=tC[:, 1:tw], in_=tB[:, 0:tw - 1]),
                reads=["l1_tB"], writes=["l1_tC"], tag="l1_ppp")
            p.v(lambda e, tw=tw, cpp=cpp: e.tensor_copy(
                out=cpp, in_=tB[:, tw - 1:tw]),
                reads=["l1_tB"], writes=["l1_cpp"], tag="l1_cppu")
            p.v(lambda e, tw=tw: e.tensor_tensor(
                out=tD[:, 0:tw], in0=tC[:, 0:tw], in1=tB[:, 0:tw], op=ALU.subtract),
                reads=["l1_tC", "l1_tB"], writes=["l1_tD"], tag="l1_lam0")
            p.s(lambda e, d=d, tw=tw: e.activation(
                out=tD[:, 0:tw], in_=tD[:, 0:tw], func=AF.Exp, bias=col(5, d),
                scale=1.0),
                reads=["l1_tD", "l1prm:w"], writes=["l1_tD"], tag="l1_lam")
            p.v(lambda e, tw=tw, kc=kc: e.tensor_tensor(
                out=tE[:, 0:tw], in0=kc, in1=tB[:, 0:tw], op=ALU.subtract),
                reads=[rk, "l1_tB"], writes=["l1_tE"], tag="l1_bt0")
            p.s(lambda e, tw=tw: e.activation(
                out=tE[:, 0:tw], in_=tE[:, 0:tw], func=AF.Exp, bias=0.0, scale=1.0),
                reads=["l1_tE"], writes=["l1_tE"], tag="l1_bt")
            p.v(lambda e, tw=tw, vc=vc: e.tensor_tensor(
                out=tF[:, 0:tw], in0=tE[:, 0:tw], in1=vc, op=ALU.mult),
                reads=["l1_tE", rv], writes=["l1_tF"], tag="l1_bv")
            p.v(lambda e, tw=tw, caa=caa: e.tensor_tensor_scan(
                out=tG[:, 0:tw], data0=tD[:, 0:tw], data1=tF[:, 0:tw],
                initial=caa, op0=ALU.mult, op1=ALU.add),
                reads=["l1_tD", "l1_tF", "l1_caa"], writes=["l1_tG", "ln_x2"], tag="l1_saa")
            p.v(lambda e, caa=caa: e.tensor_copy(out=tH[:, 0:1], in_=caa),
                reads=["l1_caa"], writes=["l1_tH", "ln_xs1"], tag="l1_aap0")
            p.v(lambda e, tw=tw: e.tensor_copy(
                out=tH[:, 1:tw], in_=tG[:, 0:tw - 1]),
                reads=["l1_tG"], writes=["l1_tH"], tag="l1_aap")
            p.v(lambda e, tw=tw, caa=caa: e.tensor_copy(
                out=caa, in_=tG[:, tw - 1:tw]),
                reads=["l1_tG"], writes=["l1_caa"], tag="l1_caau")
            p.v(lambda e, tw=tw, cbb=cbb: e.tensor_tensor_scan(
                out=tF[:, 0:tw], data0=tD[:, 0:tw], data1=tE[:, 0:tw],
                initial=cbb, op0=ALU.mult, op1=ALU.add),
                reads=["l1_tD", "l1_tE", "l1_cbb"], writes=["l1_tF"], tag="l1_sbb")
            p.v(lambda e, cbb=cbb: e.tensor_copy(out=tA[:, 0:1], in_=cbb),
                reads=["l1_cbb"], writes=["l1_tA"], tag="l1_bbp0")
            p.v(lambda e, tw=tw: e.tensor_copy(
                out=tA[:, 1:tw], in_=tF[:, 0:tw - 1]),
                reads=["l1_tF"], writes=["l1_tA"], tag="l1_bbp")
            p.v(lambda e, tw=tw, cbb=cbb: e.tensor_copy(
                out=cbb, in_=tF[:, tw - 1:tw]),
                reads=["l1_tF"], writes=["l1_cbb"], tag="l1_cbbu")
            p.v(lambda e, d=d, tw=tw, kc=kc: e.tensor_scalar(
                out=tB[:, 0:tw], in0=kc, scalar1=col(1, d), scalar2=None,
                op0=ALU.add),
                reads=[rk, pk[1]], writes=["l1_tB"], tag="l1_ww")
            p.v(lambda e, tw=tw: e.tensor_tensor(
                out=tD[:, 0:tw], in0=tC[:, 0:tw], in1=tB[:, 0:tw], op=ALU.max),
                reads=["l1_tC", "l1_tB"], writes=["l1_tD"], tag="l1_q")
            p.v(lambda e, tw=tw: e.tensor_tensor(
                out=tC[:, 0:tw], in0=tC[:, 0:tw], in1=tD[:, 0:tw], op=ALU.subtract),
                reads=["l1_tC", "l1_tD"], writes=["l1_tC"], tag="l1_e10")
            p.s(lambda e, tw=tw: e.activation(
                out=tC[:, 0:tw], in_=tC[:, 0:tw], func=AF.Exp, bias=0.0, scale=1.0),
                reads=["l1_tC"], writes=["l1_tC"], tag="l1_e1")
            p.v(lambda e, tw=tw: e.tensor_tensor(
                out=tB[:, 0:tw], in0=tB[:, 0:tw], in1=tD[:, 0:tw], op=ALU.subtract),
                reads=["l1_tB", "l1_tD"], writes=["l1_tB"], tag="l1_e20")
            p.s(lambda e, tw=tw: e.activation(
                out=tB[:, 0:tw], in_=tB[:, 0:tw], func=AF.Exp, bias=0.0, scale=1.0),
                reads=["l1_tB"], writes=["l1_tB"], tag="l1_e2")
            p.v(lambda e, tw=tw: e.tensor_tensor(
                out=tH[:, 0:tw], in0=tC[:, 0:tw], in1=tH[:, 0:tw], op=ALU.mult),
                reads=["l1_tC", "l1_tH"], writes=["l1_tH"], tag="l1_n1")
            p.v(lambda e, tw=tw, vc=vc: e.tensor_tensor(
                out=tD[:, 0:tw], in0=tB[:, 0:tw], in1=vc, op=ALU.mult),
                reads=["l1_tB", rv], writes=["l1_tD"], tag="l1_n2")
            p.v(lambda e, tw=tw: e.tensor_tensor(
                out=tH[:, 0:tw], in0=tH[:, 0:tw], in1=tD[:, 0:tw], op=ALU.add),
                reads=["l1_tH", "l1_tD"], writes=["l1_tH"], tag="l1_num")
            p.v(lambda e, tw=tw: e.tensor_tensor(
                out=tA[:, 0:tw], in0=tC[:, 0:tw], in1=tA[:, 0:tw], op=ALU.mult),
                reads=["l1_tC", "l1_tA"], writes=["l1_tA"], tag="l1_d1")
            p.v(lambda e, tw=tw: e.tensor_tensor(
                out=tA[:, 0:tw], in0=tA[:, 0:tw], in1=tB[:, 0:tw], op=ALU.add),
                reads=["l1_tA", "l1_tB"], writes=["l1_tA"], tag="l1_den")
            p.s(lambda e, tw=tw: e.activation(
                out=tA[:, 0:tw], in_=tA[:, 0:tw], func=AF.Ln, bias=0.0, scale=1.0),
                reads=["l1_tA"], writes=["l1_tA"], tag="l1_rd0")
            p.s(lambda e, tw=tw: e.activation(
                out=tA[:, 0:tw], in_=tA[:, 0:tw], func=AF.Exp, bias=0.0, scale=-1.0),
                reads=["l1_tA"], writes=["l1_tA"], tag="l1_rd1")
            p.v(lambda e, tw=tw: e.tensor_tensor(
                out=tH[:, 0:tw], in0=tH[:, 0:tw], in1=tA[:, 0:tw], op=ALU.mult),
                reads=["l1_tH", "l1_tA"], writes=["l1_tH"], tag="l1_y")
            p.v(lambda e, d=d, tw=tw, src=src: e.tensor_tensor(
                out=rw[:, d * 512:d * 512 + tw], in0=tH[:, 0:tw], in1=src,
                op=ALU.mult),
                reads=["l1_tH", rs], writes=[f"l1_rw:{d}"], tag="l1_rwkv")

        load_weight("wot", f"wo_{ti}")
        for oc in range(NCH):
            pbi = oc % 4
            for a in range(NCH):
                p.t(lambda e, a=a, oc=oc, tw=tw, pbi=pbi: e.matmul(
                    out=pb[pbi][:, 0:tw],
                    lhsT=wbuf[:, a * 1024 + oc * 128:a * 1024 + (oc + 1) * 128],
                    rhs=rw[:, a * 512:a * 512 + tw],
                    start=(a == 0), stop=(a == NCH - 1)),
                    reads=[f"wbuf:{a}", f"l1_rw:{a}"], writes=[f"pb{pbi}"],
                    tag="l1_wo")
            p.v(lambda e, tw=tw, pbi=pbi: e.tensor_copy(
                out=ob1[:, 0:tw], in_=pb[pbi][:, 0:tw]),
                reads=[f"pb{pbi}"], writes=["l1_ob", "ln_xs0"], tag="l1_wo_ev")
            p.dma("gpsimd", lambda e, oc=oc, t0=t0, tw=tw: e.dma_start(
                out=xout[oc * 128:(oc + 1) * 128, t0:t0 + tw],
                in_=ob1[:, 0:tw], accum_op=ALU.add),
                reads=["l1_ob", "ln_xs0"], writes=[f"xout:{oc}"],
                stream=f"l1_xo{oc % 4}", tag="l1_wo_acc")


    L.p.wait_all("gpsimd", [f"xout:{a}" for a in range(NCH)], tag="l1_final")


def prep_l1(inputs):
    c8 = lambda v: np.ascontiguousarray(v.reshape(NCH, 128).T)
    return {
        "td": c8(np.asarray(inputs["L1_attn_time_decay"])),
        "tf": c8(np.asarray(inputs["L1_attn_time_first"])),
        "tmk": c8(np.asarray(inputs["L1_attn_time_mix_k"]).reshape(-1)),
        "tmv": c8(np.asarray(inputs["L1_attn_time_mix_v"]).reshape(-1)),
        "tmr": c8(np.asarray(inputs["L1_attn_time_mix_r"]).reshape(-1)),
        "wkt": np.ascontiguousarray(np.asarray(inputs["L1_attn_Wk"]).T),
        "wvt": np.ascontiguousarray(np.asarray(inputs["L1_attn_Wv"]).T),
        "wrt": np.ascontiguousarray(np.asarray(inputs["L1_attn_Wr"]).T),
        "wot": np.ascontiguousarray(np.asarray(inputs["L1_attn_Wo"]).T),
        "wt": np.ascontiguousarray(np.asarray(inputs["L1_ffn_W"]).T),
        "b": c8(np.asarray(inputs["L1_ffn_b"])),
        "vr": np.ascontiguousarray(np.asarray(inputs["L1_ffn_V_r"])),
        "ct": np.ascontiguousarray(np.asarray(inputs["L1_ffn_C"]).T),
    }


L1_DRAM_SPECS = [
    ("td", [128, NCH]), ("tf", [128, NCH]), ("tmk", [128, NCH]),
    ("tmv", [128, NCH]), ("tmr", [128, NCH]),
    ("wkt", [1024, 1024]), ("wvt", [1024, 1024]), ("wrt", [1024, 1024]),
    ("wot", [1024, 1024]), ("wt", [1024, 1024]), ("b", [128, NCH]),
    ("vr", [1024, 32]), ("ct", [32, 1024]),
]

LAYER_BUILDERS = {}
LAYER_DRAM_SPECS = {}





NSEG = 5


def _l2_urdhva_fm(L, p, prefix, wslot, wkey, src_t, src_key_fmt, dst,
                  dst_key_fmt, bias, bias_idx):
    T = L.T
    pb, wbuf = L.pb, L.wbuf
    tts = t_tiles(T)
    for b in range(NCH):
        for ti, (t0, tw) in enumerate(tts):
            pbi = (b * len(tts) + ti) % 2
            parts = [(b, wslot + b)]
            if b > 0:
                parts.append((b - 1, wslot + 8 + (b - 1)))
            if b < 7:
                parts.append((b + 1, wslot + 15 + b))
            for pi, (src_c, slot) in enumerate(parts):
                p.t(lambda e, src_c=src_c, slot=slot, t0=t0, tw=tw, pbi=pbi,
                    first=(pi == 0), last=(pi == len(parts) - 1): e.matmul(
                    out=pb[pbi][:, 0:tw],
                    lhsT=wbuf[:, slot * 128:(slot + 1) * 128],
                    rhs=src_t[:, src_c * T + t0:src_c * T + t0 + tw],
                    start=first, stop=last),
                    reads=[wkey, src_key_fmt.format(src_c)], writes=[f"pb{pbi}"],
                    tag=f"l2_{prefix}_mm")
            p.v(lambda e, b=b, t0=t0, tw=tw, pbi=pbi: e.tensor_scalar(
                out=dst[:, b * T + t0:b * T + t0 + tw], in0=pb[pbi][:, 0:tw],
                scalar1=bias[:, bias_idx * NCH + b:bias_idx * NCH + b + 1],
                scalar2=None, op0=ALU.add),
                reads=[f"pb{pbi}", "l2_bias"], writes=[dst_key_fmt.format(b)],
                tag=f"l2_{prefix}_ev")


def build_l2(L, aps, xin, xout):
    nc, p, T = L.nc, L.p, L.T
    pb = L.pb
    tts = t_tiles(T)
    p.dma("sync", lambda e: e.dma_start(out=xout, in_=xin),
          writes=[f"xout:{a}" for a in range(NCH)], stream="xout_init",
          tag="l2_xout_init")
    L.layernorm_streaming(xin)
    xn = L.xn

    wbuf = L.sbuf("l2_wbuf", [128, 88 * 128], BF16)
    L.wbuf = wbuf
    wstg2 = [L.sbuf(f"l2_wstg{i}", [128, 1024], F32) for i in range(2)]
    wstg = wstg2[0]
    bias = L.sbuf("l2_bias", [128, 6 * NCH], F32)
    for i, nm in enumerate(["qb", "kb", "vb", "ob", "fsc", "fbi"]):
        p.dma("sync", lambda e, i=i, nm=nm: e.dma_start(
            out=bias[:, i * NCH:(i + 1) * NCH], in_=aps[nm]),
            writes=["l2_bias"], stream="l2_bias", tag=f"l2_ldb_{nm}")

    def bcol(i):
        return lambda b: bias[:, i * NCH + b:i * NCH + b + 1]

    SGNK = tuple(f"l2_sgn:{a}" for a in range(NCH))

    def load_blocks(name, slot0, nblk, wkey, alias=()):
        for b8 in range(0, nblk, 8):
            wbx = (b8 // 8) % 2
            cnt = min(8, nblk - b8)
            p.dma("sync", lambda e, b8=b8, cnt=cnt, name=name, wbx=wbx: e.dma_start(
                out=wstg2[wbx][:, 0:cnt * 128].rearrange("p (n c) -> p n c", c=128),
                in_=aps[name].rearrange("(n p) c -> p n c", p=128)[:, b8:b8 + cnt, :]),
                writes=[f"l2_wstg{wbx}"], tag=f"l2_ldw_{name}{b8}")
            p.v(lambda e, b8=b8, cnt=cnt, slot0=slot0, wbx=wbx: e.tensor_copy(
                out=wbuf[:, (slot0 + b8) * 128:(slot0 + b8 + cnt) * 128],
                in_=wstg2[wbx][:, 0:cnt * 128]),
                reads=[f"l2_wstg{wbx}"], writes=[wkey] + list(alias) + list(SGNK),
                tag=f"l2_cvw_{name}{b8}")

    sgn = wbuf[:, 0:NCH * 1024]
    for a in range(NCH):
        wbx = a % 2
        p.dma("sync", lambda e, a=a, wbx=wbx: e.dma_start(
            out=wstg2[wbx][:], in_=aps["fwt"][a * 128:(a + 1) * 128, :]),
            writes=[f"l2_wstg{wbx}"], tag=f"l2_ldfw{a}")
        p.v(lambda e, a=a, wbx=wbx: e.tensor_scalar(
            out=sgn[:, a * 1024:(a + 1) * 1024], in0=wstg2[wbx][:],
            scalar1=0.0, scalar2=None, op0=ALU.is_ge),
            reads=[f"l2_wstg{wbx}"], writes=[f"l2_sgn:{a}"], tag=f"l2_sg0{a}")
        p.v(lambda e, a=a: e.tensor_scalar(
            out=sgn[:, a * 1024:(a + 1) * 1024],
            in0=sgn[:, a * 1024:(a + 1) * 1024],
            scalar1=2.0, scalar2=-1.0, op0=ALU.mult, op1=ALU.add),
            reads=[f"l2_sgn:{a}"], writes=[f"l2_sgn:{a}"], tag=f"l2_sg1{a}")
    ob = L.ln_xs
    L.cap(100.0)
    L.cap(-100.0)
    for oc in range(NCH):
        for ti, (t0, tw) in enumerate(tts):
            pbi = (oc * len(tts) + ti) % 2
            for a in range(NCH):
                p.t(lambda e, a=a, oc=oc, t0=t0, tw=tw, pbi=pbi: e.matmul(
                    out=pb[pbi][:, 0:tw],
                    lhsT=sgn[:, a * 1024 + oc * 128:a * 1024 + (oc + 1) * 128],
                    rhs=xn[:, a * T + t0:a * T + t0 + tw],
                    start=(a == 0), stop=(a == NCH - 1)),
                    reads=[f"l2_sgn:{a}", f"xn:{a}"], writes=[f"pb{pbi}"],
                    tag="l2_ffn_mm")
            bo = (oc * len(tts) + ti) % 2
            p.v(lambda e, oc=oc, tw=tw, pbi=pbi, bo=bo: e.tensor_scalar(
                out=ob[bo][:, 0:tw], in0=pb[pbi][:, 0:tw],
                scalar1=bias[:, 4 * NCH + oc:4 * NCH + oc + 1],
                scalar2=bias[:, 5 * NCH + oc:5 * NCH + oc + 1],
                op0=ALU.mult, op1=ALU.add),
                reads=[f"pb{pbi}", "l2_bias"], writes=[f"l2_ob{bo}", f"ln_xs{bo}"],
                tag="l2_ffn_sb")
            p.v(lambda e, tw=tw, bo=bo: e.tensor_scalar(
                out=ob[bo][:, 0:tw], in0=ob[bo][:, 0:tw],
                scalar1=100.0, scalar2=-100.0, op0=ALU.min, op1=ALU.max),
                reads=[f"l2_ob{bo}", "fconst"],
                writes=[f"l2_ob{bo}", f"ln_xs{bo}"], tag="l2_ffn_clip")
            p.dma("gpsimd", lambda e, oc=oc, t0=t0, tw=tw, bo=bo: e.dma_start(
                out=xout[oc * 128:(oc + 1) * 128, t0:t0 + tw],
                in_=ob[bo][:, 0:tw], accum_op=ALU.add),
                reads=[f"l2_ob{bo}", f"ln_xs{bo}"], writes=[f"xout:{oc}"],
                stream=f"l2_xo{oc % 4}", tag="l2_ffn_acc")

    kf = L.sbuf("l2_kf", [128, NCH * T], BF16)
    n_tc = T // 128
    vtok = L.sbuf("l2_vt", [128, n_tc * 1024], BF16)
    yat = L.sbuf("l2_ya", [128, NCH * T], BF16)
    qt = L.sbuf("l2_qt", [128, T], BF16)
    for nm, s0 in (("kwv", 0), ("kwr", 8), ("kwl", 15)):
        load_blocks(nm, s0, 8 if nm.endswith("wv") else 7, "wb:k")
    for nm, s0 in (("vwv", 22), ("vwr", 30), ("vwl", 37)):
        load_blocks(nm, s0, 8 if nm.endswith("wv") else 7, "wb:v")
    for nm, s0 in (("qwv", 44), ("qwr", 52), ("qwl", 59)):
        load_blocks(nm, s0, 8 if nm.endswith("wv") else 7, "wb:q")
    vbrow = L.sbuf("l2_vbrow", [1, 1024], BF16)
    vbrow32 = L.sbuf("l2_vbrow32", [1, 1024], F32)
    p.dma("sync", lambda e: e.dma_start(out=vbrow32[:], in_=aps["vbrow"]),
          writes=["l2_vbrow32"], tag="l2_ldvbr")
    p.v(lambda e: e.tensor_copy(out=vbrow[:], in_=vbrow32[:]),
        reads=["l2_vbrow32"], writes=["l2_vbrow"], tag="l2_cvvbr")

    _l2_urdhva_fm(L, p, "k", 0, "wb:k", xn, "xn:{}", kf, "l2_kf:{}", bias, 1)
    for tc in range(n_tc):
        for bg in range(2):
            pbi = (tc * 2 + bg) % 2
            for bi in range(4):
                b = bg * 4 + bi
                parts = [(b, 22 + b)]
                if b > 0:
                    parts.append((b - 1, 22 + 8 + (b - 1)))
                if b < 7:
                    parts.append((b + 1, 22 + 15 + b))
                for pi, (src_c, slot) in enumerate(parts):
                    p.t(lambda e, src_c=src_c, slot=slot, tc=tc, bi=bi, pbi=pbi,
                        first=(pi == 0): e.matmul(
                        out=pb[pbi][:, bi * 128:(bi + 1) * 128],
                        lhsT=xn[:, src_c * T + tc * 128:src_c * T + (tc + 1) * 128],
                        rhs=wbuf[:, slot * 128:(slot + 1) * 128],
                        start=first, stop=False),
                        reads=["wb:v", f"xn:{src_c}"], writes=[f"pb{pbi}"],
                        tag="l2_vt_mm")
                p.t(lambda e, b=b, bi=bi, pbi=pbi: e.matmul(
                    out=pb[pbi][:, bi * 128:(bi + 1) * 128],
                    lhsT=L.ones_row_bf[0:1, 0:128],
                    rhs=vbrow[0:1, b * 128:(b + 1) * 128],
                    start=False, stop=True),
                    reads=["l2_vbrow", "ones_row_bf"], writes=[f"pb{pbi}"],
                    tag="l2_vt_b")
            p.v(lambda e, tc=tc, bg=bg, pbi=pbi: e.tensor_copy(
                out=vtok[:, tc * 1024 + bg * 512:tc * 1024 + (bg + 1) * 512],
                in_=pb[pbi][:]),
                reads=[f"pb{pbi}"], writes=[f"l2_vt:{tc}"], tag="l2_vt_ev")

    msk = [L.sbuf(f"l2_msk{i}", [128, NSEG * 128], F32) for i in range(2)]
    Sx = L.sbuf("l2_S", [128, NSEG * 128], F32)
    Af = Sx
    at_sb = L.sbuf("l2_AT", [128, NSEG * 128], BF16)
    nm_t = L.sbuf("l2_nm", [128, 4], F32)
    idf = L.sbuf("l2_idf", [128, 128], F32)
    p.dma("sync", lambda e: e.dma_start(out=idf[:], in_=aps["idf"]),
          writes=["l2_idf"], tag="l2_ldid")
    L.cap(0.125)

    for ec in range(NCH):
        for ti, (t0, tw) in enumerate(tts):
            pbi = 4
            parts = [(ec, 44 + ec)]
            if ec > 0:
                parts.append((ec - 1, 44 + 8 + (ec - 1)))
            if ec < 7:
                parts.append((ec + 1, 44 + 15 + ec))
            for pi, (src_c, slot) in enumerate(parts):
                p.t(lambda e, src_c=src_c, slot=slot, t0=t0, tw=tw, pbi=pbi,
                    first=(pi == 0), last=(pi == len(parts) - 1): e.matmul(
                    out=pb[pbi][:, 0:tw],
                    lhsT=wbuf[:, slot * 128:(slot + 1) * 128],
                    rhs=xn[:, src_c * T + t0:src_c * T + t0 + tw],
                    start=first, stop=last),
                    reads=["wb:q", f"xn:{src_c}"], writes=[f"pb{pbi}"],
                    tag="l2_q_mm")
            p.v(lambda e, ec=ec, t0=t0, tw=tw, pbi=pbi: e.tensor_scalar(
                out=qt[:, t0:t0 + tw], in0=pb[pbi][:, 0:tw],
                scalar1=bias[:, ec:ec + 1], scalar2=None, op0=ALU.add),
                reads=[f"pb{pbi}", "l2_bias"], writes=["l2_qt"], tag="l2_q_ev")
        nqc = T // 128
        for hh in range(2):
            h = 2 * ec + hh
            r0 = hh * 64
            for i in range(nqc):
                segs = [max(j, 0) for j in (i - 8, i - 4, i - 2, i - 1, i)]
                mb = i % 2
                p.dma("sync", lambda e, i=i, mb=mb: e.dma_start(
                    out=msk[mb][:], in_=aps["mask"][i * 128:(i + 1) * 128, :]),
                    writes=[f"l2_msk{mb}"], tag="l2_ldmsk")
                for s, j in enumerate(segs):
                    tgt = (0, s * 128) if s < 4 else (1, 0)
                    p.t(lambda e, j=j, i=i, r0=r0, ec=ec, tgt=tgt: e.matmul(
                        out=pb[tgt[0]][:, tgt[1]:tgt[1] + 128],
                        lhsT=qt[r0:r0 + 64, i * 128:(i + 1) * 128],
                        rhs=kf[r0:r0 + 64, ec * T + j * 128:ec * T + (j + 1) * 128],
                        start=True, stop=True),
                        reads=["l2_qt", f"l2_kf:{ec}"], writes=[f"pb{tgt[0]}"],
                        tag="l2_sc_mm")
                p.v(lambda e, mb=mb: e.tensor_tensor(
                    out=Sx[:, 0:512], in0=pb[0][:, 0:512], in1=msk[mb][:, 0:512],
                    op=ALU.add),
                    reads=["pb0", f"l2_msk{mb}"], writes=["l2_S"], tag="l2_sc_m0")
                p.v(lambda e, mb=mb: e.tensor_tensor(
                    out=Sx[:, 512:640], in0=pb[1][:, 0:128],
                    in1=msk[mb][:, 512:640], op=ALU.add),
                    reads=["pb1", f"l2_msk{mb}"], writes=["l2_S"], tag="l2_sc_m1")
                p.v(lambda e: e.reduce_max(out=nm_t[:, 0:1], in_=Sx[:],
                                           axis=AX.X, negate=True),
                    reads=["l2_S"], writes=["l2_nm"], tag="l2_rmax")
                p.v(lambda e: e.tensor_scalar(
                    out=nm_t[:, 1:2], in0=nm_t[:, 0:1], scalar1=0.125,
                    scalar2=None, op0=ALU.mult),
                    reads=["l2_nm", "fconst"], writes=["l2_nm"], tag="l2_nm8")
                p.s(lambda e: e.activation(
                    out=Sx[:], in_=Sx[:], func=AF.Exp,
                    bias=nm_t[:, 1:2], scale=0.125, accum_out=nm_t[:, 2:3]),
                    reads=["l2_S", "l2_nm", "fconst"], writes=["l2_S", "l2_nm"],
                    tag="l2_exp")
                p.v(lambda e: e.reciprocal(out=nm_t[:, 3:4], in_=nm_t[:, 2:3]),
                    reads=["l2_nm"], writes=["l2_nm"], tag="l2_rz")
                p.v(lambda e: e.tensor_scalar(
                    out=Af[:], in0=Sx[:], scalar1=nm_t[:, 3:4], scalar2=None,
                    op0=ALU.mult),
                    reads=["l2_S", "l2_nm"], writes=["l2_Af", "l2_S"],
                    tag="l2_an")
                for s in range(NSEG):
                    p.t(lambda e, s=s: e.transpose(
                        out=pb[2 + (s % 2)][:, (s // 2 % 2) * 128:(s // 2 % 2) * 128 + 128],
                        in_=Af[:, s * 128:(s + 1) * 128], identity=idf[:]),
                        reads=["l2_Af", "l2_idf"], writes=[f"pb{2 + (s % 2)}"],
                        tag="l2_tr")
                    p.v(lambda e, s=s: e.tensor_copy(
                        out=at_sb[:, s * 128:(s + 1) * 128],
                        in_=pb[2 + (s % 2)][:, (s // 2 % 2) * 128:(s // 2 % 2) * 128 + 128]),
                        reads=[f"pb{2 + (s % 2)}"], writes=["l2_AT"], tag="l2_trev")
                for s, j in enumerate(segs):
                    p.t(lambda e, s=s, j=j, h=h, first=(s == 0),
                        last=(s == NSEG - 1): e.matmul(
                        out=pb[5][0:64, 0:128],
                        lhsT=vtok[:, j * 1024 + h * 64:j * 1024 + h * 64 + 64],
                        rhs=at_sb[:, s * 128:(s + 1) * 128],
                        start=first, stop=last),
                        reads=[f"l2_vt:{j}", "l2_AT"], writes=["pb5"],
                        tag="l2_av")
                p.s(lambda e, r0=r0, ec=ec, i=i: e.copy(
                    out=yat[r0:r0 + 64, ec * T + i * 128:ec * T + (i + 1) * 128],
                    in_=pb[5][0:64, 0:128]),
                    reads=["pb5"], writes=[f"l2_ya:{ec}"], tag="l2_yev")

    for nm, s0 in (("owv", 0), ("owr", 8), ("owl", 15)):
        load_blocks(nm, s0, 8 if nm.endswith("wv") else 7, "wb:o", alias=("wb:k",))
    for b in range(NCH):
        for ti, (t0, tw) in enumerate(tts):
            pbi = (b * len(tts) + ti) % 2
            parts = [(b, b)]
            if b > 0:
                parts.append((b - 1, 8 + (b - 1)))
            if b < 7:
                parts.append((b + 1, 15 + b))
            for pi, (src_c, slot) in enumerate(parts):
                p.t(lambda e, src_c=src_c, slot=slot, t0=t0, tw=tw, pbi=pbi,
                    first=(pi == 0), last=(pi == len(parts) - 1): e.matmul(
                    out=pb[pbi][:, 0:tw],
                    lhsT=wbuf[:, slot * 128:(slot + 1) * 128],
                    rhs=yat[:, src_c * T + t0:src_c * T + t0 + tw],
                    start=first, stop=last),
                    reads=["wb:o", f"l2_ya:{src_c}"], writes=[f"pb{pbi}"],
                    tag="l2_o_mm")
            bo = (b * len(tts) + ti) % 2
            p.v(lambda e, b=b, tw=tw, pbi=pbi, bo=bo: e.tensor_scalar(
                out=ob[bo][:, 0:tw], in0=pb[pbi][:, 0:tw],
                scalar1=bias[:, 3 * NCH + b:3 * NCH + b + 1], scalar2=None,
                op0=ALU.add),
                reads=[f"pb{pbi}", "l2_bias"],
                writes=[f"l2_ob{bo}", f"ln_xs{bo}"], tag="l2_o_ev")
            p.dma("gpsimd", lambda e, b=b, t0=t0, tw=tw, bo=bo: e.dma_start(
                out=xout[b * 128:(b + 1) * 128, t0:t0 + tw],
                in_=ob[bo][:, 0:tw], accum_op=ALU.add),
                reads=[f"l2_ob{bo}", f"ln_xs{bo}"], writes=[f"xout:{b}"],
                stream=f"l2_xo{b % 4}", tag="l2_o_acc")
    L.p.wait_all("gpsimd", [f"xout:{a}" for a in range(NCH)], tag="l2_final")


def _l2_mask_host(T):
    allowed = set(range(5)) | {8 << m for m in range(20) if (8 << m) < T}
    m = np.full((T, NSEG * 128), -1e30, np.float32)
    for i in range(T // 128):
        js = [i - 8, i - 4, i - 2, i - 1, i]
        for s, j in enumerate(js):
            if j < 0:
                continue
            for r in range(128):
                q = i * 128 + r
                for c in range(128):
                    k = j * 128 + c
                    d = q - k
                    if d >= 0 and d in allowed:
                        m[q, s * 128 + c] = 0.0
    return m


_L2_MASK_CACHE = {}


def prep_l2(inputs):
    c8 = lambda v: np.ascontiguousarray(np.asarray(v).reshape(NCH, 128).T)
    d = {}
    for pnm, tag in (("q", "q"), ("k", "k"), ("v", "v"), ("o", "o")):
        d[f"{tag}wv"] = np.asarray(inputs[f"L2_attn_{pnm}_Wv"]).reshape(8 * 128, 128)
        d[f"{tag}wr"] = np.asarray(inputs[f"L2_attn_{pnm}_Wr"]).reshape(7 * 128, 128)
        d[f"{tag}wl"] = np.asarray(inputs[f"L2_attn_{pnm}_Wl"]).reshape(7 * 128, 128)
        d[f"{tag}b"] = c8(inputs[f"L2_attn_{pnm}_b"])
    d["vbrow"] = np.asarray(inputs["L2_attn_v_b"]).reshape(1, 1024)
    d["fwt"] = np.ascontiguousarray(np.asarray(inputs["L2_ffn_weights"]).T)
    d["fsc"] = c8(inputs["L2_ffn_scale"])
    d["fbi"] = c8(inputs["L2_ffn_bias"])
    d["idf"] = np.eye(128, dtype=np.float32)
    return d


L2_DRAM_SPECS = (
    [(f"{t}wv", [8 * 128, 128]) for t in "qkvo"]
    + [(f"{t}wr", [7 * 128, 128]) for t in "qkvo"]
    + [(f"{t}wl", [7 * 128, 128]) for t in "qkvo"]
    + [(f"{t}b", [128, NCH]) for t in "qkvo"]
    + [("vbrow", [1, 1024]), ("fwt", [1024, 1024]), ("fsc", [128, NCH]),
       ("fbi", [128, NCH]), ("idf", [128, 128]), ("mask", None)]
)



RMS_EPS = 1.1920928955078125e-07


def build_l3(L, aps, xin, xout):
    nc, p, T = L.nc, L.p, L.T
    pb, pbt = L.pb, L.pbt
    tts = t_tiles(T)
    n_tc = T // 128
    p.dma("sync", lambda e: e.dma_start(out=xout, in_=xin),
          writes=[f"xout:{a}" for a in range(NCH)], stream="xout_init",
          tag="l3_xout_init")
    L.layernorm_streaming(xin)
    xn = L.xn

    wstg2 = [L.sbuf(f"l3_wstg{i}", [128, 1024], F32) for i in range(2)]
    wmb2 = [L.sbuf(f"l3_wmb{i}", [128, 1024], BF16) for i in range(2)]
    wstg = wstg2[0]
    wmb = wmb2[0]

    def load_slice(name, c0, cw, tag, b=0):
        p.dma("sync", lambda e, b=b: e.dma_start(
            out=wstg2[b][:, 0:NCH * cw].rearrange("p (a f) -> p a f", a=NCH),
            in_=aps[name].rearrange("(a p) f -> p a f", p=128)[:, :, c0:c0 + cw]),
            writes=[f"l3_wstg{b}"], tag=f"l3_lds_{tag}")
        p.v(lambda e, b=b: e.tensor_copy(out=wmb2[b][:, 0:NCH * cw],
                                         in_=wstg2[b][:, 0:NCH * cw]),
            reads=[f"l3_wstg{b}"], writes=[f"l3_wmb{b}"], tag=f"l3_cvs_{tag}")

    qf = L.sbuf("l3_qf", [128, NCH * T], BF16)
    kf = L.sbuf("l3_kf", [128, NCH * T], BF16)
    vtok = L.sbuf("l3_vt", [128, n_tc * 1024], BF16)
    yat = L.xn

    wvb = (kf[:, 0:NCH * 512] if T >= 512
           else L.sbuf("l3_wvbS", [128, NCH * 512], BF16))
    for eh in range(2):
        for a in range(NCH):
            b = a % 2
            p.dma("sync", lambda e, a=a, eh=eh, b=b: e.dma_start(
                out=wstg2[b][:, 0:512],
                in_=aps["wvt"][a * 128:(a + 1) * 128, eh * 512:(eh + 1) * 512]),
                writes=[f"l3_wstg{b}"], tag="l3_ldwv")
            p.v(lambda e, a=a, b=b: e.tensor_copy(
                out=wvb[:, a * 512:(a + 1) * 512], in_=wstg2[b][:, 0:512]),
                reads=[f"l3_wstg{b}"], writes=["l3_wvb"], tag="l3_cvwv")
        for tc in range(n_tc):
            pbi = tc % 2
            for a in range(NCH):
                p.t(lambda e, a=a, tc=tc, pbi=pbi: e.matmul(
                    out=pb[pbi][:],
                    lhsT=xn[:, a * T + tc * 128:a * T + (tc + 1) * 128],
                    rhs=wvb[:, a * 512:(a + 1) * 512],
                    start=(a == 0), stop=(a == NCH - 1)),
                    reads=["l3_wvb", f"xn:{a}"], writes=[f"pb{pbi}"], tag="l3_vt_mm")
            p.v(lambda e, tc=tc, eh=eh, pbi=pbi: e.tensor_copy(
                out=vtok[:, tc * 1024 + eh * 512:tc * 1024 + (eh + 1) * 512],
                in_=pb[pbi][:]),
                reads=[f"pb{pbi}"], writes=[f"l3_vt:{tc}"], tag="l3_vt_ev")

    for wname, dst, dkey in (("wqt", qf, "l3_qf"), ("wkt", kf, "l3_kf:{}")):
        for oc in range(NCH):
            wb = oc % 2
            load_slice(wname, oc * 128, 128, f"{wname}{oc}", wb)
            for ti, (t0, tw) in enumerate(tts):
                pbi = ti % 2
                for a in range(NCH):
                    p.t(lambda e, a=a, t0=t0, tw=tw, pbi=pbi, wb=wb: e.matmul(
                        out=pb[pbi][:, 0:tw],
                        lhsT=wmb2[wb][:, a * 128:(a + 1) * 128],
                        rhs=xn[:, a * T + t0:a * T + t0 + tw],
                        start=(a == 0), stop=(a == NCH - 1)),
                        reads=[f"l3_wmb{wb}", f"xn:{a}"], writes=[f"pb{pbi}"],
                        tag=f"l3_{wname}_mm")
                p.v(lambda e, oc=oc, t0=t0, tw=tw, pbi=pbi, dst=dst,
                    dkey=dkey: e.tensor_copy(
                    out=dst[:, oc * T + t0:oc * T + t0 + tw], in_=pb[pbi][:, 0:tw]),
                    reads=[f"pb{pbi}"],
                    writes=([dkey.format(oc) if "{" in dkey else dkey]
                            + (["l3_wvb"] if wname == "wkt" and oc < 2 else [])),
                    tag=f"l3_{wname}_ev")
    prm = L.sbuf("l3_prm", [128, 4], F32)
    bm2c = L.sbuf("l3_bm2", [16, 1], F32)
    p.dma("sync", lambda e: e.dma_start(out=prm[:, 0:2], in_=aps["bm1"]),
          writes=["l3_bm1"], tag="l3_ldbm1")
    p.dma("sync", lambda e: e.dma_start(out=bm2c[:], in_=aps["bm2"]),
          writes=["l3_bm2"], tag="l3_ldbm2")
    wm2 = L.sbuf("l3_wm2", [128, 2 * 16], BF16)
    p.dma("sync", lambda e: e.dma_start(
        out=wstg[:, 0:32].rearrange("p (a r) -> p a r", a=2),
        in_=aps["wm2t"].rearrange("(a p) r -> p a r", p=128)),
        writes=["l3_wstg0"], tag="l3_ldm2")
    p.v(lambda e: e.tensor_copy(out=wm2[:], in_=wstg[:, 0:32]),
        reads=["l3_wstg0"], writes=["l3_wm2"], tag="l3_cvm2")
    mht = L.sbuf("l3_mh", [128, 2 * 512], BF16)
    lam = L.sbuf("l3_lamt", [16, 512], F32)
    lam_hbm = nc.dram_tensor("l3_lam_hbm", [16, T], F32, kind="Internal").ap()
    Bm = L.sbuf("l3_B", [16, T], F32)
    mtX = L.sbuf("l3_mt", [128, 512], F32)
    s1X = L.sbuf("l3_s1", [128, 512], F32)
    s2X = L.sbuf("l3_s2", [128, 512], F32)
    mt = mtX[0:16, :]
    s1 = s1X[0:16, :]
    s2 = s2X[0:16, :]
    dcar = L.sbuf("l3_dcar", [16, 1], F32)
    dcar2 = L.sbuf("l3_dcar2", [16, 1], F32)
    p.v(lambda e: e.memset(dcar[:], 0.0), writes=["l3_dcar"], tag="l3_dc0")
    for ti, (t0, tw) in enumerate(tts):
        for hc in range(2):
            load_slice("wm1t", hc * 128, 128, f"wm1_{ti}_{hc}", hc)
            pbi = hc % 2
            for a in range(NCH):
                p.t(lambda e, a=a, t0=t0, tw=tw, pbi=pbi, hc=hc: e.matmul(
                    out=pb[pbi][:, 0:tw],
                    lhsT=wmb2[hc][:, a * 128:(a + 1) * 128],
                    rhs=xn[:, a * T + t0:a * T + t0 + tw],
                    start=(a == 0), stop=(a == NCH - 1)),
                    reads=[f"l3_wmb{hc}", f"xn:{a}"], writes=[f"pb{pbi}"],
                    tag="l3_mh_mm")
            p.s(lambda e, hc=hc, tw=tw, pbi=pbi: e.activation(
                out=mht[:, hc * 512:hc * 512 + tw], in_=pb[pbi][:, 0:tw],
                func=AF.Relu, bias=prm[:, hc:hc + 1], scale=1.0),
                reads=[f"pb{pbi}", "l3_bm1"], writes=["l3_mh"], tag="l3_mh_ev")
        pbi = ti % 2
        for hc in range(2):
            p.t(lambda e, hc=hc, tw=tw, pbi=pbi: e.matmul(
                out=pb[pbi][0:16, 0:tw],
                lhsT=wm2[:, hc * 16:(hc + 1) * 16],
                rhs=mht[:, hc * 512:hc * 512 + tw],
                start=(hc == 0), stop=(hc == 1)),
                reads=["l3_wm2", "l3_mh"], writes=[f"pb{pbi}"], tag="l3_m_mm")
        p.v(lambda e, t0=t0, tw=tw, pbi=pbi: e.tensor_scalar(
            out=mt[:, 0:tw], in0=pb[pbi][0:16, 0:tw], scalar1=bm2c[:],
            scalar2=None, op0=ALU.add),
            reads=[f"pb{pbi}", "l3_bm2"], writes=["l3_mt"], tag="l3_m_ev")
        p.s(lambda e, tw=tw: e.activation(out=s1[:, 0:tw], in_=mt[:, 0:tw],
                                          func=AF.Abs, bias=0.0, scale=1.0),
            reads=["l3_mt"], writes=["l3_s1"], tag="l3_abs")
        p.s(lambda e, tw=tw: e.activation(out=s1[:, 0:tw], in_=s1[:, 0:tw],
                                          func=AF.Exp, bias=0.0, scale=-1.0),
            reads=["l3_s1"], writes=["l3_s1"], tag="l3_expn")
        p.s(lambda e, tw=tw: e.activation(out=s1[:, 0:tw], in_=s1[:, 0:tw],
                                          func=AF.Ln, bias=1.0, scale=1.0),
            reads=["l3_s1"], writes=["l3_s1"], tag="l3_ln1p")
        p.s(lambda e, tw=tw: e.activation(out=s2[:, 0:tw], in_=mt[:, 0:tw],
                                          func=AF.Relu, bias=0.0, scale=1.0),
            reads=["l3_mt"], writes=["l3_s2"], tag="l3_rp")
        p.v(lambda e, tw=tw: e.tensor_tensor(out=s2[:, 0:tw], in0=s2[:, 0:tw],
                                             in1=s1[:, 0:tw], op=ALU.add),
            reads=["l3_s2", "l3_s1"], writes=["l3_s2"], tag="l3_lnd0")
        p.v(lambda e, tw=tw: e.tensor_scalar(out=s2[:, 0:tw], in0=s2[:, 0:tw],
                                             scalar1=-1.0, scalar2=None,
                                             op0=ALU.mult),
            reads=["l3_s2"], writes=["l3_s2"], tag="l3_lnd")
        p.v(lambda e, tw=tw: e.tensor_tensor_scan(
            out=lam[:, 0:tw], data0=s2[:, 0:tw], data1=mt[:, 0:tw],
            initial=dcar[:], op0=ALU.add, op1=ALU.bypass),
            reads=["l3_s2", "l3_mt", "l3_dcar"], writes=["l3_lam"], tag="l3_cums")
        p.s(lambda e, tw=tw: e.activation(out=mt[:, 0:tw], in_=mt[:, 0:tw],
                                          func=AF.Relu, bias=0.0, scale=-1.0),
            reads=["l3_mt"], writes=["l3_mt"], tag="l3_rn")
        p.v(lambda e, tw=tw: e.tensor_tensor(out=s1[:, 0:tw], in0=mt[:, 0:tw],
                                             in1=s1[:, 0:tw], op=ALU.add),
            reads=["l3_mt", "l3_s1"], writes=["l3_s1"], tag="l3_lng0")
        p.v(lambda e, t0=t0, tw=tw: e.scalar_tensor_tensor(
            out=Bm[:, t0:t0 + tw], in0=s1[:, 0:tw], scalar=-1.0,
            in1=lam[:, 0:tw], op0=ALU.mult, op1=ALU.subtract),
            reads=["l3_s1", "l3_lam"], writes=["l3_B"], tag="l3_Bm")
        p.v(lambda e, tw=tw: e.tensor_copy(out=dcar2[:], in_=lam[:, tw - 1:tw]),
            reads=["l3_lam"], writes=["l3_dcar2"], tag="l3_dc2")
        for c in reversed(range(tw // 128)):
            sc = (dcar[:] if c == 0 else lam[:, c * 128 - 1:c * 128])
            sck = "l3_dcar" if c == 0 else "l3_lam"
            p.v(lambda e, c=c, t0=t0, sc=sc: e.tensor_scalar(
                out=Bm[:, t0 + c * 128:t0 + (c + 1) * 128],
                in0=Bm[:, t0 + c * 128:t0 + (c + 1) * 128],
                scalar1=sc, scalar2=None, op0=ALU.add),
                reads=["l3_B", sck], writes=["l3_B"], tag="l3_Brel")
            p.v(lambda e, c=c, sc=sc: e.tensor_scalar(
                out=lam[:, c * 128:(c + 1) * 128],
                in0=lam[:, c * 128:(c + 1) * 128],
                scalar1=sc, scalar2=None, op0=ALU.subtract),
                reads=["l3_lam", sck], writes=["l3_lam"], tag="l3_Lrel")
        p.dma("sync", lambda e, t0=t0, tw=tw: e.dma_start(
            out=lam_hbm[:, t0:t0 + tw], in_=lam[:, 0:tw]),
            reads=["l3_lam"], writes=["l3_lamhbm"], stream="l3_lamhbm",
            tag="l3_lam_st")
        p.v(lambda e: e.tensor_copy(out=dcar[:], in_=dcar2[:]),
            reads=["l3_dcar2"], writes=["l3_dcar"], tag="l3_dcaru")
    btok = L.sbuf("l3_btok", [128, n_tc * 16], F32)
    idf = L.sbuf("l3_idf", [128, 128], F32)
    idb = L.sbuf("l3_idb", [128, 128], BF16)
    trim = L.sbuf("l3_trim", [128, 128], F32)
    p.dma("sync", lambda e: e.dma_start(out=idf[:], in_=aps["idf"]),
          writes=["l3_idf"], tag="l3_ldid")
    p.dma("sync", lambda e: e.dma_start(out=trim[:], in_=aps["trimask"]),
          writes=["l3_trim"], tag="l3_ldtm")
    p.v(lambda e: e.tensor_copy(out=idb[:], in_=idf[:]), reads=["l3_idf"],
        writes=["l3_idb"], tag="l3_cvid")
    for tc in range(n_tc):
        pbi = tc % 2
        p.t(lambda e, tc=tc, pbi=pbi: e.transpose(
            out=pb[pbi][0:128, 0:16], in_=Bm[0:16, tc * 128:(tc + 1) * 128],
            identity=idf[0:16, 0:16]),
            reads=["l3_B", "l3_idf"], writes=[f"pb{pbi}"], tag="l3_btr")
        p.v(lambda e, tc=tc, pbi=pbi: e.tensor_copy(
            out=btok[:, tc * 16:(tc + 1) * 16], in_=pb[pbi][0:128, 0:16]),
            reads=[f"pb{pbi}"], writes=["l3_btok"], tag="l3_btr_ev")

    lrow = L.sbuf("l3_lrow", [1, T], F32)
    dt_sb = L.sbuf("l3_dt", [128, 128], F32)
    dc_sb = L.sbuf("l3_dc", [128, 128], F32)
    atw = L.sbuf("l3_atw", [128, 128], BF16)
    acr = L.sbuf("l3_acr", [64, 128], BF16)
    ktr = L.sbuf("l3_ktr", [128, 64], BF16)
    ks = L.sbuf("l3_ks", [128, 64], BF16)
    stf = L.sbuf("l3_stf", [64, 16 * 64], F32)
    stb = L.sbuf("l3_stb", [64, 16 * 64], BF16)
    p.v(lambda e: e.memset(stf[:], 0.0), writes=["l3_stf"], tag="l3_st0")
    p.v(lambda e: e.memset(stb[:], 0.0), writes=["l3_stb"], tag="l3_stb0")
    for h in range(16):
        ec, hh = h // 2, h % 2
        r0 = hh * 64
        p.dma("sync", lambda e, h=h: e.dma_start(
            out=lrow[:], in_=None), reads=["l3_lam"], writes=["l3_lrow"],
            stream="l3_lrow", tag="noemit") if False else None
        p.dma("sync", lambda e, h=h: e.dma_start(
            out=lrow[0:1, :], in_=lam_hbm[h:h + 1, :]),
            reads=["l3_lamhbm"], writes=["l3_lrow"], stream="l3_lrow",
            tag="l3_lrowdma")
        sth = stf[:, h * 64:(h + 1) * 64]
        stbh = stb[:, h * 64:(h + 1) * 64]
        for tc in range(n_tc):
            p.t(lambda e, tc=tc: e.matmul(
                out=pb[2][:, 0:128], lhsT=L.ones_row[0:1, 0:128],
                rhs=lrow[0:1, tc * 128:(tc + 1) * 128], start=True, stop=True),
                reads=["l3_lrow", "ones_row"], writes=["pb2"], tag="l3_bcA")
            p.t(lambda e, tc=tc: e.matmul(
                out=pb[3][:, 0:128], lhsT=L.ones_row[0:1, 0:128],
                rhs=lrow[0:1, tc * 128:(tc + 1) * 128], start=True, stop=False),
                reads=["l3_lrow", "ones_row"], writes=["pb3"], tag="l3_bcB0")
            p.t(lambda e: e.matmul(
                out=pb[3][:, 0:128], lhsT=idf[:], rhs=trim[:],
                start=False, stop=True),
                reads=["l3_trim", "l3_idf"], writes=["pb3"], tag="l3_bcB1")
            p.s(lambda e, tc=tc, h=h: e.activation(
                out=dt_sb[:], in_=pb[3][:, 0:128], func=AF.Exp,
                bias=btok[:, tc * 16 + h:tc * 16 + h + 1], scale=1.0),
                reads=["pb3", "l3_btok"], writes=["l3_dt"], tag="l3_dt")
            p.s(lambda e: e.activation(
                out=dc_sb[:], in_=pb[2][:, 0:128], func=AF.Exp,
                bias=0.0, scale=1.0),
                reads=["pb2"], writes=["l3_dc"], tag="l3_dcx")
            p.t(lambda e, ec=ec, r0=r0, tc=tc: e.matmul(
                out=pb[4][:, 0:128],
                lhsT=kf[r0:r0 + 64, ec * T + tc * 128:ec * T + (tc + 1) * 128],
                rhs=qf[r0:r0 + 64, ec * T + tc * 128:ec * T + (tc + 1) * 128],
                start=True, stop=True),
                reads=[f"l3_kf:{ec}", "l3_qf"], writes=["pb4"], tag="l3_sc")
            p.v(lambda e: e.tensor_tensor(
                out=atw[:], in0=pb[4][:, 0:128], in1=dt_sb[:], op=ALU.mult),
                reads=["pb4", "l3_dt"], writes=["l3_atw"], tag="l3_atw")
            p.v(lambda e, ec=ec, r0=r0, tc=tc: e.tensor_tensor(
                out=acr[:],
                in0=qf[r0:r0 + 64, ec * T + tc * 128:ec * T + (tc + 1) * 128],
                in1=dc_sb[r0:r0 + 64, :], op=ALU.mult),
                reads=["l3_qf", "l3_dc"], writes=["l3_acr"], tag="l3_acr")
            p.t(lambda e, tc=tc, h=h: e.matmul(
                out=pb[5][0:64, 0:128],
                lhsT=vtok[:, tc * 1024 + h * 64:tc * 1024 + h * 64 + 64],
                rhs=atw[:], start=True, stop=False),
                reads=[f"l3_vt:{tc}", "l3_atw"], writes=["pb5"], tag="l3_av")
            p.t(lambda e, h=h: e.matmul(
                out=pb[5][0:64, 0:128], lhsT=stb[:, h * 64:(h + 1) * 64],
                rhs=acr[:], start=False, stop=True),
                reads=["l3_stb", "l3_acr"], writes=["pb5"], tag="l3_avc")
            p.s(lambda e, r0=r0, ec=ec, tc=tc: e.copy(
                out=yat[r0:r0 + 64, ec * T + tc * 128:ec * T + (tc + 1) * 128],
                in_=pb[5][0:64, 0:128]),
                reads=["pb5"], writes=[f"l3_ya:{ec}", f"xn:{ec}"], tag="l3_yev")
            p.t(lambda e, ec=ec, r0=r0, tc=tc: e.transpose(
                out=pbt[:, 0:64],
                in_=kf[r0:r0 + 64, ec * T + tc * 128:ec * T + (tc + 1) * 128],
                identity=idb[r0:r0 + 64, r0:r0 + 64]),
                reads=[f"l3_kf:{ec}", "l3_idb"], writes=["pbt0"], tag="l3_ktr")
            p.v(lambda e: e.tensor_copy(out=ktr[:], in_=pbt[:, 0:64]),
                reads=["pbt0"], writes=["l3_ktr"], tag="l3_ktrev")
            p.v(lambda e: e.tensor_scalar(
                out=ks[:], in0=ktr[:], scalar1=dt_sb[:, 127:128], scalar2=None,
                op0=ALU.mult),
                reads=["l3_ktr", "l3_dt"], writes=["l3_ks"], tag="l3_ks")
            p.t(lambda e, tc=tc, h=h: e.matmul(
                out=pb[0][0:64, 0:64], lhsT=ks[:],
                rhs=vtok[:, tc * 1024 + h * 64:tc * 1024 + h * 64 + 64],
                start=True, stop=True),
                reads=["l3_ks", f"l3_vt:{tc}"], writes=["pb0"], tag="l3_kv")
            p.v(lambda e, sth=sth: e.scalar_tensor_tensor(
                out=sth, in0=sth, scalar=dc_sb[0:64, 127:128],
                in1=pb[0][0:64, 0:64],
                op0=ALU.mult, op1=ALU.add),
                reads=["l3_stf", "l3_dc", "pb0"], writes=["l3_stf"], tag="l3_stup")
            p.v(lambda e, sth=sth, stbh=stbh: e.tensor_copy(out=stbh, in_=sth),
                reads=["l3_stf"], writes=["l3_stb"], tag="l3_stbcv")

    y2bf = qf
    _ob0 = L.ln_xs[0]
    ob = [_ob0, _ob0]
    sq = L.xn_scratch_a
    rms_row = Bm[0:1, 0:T]
    for oc in range(NCH):
        wb = oc % 2
        load_slice("wprojt", oc * 128, 128, f"pj{oc}", wb)
        for ti, (t0, tw) in enumerate(tts):
            pbi = oc % 2
            for a in range(NCH):
                p.t(lambda e, a=a, oc=oc, t0=t0, tw=tw, pbi=pbi, wb=wb: e.matmul(
                    out=pb[pbi][:, 0:tw],
                    lhsT=wmb2[wb][:, a * 128:(a + 1) * 128],
                    rhs=yat[:, a * T + t0:a * T + t0 + tw],
                    start=(a == 0), stop=(a == NCH - 1)),
                    reads=[f"l3_wmb{wb}", f"l3_ya:{a}"], writes=[f"pb{pbi}"],
                    tag="l3_pj_mm")
            bo = oc % 2
            p.v(lambda e, tw=tw, pbi=pbi, bo=bo: e.tensor_copy(
                out=ob[bo][:, 0:tw], in_=pb[pbi][:, 0:tw]),
                reads=[f"pb{pbi}"], writes=["l3_ob0", "ln_xs0"], tag="l3_pj_ev")
            p.dma("gpsimd", lambda e, oc=oc, t0=t0, tw=tw, bo=bo: e.dma_start(
                out=xout[oc * 128:(oc + 1) * 128, t0:t0 + tw],
                in_=ob[bo][:, 0:tw], accum_op=ALU.add),
                reads=["l3_ob0", "ln_xs0"], writes=[f"xout:{oc}"],
                stream=f"l3_xo{oc % 4}", tag="l3_pj_acc")
            p.v(lambda e, oc=oc, t0=t0, tw=tw, bo=bo: e.tensor_copy(
                out=y2bf[:, oc * T + t0:oc * T + t0 + tw], in_=ob[bo][:, 0:tw]),
                reads=["l3_ob0"], writes=["l3_y2", "l3_qf"], tag="l3_y2cv")
            p.s(lambda e, tw=tw, bo=bo: e.activation(
                out=sq[:, 0:tw], in_=ob[bo][:, 0:tw], func=AF.Square,
                bias=0.0, scale=1.0),
                reads=["l3_ob0"], writes=["l3_sq", "ln_x2"], tag="l3_sqev")
            p.t(lambda e, ti=ti, tw=tw, oc=oc: e.matmul(
                out=pb[2 + ti][0:1, 0:tw],
                lhsT=L.ones_col, rhs=sq[:, 0:tw],
                start=(oc == 0), stop=(oc == NCH - 1)),
                reads=["l3_sq", "cst"], writes=[f"pb{2 + ti}"], tag="l3_rms_mm")
    for ti, (t0, tw) in enumerate(t_tiles(T)):
        p.s(lambda e, ti=ti, t0=t0, tw=tw: e.copy(
            out=rms_row[:, t0:t0 + tw], in_=pb[2 + ti][0:1, 0:tw]),
            reads=[f"pb{2 + ti}"], writes=["l3_rmsr", "l3_B"], tag="l3_rms_ev")
    L.cap(RMS_EPS)
    p.s(lambda e: e.activation(out=rms_row[:], in_=rms_row[:], func=AF.Ln,
                               bias=RMS_EPS, scale=1.0 / 1024.0),
        reads=["l3_rmsr", "fconst"], writes=["l3_rmsr"], tag="l3_rms_ln")
    p.s(lambda e: e.activation(out=rms_row[:], in_=rms_row[:], func=AF.Exp,
                               bias=0.0, scale=-0.5),
        reads=["l3_rmsr", "fconst"], writes=["l3_rmsr"], tag="l3_rms_ex")
    for ti, (t0, tw) in enumerate(tts):
        p.t(lambda e, t0=t0, tw=tw: e.matmul(
            out=pb[2][:, 0:tw], lhsT=L.ones_row[0:1, 0:128],
            rhs=rms_row[:, t0:t0 + tw], start=True, stop=True),
            reads=["l3_rmsr", "ones_row"], writes=["pb2"], tag="l3_rms_bc")
        for a in range(NCH):
            p.v(lambda e, a=a, t0=t0, tw=tw: e.tensor_tensor(
                out=y2bf[:, a * T + t0:a * T + t0 + tw],
                in0=y2bf[:, a * T + t0:a * T + t0 + tw],
                in1=pb[2][:, 0:tw], op=ALU.mult),
                reads=["l3_y2", "pb2"], writes=["l3_y2", "l3_qf"], tag="l3_yn")

    hid = kf if NCH * T >= 32 * 512 else L.sbuf("l3_hidS", [128, 32 * 512], BF16)
    for ti, (t0, tw) in enumerate(tts):
        for fc in range(32):
            wb = fc % 2
            p.dma("sync", lambda e, fc=fc, wb=wb: e.dma_start(
                out=wstg2[wb][:, 0:1024].rearrange("p (a f) -> p a f", a=8),
                in_=aps["wfct"].rearrange("(a p) f -> p a f", p=128)[:, :, fc * 128:(fc + 1) * 128]),
                writes=[f"l3_wstg{wb}"], tag="l3_ldfc")
            p.v(lambda e, wb=wb: e.tensor_copy(out=wmb2[wb][:], in_=wstg2[wb][:]),
                reads=[f"l3_wstg{wb}"], writes=[f"l3_wmb{wb}"], tag="l3_cvfc")
            pbi = fc % 2
            for a in range(NCH):
                p.t(lambda e, a=a, t0=t0, tw=tw, pbi=pbi, wb=wb: e.matmul(
                    out=pb[pbi][:, 0:tw],
                    lhsT=wmb2[wb][:, a * 128:(a + 1) * 128],
                    rhs=y2bf[:, a * T + t0:a * T + t0 + tw],
                    start=(a == 0), stop=(a == NCH - 1)),
                    reads=[f"l3_wmb{wb}", "l3_y2"], writes=[f"pb{pbi}"],
                    tag="l3_fc_mm")
            p.s(lambda e, fc=fc, tw=tw, pbi=pbi: e.activation(
                out=sq[:, 0:tw], in_=pb[pbi][:, 0:tw], func=AF.Relu,
                bias=0.0, scale=1.0),
                reads=[f"pb{pbi}"], writes=["l3_sq"], tag="l3_fc_r")
            p.s(lambda e, fc=fc, tw=tw: e.activation(
                out=hid[:, fc * 512:fc * 512 + tw], in_=sq[:, 0:tw],
                func=AF.Square, bias=0.0, scale=1.0),
                reads=["l3_sq"], writes=["l3_hid", "l3_kf:0"], tag="l3_fc_s")
        for oh in range(2):
            for fc in range(32):
                wb = fc % 2
                p.dma("sync", lambda e, fc=fc, oh=oh, wb=wb: e.dma_start(
                    out=wstg2[wb][:, 0:512],
                    in_=aps["wcpt"][fc * 128:(fc + 1) * 128,
                                    oh * 512:(oh + 1) * 512]),
                    writes=[f"l3_wstg{wb}"], tag="l3_ldcp")
                p.v(lambda e, wb=wb: e.tensor_copy(out=wmb2[wb][:, 0:512],
                                                   in_=wstg2[wb][:, 0:512]),
                    reads=[f"l3_wstg{wb}"], writes=[f"l3_wmb{wb}"], tag="l3_cvcp")
                for o4 in range(4):
                    p.t(lambda e, fc=fc, o4=o4, tw=tw, wb=wb: e.matmul(
                        out=pb[2 + o4][:, 0:tw],
                        lhsT=wmb2[wb][:, o4 * 128:(o4 + 1) * 128],
                        rhs=hid[:, fc * 512:fc * 512 + tw],
                        start=(fc == 0), stop=(fc == 31)),
                        reads=[f"l3_wmb{wb}", "l3_hid"], writes=[f"pb{2 + o4}"],
                        tag="l3_cp_mm")
            for o4 in range(4):
                oc = oh * 4 + o4
                bo = o4 % 2
                p.v(lambda e, o4=o4, tw=tw, bo=bo: e.tensor_copy(
                    out=ob[bo][:, 0:tw], in_=pb[2 + o4][:, 0:tw]),
                    reads=[f"pb{2 + o4}"], writes=["l3_ob0", "ln_xs0"], tag="l3_cp_ev")
                p.dma("gpsimd", lambda e, oc=oc, t0=t0, tw=tw, bo=bo: e.dma_start(
                    out=xout[oc * 128:(oc + 1) * 128, t0:t0 + tw],
                    in_=ob[bo][:, 0:tw], accum_op=ALU.add),
                    reads=["l3_ob0", "ln_xs0"], writes=[f"xout:{oc}"],
                    stream=f"l3_xo{oc % 4}", tag="l3_cp_acc")

    stat2 = L.ln_stat2
    xs = L.ln_xs
    x2r = L.xn_scratch_a
    nld = 0
    for ti, (t0, tw) in enumerate(tts):
        p.t(lambda e, t0=t0, tw=tw: e.matmul(
            out=pb[2][:, 0:tw], lhsT=L.ones_row[0:1, 0:128],
            rhs=stat2[0:1, T + t0:T + t0 + tw], start=True, stop=True),
            reads=["ln_q", "ones_row"], writes=["pb2"], tag="l3_rb_bcr")
        p.t(lambda e, t0=t0, tw=tw: e.matmul(
            out=pb[3][:, 0:tw], lhsT=L.ones_row[0:1, 0:128],
            rhs=stat2[0:1, t0:t0 + tw], start=True, stop=True),
            reads=["ln_m", "ones_row"], writes=["pb3"], tag="l3_rb_bcm")
        for a in range(NCH):
            b = nld % 2
            nld += 1
            p.dma("sync", lambda e, a=a, t0=t0, tw=tw, b=b: e.dma_start(
                out=xs[b][:, 0:tw],
                in_=xin[a * 128:(a + 1) * 128, t0:t0 + tw]),
                writes=[f"ln_xs{b}"], tag="l3_rb_ldx")
            p.v(lambda e, tw=tw, b=b: e.tensor_tensor(
                out=x2r[:, 0:tw], in0=xs[b][:, 0:tw], in1=pb[2][:, 0:tw],
                op=ALU.mult),
                reads=[f"ln_xs{b}", "pb2"], writes=["ln_x2"], tag="l3_rb_xr")
            p.v(lambda e, a=a, t0=t0, tw=tw: e.tensor_tensor(
                out=xn[:, a * T + t0:a * T + t0 + tw], in0=x2r[:, 0:tw],
                in1=pb[3][:, 0:tw], op=ALU.subtract),
                reads=["ln_x2", "pb3"],
                writes=[f"xn:{a}", f"l3_ya:{a}"], tag="l3_rb_xn")

    rt = mtX[:, 0:n_tc * 8]
    wsel = mtX[:, 256:256 + n_tc * 8]
    mxp = s1X[:, 0:4]
    exf = s1X[:, 8:16]
    g8 = s2X[0:8, 0:128]
    g8_hbm = nc.dram_tensor("l3_g8_hbm", [8, n_tc * 128], F32, kind="Internal").ap()
    ge_row = lrow
    gw_sb = L.xn_scratch_a[:, 0:512]
    wr32 = L.sbuf("l3_wr32", [128, NCH * 8], F32)
    br_32 = L.sbuf("l3_br32", [1, 8], F32)
    p.dma("sync", lambda e: e.dma_start(
        out=wr32[:].rearrange("p (a r) -> p a r", a=NCH),
        in_=aps["wr"].rearrange("(a p) r -> p a r", p=128)),
        writes=["l3_wr32"], tag="l3_ldwr")
    p.dma("sync", lambda e: e.dma_start(out=br_32[:], in_=aps["br"]),
          writes=["l3_br32"], tag="l3_ldbr")
    for a in range(NCH):
        p.t(lambda e, a=a: e.matmul(
            out=pb[4][0:1, 0:8], lhsT=L.ones_col, rhs=wr32[:, a * 8:(a + 1) * 8],
            start=(a == 0), stop=(a == NCH - 1)),
            reads=["l3_wr32", "cst"], writes=["pb4"], tag="l3_sr_mm")
    srow = L.sbuf("l3_srow", [1, 8], F32)
    srow_bc = L.sbuf("l3_srbc", [128, 8], F32)
    p.v(lambda e: e.tensor_scalar(out=srow[:], in0=pb[4][0:1, 0:8],
                                  scalar1=-1.0, scalar2=None, op0=ALU.mult),
        reads=["pb4"], writes=["l3_srow"], tag="l3_sr_ev")
    p.t(lambda e: e.matmul(out=pb[4][:, 0:8], lhsT=L.ones_row[0:1, 0:128],
                           rhs=srow[:], start=True, stop=True),
        reads=["l3_srow", "ones_row"], writes=["pb4"], tag="l3_sr_bc")
    p.v(lambda e: e.tensor_copy(out=srow_bc[:], in_=pb[4][:, 0:8]),
        reads=["pb4"], writes=["l3_srbc"], tag="l3_sr_bcev")
    br_bc = L.sbuf("l3_brbc", [128, 8], F32)
    p.t(lambda e: e.matmul(out=pb[4][:, 0:8], lhsT=L.ones_row[0:1, 0:128],
                           rhs=br_32[:], start=True, stop=True),
        reads=["l3_br32", "ones_row"], writes=["pb4"], tag="l3_br_bc")
    p.v(lambda e: e.tensor_copy(out=br_bc[:], in_=pb[4][:, 0:8]),
        reads=["pb4"], writes=["l3_brbc"], tag="l3_br_bcev")
    rmcols = L.sbuf("l3_rmc", [128, 2 * n_tc], F32)
    for tc in range(n_tc):
        p.t(lambda e, tc=tc: e.transpose(
            out=pb[4][:, 0:1], in_=stat2[0:1, T + tc * 128:T + (tc + 1) * 128],
            identity=L.ones_row[0:1, 0:1]),
            reads=["ln_q", "ones_row"], writes=["pb4"], tag="l3_rst_tr")
        p.v(lambda e, tc=tc: e.tensor_copy(out=rmcols[:, tc:tc + 1],
                                           in_=pb[4][:, 0:1]),
            reads=["pb4"], writes=["l3_rmc"], tag="l3_rst_ev")
        p.t(lambda e, tc=tc: e.transpose(
            out=pb[4][:, 0:1], in_=stat2[0:1, tc * 128:(tc + 1) * 128],
            identity=L.ones_row[0:1, 0:1]),
            reads=["ln_m", "ones_row"], writes=["pb4"], tag="l3_msc_tr")
        p.v(lambda e, tc=tc: e.tensor_copy(out=rmcols[:, n_tc + tc:n_tc + tc + 1],
                                           in_=pb[4][:, 0:1]),
            reads=["pb4"], writes=["l3_rmc"], tag="l3_msc_ev")
    xs_r = L.ln_xs
    nld2 = 0
    for ti, (t0, tw) in enumerate(tts):
        nsub = tw // 128
        for a in range(NCH):
            b = nld2 % 2
            nld2 += 1
            p.dma("sync", lambda e, a=a, t0=t0, tw=tw, b=b: e.dma_start(
                out=xs_r[b][:, 0:tw],
                in_=xin[a * 128:(a + 1) * 128, t0:t0 + tw]),
                writes=[f"ln_xs{b}"], tag="l3_rt_ldx")
            for su in range(nsub):
                sb_ = [0, 1, 4, 5][su]
                p.t(lambda e, a=a, su=su, b=b, sb_=sb_: e.matmul(
                    out=pb[sb_][:, 0:8],
                    lhsT=xs_r[b][:, su * 128:(su + 1) * 128],
                    rhs=wr32[:, a * 8:(a + 1) * 8],
                    start=(a == 0), stop=False),
                    reads=[f"ln_xs{b}", "l3_wr32"], writes=[f"pb{sb_}"],
                    tag="l3_rt_mm")
        for su in range(nsub):
            sb_ = [0, 1, 4, 5][su]
            p.t(lambda e, su=su, sb_=sb_: e.matmul(
                out=pb[sb_][:, 0:8], lhsT=L.ones_row[0:1, 0:128],
                rhs=br_32[:], start=False, stop=True),
                reads=["l3_br32", "ones_row"], writes=[f"pb{sb_}"],
                tag="l3_rt_b")
        for su in range(nsub):
            tc = (t0 // 128) + su
            sb_ = [0, 1, 4, 5][su]
            p.v(lambda e, su=su, sb_=sb_: e.tensor_tensor(
                out=exf[:], in0=pb[sb_][:, 0:8], in1=br_bc[:],
                op=ALU.subtract),
                reads=[f"pb{sb_}", "l3_brbc"], writes=["l3_exf", "l3_s1"],
                tag="l3_rt_x")
            p.v(lambda e, tc=tc: e.tensor_scalar(
                out=exf[:], in0=exf[:], scalar1=rmcols[:, tc:tc + 1],
                scalar2=None, op0=ALU.mult),
                reads=["l3_exf", "l3_rmc"], writes=["l3_exf", "l3_s1"],
                tag="l3_rt_a")
            p.v(lambda e, tc=tc: e.scalar_tensor_tensor(
                out=exf[:], in0=srow_bc[:], scalar=rmcols[:, n_tc + tc:n_tc + tc + 1],
                in1=exf[:], op0=ALU.mult, op1=ALU.add),
                reads=["l3_srbc", "l3_rmc", "l3_exf"], writes=["l3_exf", "l3_s1"],
                tag="l3_rt_ms")
            p.v(lambda e: e.tensor_tensor(
                out=exf[:], in0=exf[:], in1=br_bc[:], op=ALU.add),
                reads=["l3_exf", "l3_brbc"], writes=["l3_exf", "l3_s1"],
                tag="l3_rt_br")
            p.s(lambda e, tc=tc: e.activation(
                out=rt[:, tc * 8:(tc + 1) * 8], in_=exf[:], func=AF.Sigmoid,
                bias=0.0, scale=1.0),
                reads=["l3_exf"], writes=["l3_rt", "l3_mt"], tag="l3_rt_sg")
    for tc in range(n_tc):
        pbi = tc % 2
        p.v(lambda e, tc=tc: e.reduce_max(
            out=mxp[:, 0:1], in_=rt[:, tc * 8:(tc + 1) * 8], axis=AX.X),
            reads=["l3_rt"], writes=["l3_mxp", "l3_s1"], tag="l3_mx")
        p.v(lambda e, tc=tc: e.tensor_tensor_scan(
            out=wsel[:, tc * 8:(tc + 1) * 8], data0=rt[:, tc * 8:(tc + 1) * 8],
            data1=rt[:, tc * 8:(tc + 1) * 8], initial=-1e30,
            op0=ALU.max, op1=ALU.bypass),
            reads=["l3_rt"], writes=["l3_wsel", "l3_mt"], tag="l3_pfx")
        p.v(lambda e: e.memset(mxp[:, 1:2], -1e30), writes=["l3_mxp", "l3_s1"],
            tag="l3_pfx0")
        p.v(lambda e, tc=tc: e.tensor_copy(
            out=mxp[:, 2:3], in_=mxp[:, 0:1]),
            reads=["l3_mxp"], writes=["l3_mxp", "l3_s1"], tag="l3_mxcp")
        p.v(lambda e, tc=tc: e.tensor_scalar(
            out=exf[:, 1:8], in0=wsel[:, tc * 8:tc * 8 + 7],
            scalar1=mxp[:, 0:1], scalar2=None, op0=ALU.is_lt),
            reads=["l3_wsel", "l3_mxp"], writes=["l3_exf", "l3_s1"], tag="l3_exlt")
        p.v(lambda e: e.memset(exf[:, 0:1], 1.0), writes=["l3_exf", "l3_s1"], tag="l3_ex0")
        p.v(lambda e, tc=tc: e.tensor_copy(out=wsel[:, tc * 8:(tc + 1) * 8],
                                           in_=exf[:]),
            reads=["l3_exf"], writes=["l3_wsel", "l3_mt"], tag="l3_excp")
        p.v(lambda e, tc=tc: e.tensor_scalar(
            out=rt[:, tc * 8:(tc + 1) * 8], in0=rt[:, tc * 8:(tc + 1) * 8],
            scalar1=mxp[:, 0:1], scalar2=None, op0=ALU.is_ge),
            reads=["l3_rt", "l3_mxp"], writes=["l3_rt", "l3_mt"], tag="l3_isge")
        p.v(lambda e, tc=tc: e.tensor_tensor(
            out=wsel[:, tc * 8:(tc + 1) * 8], in0=wsel[:, tc * 8:(tc + 1) * 8],
            in1=rt[:, tc * 8:(tc + 1) * 8], op=ALU.mult),
            reads=["l3_wsel", "l3_rt"], writes=["l3_wsel", "l3_mt"], tag="l3_msk")
        p.v(lambda e, tc=tc: e.tensor_scalar(
            out=wsel[:, tc * 8:(tc + 1) * 8], in0=wsel[:, tc * 8:(tc + 1) * 8],
            scalar1=mxp[:, 0:1], scalar2=None, op0=ALU.mult),
            reads=["l3_wsel", "l3_mxp"], writes=["l3_wsel", "l3_mt"], tag="l3_wsc")
        p.t(lambda e, tc=tc, pbi=pbi: e.transpose(
            out=pb[2 + pbi][0:8, 0:128], in_=wsel[:, tc * 8:(tc + 1) * 8],
            identity=idf[:]),
            reads=["l3_wsel", "l3_idf"], writes=[f"pb{2 + pbi}"], tag="l3_gtr")
        p.v(lambda e, tc=tc, pbi=pbi: e.tensor_copy(
            out=g8[:], in_=pb[2 + pbi][0:8, 0:128]),
            reads=[f"pb{2 + pbi}"], writes=["l3_g8", "l3_s2"], tag="l3_gtr_ev")
        p.dma("sync", lambda e, tc=tc: e.dma_start(
            out=g8_hbm[:, tc * 128:(tc + 1) * 128], in_=g8[:]),
            reads=["l3_g8"], writes=["l3_g8hbm"], stream="l3_g8hbm",
            tag="l3_g8_st")
    for ex in range(8):
        p.dma("sync", lambda e, ex=ex: e.dma_start(
            out=ge_row[:], in_=g8_hbm[ex:ex + 1, :]),
            reads=["l3_g8hbm"], writes=["l3_ge", "l3_lrow"], stream="l3_ge",
            tag="l3_gedma")
        for ti, (t0, tw) in enumerate(tts):
            for fc in range(32):
                wb = fc % 2
                p.dma("sync", lambda e, ex=ex, fc=fc, wb=wb: e.dma_start(
                    out=wstg2[wb][:].rearrange("p (a f) -> p a f", a=8),
                    in_=aps[f"w1_{ex}"].rearrange("(a p) f -> p a f", p=128)[:, :, fc * 128:(fc + 1) * 128]),
                    writes=[f"l3_wstg{wb}"], tag="l3_ldw1")
                p.v(lambda e, wb=wb: e.tensor_copy(out=wmb2[wb][:], in_=wstg2[wb][:]),
                    reads=[f"l3_wstg{wb}"], writes=[f"l3_wmb{wb}"], tag="l3_cvw1")
                pbi = fc % 2
                for a in range(NCH):
                    p.t(lambda e, a=a, t0=t0, tw=tw, pbi=pbi, wb=wb: e.matmul(
                        out=pb[pbi][:, 0:tw],
                        lhsT=wmb2[wb][:, a * 128:(a + 1) * 128],
                        rhs=xn[:, a * T + t0:a * T + t0 + tw],
                        start=(a == 0), stop=(a == NCH - 1)),
                        reads=[f"l3_wmb{wb}", f"xn:{a}"], writes=[f"pb{pbi}"],
                        tag="l3_e1_mm")
                p.s(lambda e, fc=fc, tw=tw, pbi=pbi: e.activation(
                    out=hid[:, fc * 512:fc * 512 + tw], in_=pb[pbi][:, 0:tw],
                    func=AF.Gelu, bias=0.0, scale=1.0),
                    reads=[f"pb{pbi}"], writes=["l3_hid", "l3_kf:0"],
                    tag="l3_e1_g")
            p.t(lambda e, t0=t0, tw=tw: e.matmul(
                out=pb[2][:, 0:tw], lhsT=L.ones_row[0:1, 0:128],
                rhs=ge_row[0:1, t0:t0 + tw], start=True, stop=True),
                reads=["l3_ge", "ones_row"], writes=["pb2"], tag="l3_gbc")
            p.v(lambda e, tw=tw: e.tensor_copy(out=gw_sb[:, 0:tw],
                                               in_=pb[2][:, 0:tw]),
                reads=["pb2"], writes=["l3_gwsb", "ln_x2"], tag="l3_gbc_ev")
            for oh in range(2):
                for fc in range(32):
                    wb = fc % 2
                    p.dma("sync", lambda e, ex=ex, fc=fc, oh=oh, wb=wb: e.dma_start(
                        out=wstg2[wb][:, 0:512],
                        in_=aps[f"w2_{ex}"][fc * 128:(fc + 1) * 128,
                                            oh * 512:(oh + 1) * 512]),
                        writes=[f"l3_wstg{wb}"], tag="l3_ldw2")
                    p.v(lambda e, wb=wb: e.tensor_copy(out=wmb2[wb][:, 0:512],
                                                       in_=wstg2[wb][:, 0:512]),
                        reads=[f"l3_wstg{wb}"], writes=[f"l3_wmb{wb}"],
                        tag="l3_cvw2")
                    for o4 in range(4):
                        p.t(lambda e, fc=fc, o4=o4, tw=tw, wb=wb: e.matmul(
                            out=pb[2 + o4][:, 0:tw],
                            lhsT=wmb2[wb][:, o4 * 128:(o4 + 1) * 128],
                            rhs=hid[:, fc * 512:fc * 512 + tw],
                            start=(fc == 0), stop=(fc == 31)),
                            reads=[f"l3_wmb{wb}", "l3_hid"], writes=[f"pb{2 + o4}"],
                            tag="l3_e2_mm")
                for o4 in range(4):
                    oc = oh * 4 + o4
                    bo = o4 % 2
                    p.v(lambda e, o4=o4, tw=tw, bo=bo: e.tensor_tensor(
                        out=ob[bo][:, 0:tw], in0=pb[2 + o4][:, 0:tw],
                        in1=gw_sb[:, 0:tw], op=ALU.mult),
                        reads=[f"pb{2 + o4}", "l3_gwsb"], writes=["l3_ob0"],
                        tag="l3_e2_ev")
                    p.dma("gpsimd", lambda e, oc=oc, t0=t0, tw=tw, bo=bo: e.dma_start(
                        out=xout[oc * 128:(oc + 1) * 128, t0:t0 + tw],
                        in_=ob[bo][:, 0:tw], accum_op=ALU.add),
                        reads=["l3_ob0", "ln_xs0"], writes=[f"xout:{oc}"],
                        stream=f"l3_xo{oc % 4}", tag="l3_e2_acc")
    L.p.wait_all("gpsimd", [f"xout:{a}" for a in range(NCH)], tag="l3_final")


def prep_l3(inputs):
    c8 = lambda v, n: np.ascontiguousarray(np.asarray(v).reshape(n, 128).T)
    tr = lambda v: np.ascontiguousarray(np.asarray(v).T)
    d = {
        "wqt": tr(inputs["L3_attn_Wq"]), "wkt": tr(inputs["L3_attn_Wk"]),
        "wvt": tr(inputs["L3_attn_Wv"]),
        "wm1t": tr(inputs["L3_attn_Wm1"]), "bm1": c8(inputs["L3_attn_bm1"], 2),
        "wm2t": tr(inputs["L3_attn_Wm2"]),
        "bm2": np.ascontiguousarray(np.asarray(inputs["L3_attn_bm2"]).reshape(16, 1)),
        "wprojt": tr(inputs["L3_attn_Wproj"]),
        "wfct": tr(inputs["L3_attn_Wfc"]), "wcpt": tr(inputs["L3_attn_Wcp"]),
        "wr": np.ascontiguousarray(np.asarray(inputs["L3_ffn_Wr"])),
        "br": np.asarray(inputs["L3_ffn_br"]).reshape(1, 8),
        "idf": np.eye(128, dtype=np.float32),
    }
    tm = np.full((128, 128), -1e30, np.float32)
    tm[np.tril_indices(128)] = 0.0
    tm = np.full((128, 128), -1e30, np.float32)
    su, tu = np.meshgrid(np.arange(128), np.arange(128), indexing="ij")
    tm[su <= tu] = 0.0
    d["trimask"] = tm
    w1 = np.asarray(inputs["L3_ffn_W1"])
    w2 = np.asarray(inputs["L3_ffn_W2"])
    for ex in range(8):
        d[f"w1_{ex}"] = np.ascontiguousarray(w1[ex])
        d[f"w2_{ex}"] = np.ascontiguousarray(w2[ex])
    return d


L3_DRAM_SPECS = (
    [("wqt", [1024, 1024]), ("wkt", [1024, 1024]), ("wvt", [1024, 1024]),
     ("wm1t", [1024, 256]), ("bm1", [128, 2]), ("wm2t", [256, 16]),
     ("bm2", [16, 1]), ("wprojt", [1024, 1024]), ("wfct", [1024, 4096]),
     ("wcpt", [4096, 1024]), ("wr", [1024, 8]), ("br", [1, 8]),
     ("idf", [128, 128]), ("trimask", [128, 128])]
    + [(f"w1_{ex}", [1024, 4096]) for ex in range(8)]
    + [(f"w2_{ex}", [4096, 1024]) for ex in range(8)]
)



KERR = 2.7e-18 * 1000000.0
PI = 3.141592653589793


def build_l0(L, aps, xin, xout):
    nc, p, T = L.nc, L.p, L.T
    pb = L.pb
    tts = t_tiles(T)
    n_tc = T // 128
    p.dma("sync", lambda e: e.dma_start(out=xout, in_=xin),
          writes=[f"xout:{a}" for a in range(NCH)], stream="xout_init",
          tag="l0_xout_init")
    L.layernorm_streaming(xin)
    xn = L.xn
    stat2 = L.ln_stat2

    wstg2 = [L.sbuf(f"l0_wstg{i}", [128, 1024], F32) for i in range(2)]
    wmb2 = [L.sbuf(f"l0_wmb{i}", [128, 1024], BF16) for i in range(2)]
    wstg = wstg2[0]
    idf = L.sbuf("l0_idf", [128, 128], F32)
    p.dma("sync", lambda e: e.dma_start(out=idf[:], in_=aps["idf"]),
          writes=["l0_idf"], tag="l0_ldid")

    def load_slice(name, c0, cw, tag, b=0):
        p.dma("sync", lambda e, b=b: e.dma_start(
            out=wstg2[b][:, 0:NCH * cw].rearrange("p (a f) -> p a f", a=NCH),
            in_=aps[name].rearrange("(a p) f -> p a f", p=128)[:, :, c0:c0 + cw]),
            writes=[f"l0_wstg{b}"], tag=f"l0_lds_{tag}")
        p.v(lambda e, b=b: e.tensor_copy(out=wmb2[b][:, 0:NCH * cw],
                                         in_=wstg2[b][:, 0:NCH * cw]),
            reads=[f"l0_wstg{b}"], writes=[f"l0_wmb{b}"], tag=f"l0_cvs_{tag}")

    bqk = L.sbuf("l0_bqk", [128, 3 * NCH], F32)
    for i, nm in enumerate(["bq", "bk", "bo"]):
        p.dma("sync", lambda e, i=i, nm=nm: e.dma_start(
            out=bqk[:, i * NCH:(i + 1) * NCH], in_=aps[nm]),
            writes=["l0_bqk"], stream="l0_bqk", tag=f"l0_ld{nm}")
    vbrow32 = wstg[0:1, :]
    vbrow = L.sbuf("l0_vbrow", [1, 1024], BF16)
    p.dma("sync", lambda e: e.dma_start(out=vbrow32, in_=aps["vbrow"]),
          writes=["l0_wstg0"], tag="l0_ldvb")
    p.v(lambda e: e.tensor_copy(out=vbrow[:], in_=vbrow32),
        reads=["l0_wstg0"], writes=["l0_vbrow"], tag="l0_cvvb")

    gw32 = L.sbuf("l0_gw32", [128, NCH * 8], F32)
    p.dma("sync", lambda e: e.dma_start(
        out=gw32[:].rearrange("p (a r) -> p a r", a=NCH),
        in_=aps["gwt"].rearrange("(a p) r -> p a r", p=128)),
        writes=["l0_gw32"], tag="l0_ldgw")
    for a in range(NCH):
        p.t(lambda e, a=a: e.matmul(
            out=pb[4][0:1, 0:8], lhsT=L.ones_col, rhs=gw32[:, a * 8:(a + 1) * 8],
            start=(a == 0), stop=(a == NCH - 1)),
            reads=["l0_gw32", "cst"], writes=["pb4"], tag="l0_sr_mm")
    srow = L.sbuf("l0_srow", [1, 8], F32)
    srow_bc = L.sbuf("l0_srbc", [128, 8], F32)
    p.v(lambda e: e.tensor_scalar(out=srow[:], in0=pb[4][0:1, 0:8],
                                  scalar1=-1.0, scalar2=None, op0=ALU.mult),
        reads=["pb4"], writes=["l0_srow"], tag="l0_sr_ev")
    p.t(lambda e: e.matmul(out=pb[4][:, 0:8], lhsT=L.ones_row[0:1, 0:128],
                           rhs=srow[:], start=True, stop=True),
        reads=["l0_srow", "ones_row"], writes=["pb4"], tag="l0_sr_bc")
    p.v(lambda e: e.tensor_copy(out=srow_bc[:], in_=pb[4][:, 0:8]),
        reads=["pb4"], writes=["l0_srbc"], tag="l0_sr_bcev")
    rmcols = L.sbuf("l0_rmc", [128, 2 * n_tc], F32)
    for tc in range(n_tc):
        p.t(lambda e, tc=tc: e.transpose(
            out=pb[4][:, 0:1], in_=stat2[0:1, T + tc * 128:T + (tc + 1) * 128],
            identity=L.ones_row[0:1, 0:1]),
            reads=["ln_q", "ones_row"], writes=["pb4"], tag="l0_rst_tr")
        p.v(lambda e, tc=tc: e.tensor_copy(out=rmcols[:, tc:tc + 1],
                                           in_=pb[4][:, 0:1]),
            reads=["pb4"], writes=["l0_rmc"], tag="l0_rst_ev")
        p.t(lambda e, tc=tc: e.transpose(
            out=pb[4][:, 0:1], in_=stat2[0:1, tc * 128:(tc + 1) * 128],
            identity=L.ones_row[0:1, 0:1]),
            reads=["ln_m", "ones_row"], writes=["pb4"], tag="l0_msc_tr")
        p.v(lambda e, tc=tc: e.tensor_copy(out=rmcols[:, n_tc + tc:n_tc + tc + 1],
                                           in_=pb[4][:, 0:1]),
            reads=["pb4"], writes=["l0_rmc"], tag="l0_msc_ev")
    rsc = L.sbuf("l0_rsc", [128, 64], F32)
    wsel = L.sbuf("l0_wsel", [128, n_tc * 8], F32)
    g8 = L.sbuf("l0_g8", [8, 128], F32)
    g8_hbm = nc.dram_tensor("l0_g8_hbm", [8, n_tc * 128], F32, kind="Internal").ap()
    xs_r = L.ln_xs
    nld2 = 0
    for ti, (t0, tw) in enumerate(tts):
        nsub = tw // 128
        for a in range(NCH):
            b = nld2 % 2
            nld2 += 1
            p.dma("sync", lambda e, a=a, t0=t0, tw=tw, b=b: e.dma_start(
                out=xs_r[b][:, 0:tw],
                in_=xin[a * 128:(a + 1) * 128, t0:t0 + tw]),
                writes=[f"ln_xs{b}"], tag="l0_rt_ldx")
            for su in range(nsub):
                sb_ = [0, 1, 4, 5][su]
                p.t(lambda e, a=a, su=su, b=b, sb_=sb_: e.matmul(
                    out=pb[sb_][:, 0:8],
                    lhsT=xs_r[b][:, su * 128:(su + 1) * 128],
                    rhs=gw32[:, a * 8:(a + 1) * 8],
                    start=(a == 0), stop=(a == NCH - 1)),
                    reads=[f"ln_xs{b}", "l0_gw32"], writes=[f"pb{sb_}"],
                    tag="l0_rt_mm")
        for su in range(nsub):
            tc = (t0 // 128) + su
            sb_ = [0, 1, 4, 5][su]
            p.v(lambda e, su=su, tc=tc, sb_=sb_: e.tensor_scalar(
                out=rsc[:, 0:8], in0=pb[sb_][:, 0:8],
                scalar1=rmcols[:, tc:tc + 1], scalar2=None, op0=ALU.mult),
                reads=[f"pb{sb_}", "l0_rmc"], writes=["l0_rsc"], tag="l0_rt_a")
            p.v(lambda e, tc=tc: e.scalar_tensor_tensor(
                out=rsc[:, 0:8], in0=srow_bc[:],
                scalar=rmcols[:, n_tc + tc:n_tc + tc + 1],
                in1=rsc[:, 0:8], op0=ALU.mult, op1=ALU.add),
                reads=["l0_srbc", "l0_rmc", "l0_rsc"], writes=["l0_rsc"],
                tag="l0_rt_ms")
            p.v(lambda e: e.reduce_max(out=rsc[:, 16:17], in_=rsc[:, 0:8],
                                       axis=AX.X, negate=True),
                reads=["l0_rsc"], writes=["l0_rsc"], tag="l0_rt_mx")
            p.s(lambda e: e.activation(out=rsc[:, 8:16], in_=rsc[:, 0:8],
                                       func=AF.Exp, bias=rsc[:, 16:17],
                                       scale=1.0, accum_out=rsc[:, 17:18]),
                reads=["l0_rsc"], writes=["l0_rsc"], tag="l0_rt_ex")
            p.v(lambda e: e.reciprocal(out=rsc[:, 18:19], in_=rsc[:, 17:18]),
                reads=["l0_rsc"], writes=["l0_rsc"], tag="l0_rt_rz")
            p.v(lambda e: e.tensor_scalar(
                out=rsc[:, 8:16], in0=rsc[:, 8:16], scalar1=rsc[:, 18:19],
                scalar2=None, op0=ALU.mult),
                reads=["l0_rsc"], writes=["l0_rsc"], tag="l0_rt_p")
            p.v(lambda e: e.reduce_max(out=rsc[:, 16:17], in_=rsc[:, 8:16],
                                       axis=AX.X),
                reads=["l0_rsc"], writes=["l0_rsc"], tag="l0_t2_m1")
            p.v(lambda e: e.tensor_tensor_scan(
                out=rsc[:, 24:32], data0=rsc[:, 8:16], data1=rsc[:, 8:16],
                initial=-1e30, op0=ALU.max, op1=ALU.bypass),
                reads=["l0_rsc"], writes=["l0_rsc"], tag="l0_t2_pfx")
            p.v(lambda e: e.tensor_scalar(
                out=rsc[:, 33:40], in0=rsc[:, 24:31], scalar1=rsc[:, 16:17],
                scalar2=None, op0=ALU.is_lt),
                reads=["l0_rsc"], writes=["l0_rsc"], tag="l0_t2_ex")
            p.v(lambda e: e.memset(rsc[:, 32:33], 1.0),
                writes=["l0_rsc"], tag="l0_t2_e0")
            p.v(lambda e: e.tensor_scalar(
                out=rsc[:, 40:48], in0=rsc[:, 8:16], scalar1=rsc[:, 16:17],
                scalar2=None, op0=ALU.is_ge),
                reads=["l0_rsc"], writes=["l0_rsc"], tag="l0_t2_ge")
            p.v(lambda e: e.tensor_tensor(
                out=rsc[:, 40:48], in0=rsc[:, 40:48], in1=rsc[:, 32:40],
                op=ALU.mult),
                reads=["l0_rsc"], writes=["l0_rsc"], tag="l0_t2_m1f")
            p.v(lambda e: e.scalar_tensor_tensor(
                out=rsc[:, 48:56], in0=rsc[:, 40:48], scalar=-2.0,
                in1=rsc[:, 8:16], op0=ALU.mult, op1=ALU.add),
                reads=["l0_rsc"], writes=["l0_rsc"], tag="l0_t2_p2")
            p.v(lambda e: e.reduce_max(out=rsc[:, 16:17], in_=rsc[:, 48:56],
                                       axis=AX.X),
                reads=["l0_rsc"], writes=["l0_rsc"], tag="l0_t2_m2")
            p.v(lambda e: e.tensor_tensor_scan(
                out=rsc[:, 24:32], data0=rsc[:, 48:56], data1=rsc[:, 48:56],
                initial=-1e30, op0=ALU.max, op1=ALU.bypass),
                reads=["l0_rsc"], writes=["l0_rsc"], tag="l0_t2_pfx2")
            p.v(lambda e: e.tensor_scalar(
                out=rsc[:, 33:40], in0=rsc[:, 24:31], scalar1=rsc[:, 16:17],
                scalar2=None, op0=ALU.is_lt),
                reads=["l0_rsc"], writes=["l0_rsc"], tag="l0_t2_ex2")
            p.v(lambda e: e.memset(rsc[:, 32:33], 1.0),
                writes=["l0_rsc"], tag="l0_t2_e02")
            p.v(lambda e: e.tensor_scalar(
                out=rsc[:, 56:64], in0=rsc[:, 48:56], scalar1=rsc[:, 16:17],
                scalar2=None, op0=ALU.is_ge),
                reads=["l0_rsc"], writes=["l0_rsc"], tag="l0_t2_ge2")
            p.v(lambda e: e.tensor_tensor(
                out=rsc[:, 56:64], in0=rsc[:, 56:64], in1=rsc[:, 32:40],
                op=ALU.mult),
                reads=["l0_rsc"], writes=["l0_rsc"], tag="l0_t2_m2f")
            p.v(lambda e: e.tensor_tensor(
                out=rsc[:, 40:48], in0=rsc[:, 40:48], in1=rsc[:, 56:64],
                op=ALU.add),
                reads=["l0_rsc"], writes=["l0_rsc"], tag="l0_t2_mm")
            p.v(lambda e, tc=tc: e.tensor_tensor(
                out=wsel[:, tc * 8:(tc + 1) * 8], in0=rsc[:, 8:16],
                in1=rsc[:, 40:48], op=ALU.mult),
                reads=["l0_rsc"], writes=["l0_wsel"], tag="l0_t2_w")
            p.t(lambda e, tc=tc: e.transpose(
                out=pb[2][0:8, 0:128], in_=wsel[:, tc * 8:(tc + 1) * 8],
                identity=idf[:]),
                reads=["l0_wsel", "l0_idf"], writes=["pb2"], tag="l0_gtr")
            p.v(lambda e: e.tensor_copy(out=g8[:], in_=pb[2][0:8, 0:128]),
                reads=["pb2"], writes=["l0_g8"], tag="l0_gtr_ev")
            p.dma("sync", lambda e, tc=tc: e.dma_start(
                out=g8_hbm[:, tc * 128:(tc + 1) * 128], in_=g8[:]),
                reads=["l0_g8"], writes=["l0_g8hbm"], stream="l0_g8hbm",
                tag="l0_g8_st")

    vtok = L.sbuf("l0_vt", [128, n_tc * 1024], BF16)
    yat = L.sbuf("l0_ya", [128, NCH * T], BF16)
    ktc = L.sbuf("l0_ktc", [128, T], BF16)
    for eo in range(NCH):
        wb = eo % 2
        load_slice("wv", eo * 128, 128, f"wv{eo}", wb)
        for tc in range(n_tc):
            pbi = tc % 2
            for a in range(NCH):
                p.t(lambda e, a=a, tc=tc, pbi=pbi, wb=wb: e.matmul(
                    out=pb[pbi][:, 0:128],
                    lhsT=xn[:, a * T + tc * 128:a * T + (tc + 1) * 128],
                    rhs=wmb2[wb][:, a * 128:(a + 1) * 128],
                    start=(a == 0), stop=False),
                    reads=[f"l0_wmb{wb}", f"xn:{a}"], writes=[f"pb{pbi}"],
                    tag="l0_v_mm")
            p.t(lambda e, eo=eo, pbi=pbi: e.matmul(
                out=pb[pbi][:, 0:128], lhsT=L.ones_row_bf[0:1, 0:128],
                rhs=vbrow[0:1, eo * 128:(eo + 1) * 128],
                start=False, stop=True),
                reads=["l0_vbrow", "ones_row_bf"], writes=[f"pb{pbi}"],
                tag="l0_v_b")
            p.v(lambda e, tc=tc, eo=eo, pbi=pbi: e.tensor_copy(
                out=vtok[:, tc * 1024 + eo * 128:tc * 1024 + (eo + 1) * 128],
                in_=pb[pbi][:, 0:128]),
                reads=[f"pb{pbi}"], writes=[f"l0_vt:{tc}"], tag="l0_v_ev")

    pmr = L.sbuf("l0_pmr", [1, 3 * 256], F32)
    iwr = L.sbuf("l0_iwr", [1, 16], F32)
    p.dma("sync", lambda e: e.dma_start(out=pmr[0:1, 0:256], in_=aps["pmrow"]),
          writes=["l0_pmr"], tag="l0_ldpm")
    p.dma("sync", lambda e: e.dma_start(out=iwr[:], in_=aps["iwrow"]),
          writes=["l0_iwr"], tag="l0_ldiw")
    L.cap(PI / 2)
    p.v(lambda e: e.memset(pmr[0:1, 512:768], 0.0), writes=["l0_pmr"],
        tag="l0_pm_k0")
    for i in range(4):
        thr = (2 * i + 1) * PI
        p.v(lambda e, thr=thr: e.tensor_scalar(
            out=pmr[0:1, 256:512], in0=pmr[0:1, 0:256], scalar1=thr,
            scalar2=None, op0=ALU.is_ge),
            reads=["l0_pmr"], writes=["l0_pmr"], tag="l0_pm_ge")
        p.v(lambda e: e.tensor_tensor(
            out=pmr[0:1, 512:768], in0=pmr[0:1, 512:768],
            in1=pmr[0:1, 256:512], op=ALU.add),
            reads=["l0_pmr"], writes=["l0_pmr"], tag="l0_pm_ka")
        p.v(lambda e, thr=thr: e.tensor_scalar(
            out=pmr[0:1, 256:512], in0=pmr[0:1, 0:256], scalar1=-thr,
            scalar2=None, op0=ALU.is_le),
            reads=["l0_pmr"], writes=["l0_pmr"], tag="l0_pm_le")
        p.v(lambda e: e.tensor_tensor(
            out=pmr[0:1, 512:768], in0=pmr[0:1, 512:768],
            in1=pmr[0:1, 256:512], op=ALU.subtract),
            reads=["l0_pmr"], writes=["l0_pmr"], tag="l0_pm_ks")
    p.v(lambda e: e.scalar_tensor_tensor(
        out=pmr[0:1, 256:512], in0=pmr[0:1, 512:768], scalar=-2 * PI,
        in1=pmr[0:1, 0:256], op0=ALU.mult, op1=ALU.add),
        reads=["l0_pmr"], writes=["l0_pmr"], tag="l0_pm_rr")
    p.s(lambda e: e.activation(out=pmr[0:1, 256:512], in_=pmr[0:1, 256:512],
                               func=AF.Abs, bias=0.0, scale=1.0),
        reads=["l0_pmr"], writes=["l0_pmr"], tag="l0_pm_abs")
    p.s(lambda e: e.activation(out=pmr[0:1, 256:512], in_=pmr[0:1, 256:512],
                               func=AF.Sin, bias=PI / 2, scale=-1.0),
        reads=["l0_pmr", "fconst"], writes=["l0_pmr"], tag="l0_pm_cos")
    p.s(lambda e: e.activation(out=pmr[0:1, 256:512], in_=pmr[0:1, 256:512],
                               func=AF.Square, bias=0.0, scale=1.0),
        reads=["l0_pmr"], writes=["l0_pmr"], tag="l0_pm_sq")
    p.v(lambda e: e.tensor_scalar(
        out=pmr[0:1, 256:512], in0=pmr[0:1, 256:512], scalar1=0.125,
        scalar2=None, op0=ALU.mult),
        reads=["l0_pmr", "fconst"], writes=["l0_pmr"], tag="l0_pm_c8")
    p.s(lambda e: e.activation(out=pmr[0:1, 512:768], in_=pmr[0:1, 256:512],
                               func=AF.Square, bias=0.0, scale=1.0),
        reads=["l0_pmr"], writes=["l0_pmr"], tag="l0_pm_k0")
    p.v(lambda e: e.tensor_scalar(
        out=pmr[0:1, 512:768], in0=pmr[0:1, 512:768], scalar1=-KERR,
        scalar2=None, op0=ALU.mult),
        reads=["l0_pmr"], writes=["l0_pmr"], tag="l0_pm_k1")
    p.v(lambda e: e.tensor_scalar(
        out=iwr[:], in0=iwr[:], scalar1=1.0 / 16.0, scalar2=None, op0=ALU.mult),
        reads=["l0_iwr", "fconst"], writes=["l0_iwr"], tag="l0_iw16")
    c8bc = L.sbuf("l0_c8bc", [128, 256], F32)
    nkbc = L.sbuf("l0_nkbc", [128, 256], F32)
    iwbc = L.sbuf("l0_iwbc", [128, 16], F32)
    for nm, srcsl, dst, dk in (("c8", (256, 512), c8bc, "l0_c8bc"),
                               ("nk", (512, 768), nkbc, "l0_nkbc")):
        p.t(lambda e, srcsl=srcsl: e.matmul(
            out=pb[2][:, 0:256], lhsT=L.ones_row[0:1, 0:128],
            rhs=pmr[0:1, srcsl[0]:srcsl[1]], start=True, stop=True),
            reads=["l0_pmr", "ones_row"], writes=["pb2"], tag=f"l0_bc_{nm}")
        p.v(lambda e, dst=dst, dk=dk: e.tensor_copy(out=dst[:], in_=pb[2][:, 0:256]),
            reads=["pb2"], writes=[dk], tag=f"l0_bc_{nm}ev")
    p.t(lambda e: e.matmul(
        out=pb[2][:, 0:16], lhsT=L.ones_row[0:1, 0:128], rhs=iwr[:],
        start=True, stop=True),
        reads=["l0_iwr", "ones_row"], writes=["pb2"], tag="l0_bc_iw")
    p.v(lambda e: e.tensor_copy(out=iwbc[:], in_=pb[2][:, 0:16]),
        reads=["pb2"], writes=["l0_iwbc"], tag="l0_bc_iwev")

    qt = L.sbuf("l0_qt", [128, T], BF16)
    Sx2t = L.sbuf("l0_Sx", [128, 2 * T], BF16)
    t22t = L.sbuf("l0_t2", [128, 2 * T], BF16)
    ck4 = L.sbuf("l0_ck4", [128, 4 * T], BF16)
    sk4 = L.sbuf("l0_sk4", [128, 4 * T], BF16)
    atb = ck4[:, 0:T]
    Ab = ck4[:, T:2 * T]
    db2 = L.sbuf("l0_db", [128, 2 * 128], BF16)
    idb = L.sbuf("l0_idb", [128, 128], BF16)
    mc = L.sbuf("l0_mc", [128, 48], F32)
    p.v(lambda e: e.tensor_copy(out=idb[:], in_=idf[:]), reads=["l0_idf"],
        writes=["l0_idb"], tag="l0_cvid")
    pihalf = L.sbuf("l0_pih", [128, 1], F32)
    p.v(lambda e: e.memset(pihalf[:], PI / 2), writes=["l0_pih"], tag="l0_pih")
    ge_row = stat2[0:1, 0:T]
    for ec in range(NCH):
        wb = ec % 2
        load_slice("wq", ec * 128, 128, f"wq{ec}", wb)
        for ti, (t0, tw) in enumerate(tts):
            pbi = ti % 2
            for a in range(NCH):
                p.t(lambda e, a=a, t0=t0, tw=tw, pbi=pbi, wb=wb: e.matmul(
                    out=pb[pbi][:, 0:tw],
                    lhsT=wmb2[wb][:, a * 128:(a + 1) * 128],
                    rhs=xn[:, a * T + t0:a * T + t0 + tw],
                    start=(a == 0), stop=(a == NCH - 1)),
                    reads=[f"l0_wmb{wb}", f"xn:{a}"], writes=[f"pb{pbi}"],
                    tag="l0_q_mm")
            p.v(lambda e, ec=ec, t0=t0, tw=tw, pbi=pbi: e.tensor_scalar(
                out=qt[:, t0:t0 + tw], in0=pb[pbi][:, 0:tw],
                scalar1=bqk[:, ec:ec + 1], scalar2=None, op0=ALU.add),
                reads=[f"pb{pbi}", "l0_bqk"], writes=["l0_qt"], tag="l0_q_ev")
        wb2 = (ec + 1) % 2
        load_slice("wk", ec * 128, 128, f"wk{ec}", wb2)
        for ti, (t0, tw) in enumerate(tts):
            pbi = 2 + ti % 2
            for a in range(NCH):
                p.t(lambda e, a=a, t0=t0, tw=tw, pbi=pbi, wb2=wb2: e.matmul(
                    out=pb[pbi][:, 0:tw],
                    lhsT=wmb2[wb2][:, a * 128:(a + 1) * 128],
                    rhs=xn[:, a * T + t0:a * T + t0 + tw],
                    start=(a == 0), stop=(a == NCH - 1)),
                    reads=[f"l0_wmb{wb2}", f"xn:{a}"], writes=[f"pb{pbi}"],
                    tag="l0_kc_mm")
            p.v(lambda e, ec=ec, t0=t0, tw=tw, pbi=pbi: e.tensor_scalar(
                out=ktc[:, t0:t0 + tw], in0=pb[pbi][:, 0:tw],
                scalar1=bqk[:, NCH + ec:NCH + ec + 1], scalar2=None, op0=ALU.add),
                reads=[f"pb{pbi}", "l0_bqk"], writes=["l0_ktc"], tag="l0_kc_ev")
        for hh in range(2):
            h = 2 * ec + hh
            r0 = hh * 64
            for qc in range(n_tc):
                pr = (hh * n_tc + qc) % 2
                Sx = Sx2t[:, pr * T:(pr + 1) * T]
                t2 = t22t[:, pr * T:(pr + 1) * T]
                kS, kT2 = f"l0_Sx:{pr}", f"l0_t2:{pr}"
                mb0 = 24 * pr
                kM0 = f"l0_mc0:{pr}"
                for kt4, (k0, kw) in enumerate(tts):
                    pbs = 4 + kt4 % 2
                    p.t(lambda e, pbs=pbs, k0=k0, kw=kw, qc=qc, r0=r0: e.matmul(
                        out=pb[pbs][:, 0:kw],
                        lhsT=qt[r0:r0 + 64, qc * 128:(qc + 1) * 128],
                        rhs=ktc[r0:r0 + 64, k0:k0 + kw],
                        start=True, stop=True),
                        reads=["l0_qt", "l0_ktc"], writes=[f"pb{pbs}"],
                        tag="l0_sc_mm")
                    p.v(lambda e, pbs=pbs, k0=k0, kw=kw, Sx=Sx: e.tensor_copy(
                        out=Sx[:, k0:k0 + kw], in_=pb[pbs][:, 0:kw]),
                        reads=[f"pb{pbs}"], writes=[kS], tag="l0_sc_ev")
                p.v(lambda e, Sx=Sx, mb0=mb0: e.reduce_max(
                    out=mc[:, mb0:mb0 + 1], in_=Sx[:], axis=AX.X, negate=True),
                    reads=[kS], writes=[kM0], tag="l0_rmax")
                p.g(lambda e, Sx=Sx, t2=t2: e.tensor_tensor(
                    out=t2[:], in0=Sx[:], in1=Sx[:], op=ALU.mult),
                    reads=[kS], writes=[kT2], tag="l0_t2")
                for g in range(4):
                    for i in range(4):
                        wl = 4 * g + i
                        j = wl * 16 + h
                        p.s(lambda e, j=j, i=i, t2=t2: e.activation(
                            out=ck4[:, i * T:(i + 1) * T], in_=t2[:],
                            func=AF.Sin, bias=pihalf[:],
                            scale=nkbc[:, j:j + 1]),
                            reads=[kT2, "l0_nkbc", "l0_pih"],
                            writes=[f"l0_ck4:{i}"], tag="l0_ck")
                    for i in range(4):
                        skv = sk4[:, i * T:(i + 1) * T]
                        if i % 2 == 0:
                            p.g(lambda e, i=i, skv=skv, Sx=Sx: e.tensor_tensor(
                                out=skv, in0=Sx[:], in1=ck4[:, i * T:(i + 1) * T],
                                op=ALU.mult),
                                reads=[kS, f"l0_ck4:{i}"],
                                writes=[f"l0_sk:{i}"], tag="l0_sk")
                        else:
                            p.v(lambda e, i=i, skv=skv, Sx=Sx: e.tensor_tensor(
                                out=skv, in0=Sx[:], in1=ck4[:, i * T:(i + 1) * T],
                                op=ALU.mult),
                                reads=[kS, f"l0_ck4:{i}"],
                                writes=[f"l0_sk:{i}"], tag="l0_sk")
                    for i in range(4):
                        wl = 4 * g + i
                        j = wl * 16 + h
                        cb = mb0 + 4 + 4 * i
                        kMi = f"l0_mc:{pr}:{i}"
                        skv = sk4[:, i * T:(i + 1) * T]
                        p.v(lambda e, j=j, cb=cb, mb0=mb0: e.tensor_scalar(
                            out=mc[:, cb:cb + 1], in0=mc[:, mb0:mb0 + 1],
                            scalar1=c8bc[:, j:j + 1], scalar2=None, op0=ALU.mult),
                            reads=[kM0, "l0_c8bc"], writes=[kMi],
                            tag="l0_nmj")
                        p.s(lambda e, j=j, cb=cb, skv=skv, i=i: e.activation(
                            out=skv, in_=skv, func=AF.Exp,
                            bias=mc[:, cb:cb + 1], scale=c8bc[:, j:j + 1],
                            accum_out=mc[:, cb + 1:cb + 2]),
                            reads=[f"l0_sk:{i}", kMi, "l0_c8bc"],
                            writes=[f"l0_sk:{i}", kMi], tag="l0_exp")
                        p.v(lambda e, cb=cb: e.reciprocal(
                            out=mc[:, cb + 2:cb + 3], in_=mc[:, cb + 1:cb + 2]),
                            reads=[kMi], writes=[kMi], tag="l0_rz")
                        p.v(lambda e, wl=wl, cb=cb: e.tensor_scalar(
                            out=mc[:, cb + 3:cb + 4], in0=mc[:, cb + 2:cb + 3],
                            scalar1=iwbc[:, wl:wl + 1], scalar2=None,
                            op0=ALU.mult),
                            reads=[kMi, "l0_iwbc"], writes=[kMi], tag="l0_al")
                        dbv = db2[:, (wl % 2) * 128:(wl % 2) * 128 + 128]
                        p.v(lambda e, cb=cb, dbv=dbv: e.tensor_scalar(
                            out=dbv, in0=idb[:], scalar1=mc[:, cb + 3:cb + 4],
                            scalar2=None, op0=ALU.mult),
                            reads=["l0_idb", kMi], writes=[f"l0_db:{wl % 2}"],
                            tag="l0_db")
                        for kt4, (k0, kw) in enumerate(tts):
                            p.t(lambda e, kt4=kt4, k0=k0, kw=kw, skv=skv,
                                dbv=dbv, first=(wl == 0), last=(wl == 15): e.matmul(
                                out=pb[kt4][:, 0:kw], lhsT=dbv,
                                rhs=skv[:, k0:k0 + kw],
                                start=first, stop=last),
                                reads=[f"l0_db:{wl % 2}", f"l0_sk:{i}"],
                                writes=[f"pb{kt4}"], tag="l0_aacc")
                for kt4, (k0, kw) in enumerate(tts):
                    p.v(lambda e, kt4=kt4, k0=k0, kw=kw: e.tensor_copy(
                        out=Ab[:, k0:k0 + kw], in_=pb[kt4][:, 0:kw]),
                        reads=[f"pb{kt4}"], writes=["l0_Ab", "l0_ck4:1"],
                        tag="l0_abev")
                for kc in range(n_tc):
                    bk = kc % 2
                    sl = (kc // 2) % 2
                    p.t(lambda e, kc=kc, bk=bk, sl=sl: e.transpose(
                        out=L.pbt2[bk][:, sl * 128:(sl + 1) * 128],
                        in_=Ab[:, kc * 128:(kc + 1) * 128], identity=idb[:]),
                        reads=["l0_Ab", "l0_idb"], writes=[f"pbt{bk}"],
                        tag="l0_atr")
                    p.v(lambda e, kc=kc, bk=bk, sl=sl: e.tensor_copy(
                        out=atb[:, kc * 128:(kc + 1) * 128],
                        in_=L.pbt2[bk][:, sl * 128:(sl + 1) * 128]),
                        reads=[f"pbt{bk}"], writes=["l0_atb", "l0_ck4:0"],
                        tag="l0_atr_ev")
                for kc in range(n_tc):
                    p.t(lambda e, kc=kc, h=h, first=(kc == 0),
                        last=(kc == n_tc - 1): e.matmul(
                        out=pb[0][0:64, 0:128],
                        lhsT=vtok[:, kc * 1024 + h * 64:kc * 1024 + h * 64 + 64],
                        rhs=atb[:, kc * 128:(kc + 1) * 128],
                        start=first, stop=last),
                        reads=[f"l0_vt:{kc}", "l0_atb", "l0_Ab"], writes=["pb0"],
                        tag="l0_av")
                p.v(lambda e, r0=r0, ec=ec, qc=qc: e.tensor_copy(
                    out=yat[r0:r0 + 64, ec * T + qc * 128:ec * T + (qc + 1) * 128],
                    in_=pb[0][0:64, 0:128]),
                    reads=["pb0"], writes=[f"l0_ya:{ec}"], tag="l0_yev")

    _ob0 = L.ln_xs[0]
    for oc in range(NCH):
        wb = oc % 2
        load_slice("wo", oc * 128, 128, f"wo{oc}", wb)
        for ti, (t0, tw) in enumerate(tts):
            pbi = ti % 2
            for a in range(NCH):
                p.t(lambda e, a=a, t0=t0, tw=tw, pbi=pbi, wb=wb: e.matmul(
                    out=pb[pbi][:, 0:tw],
                    lhsT=wmb2[wb][:, a * 128:(a + 1) * 128],
                    rhs=yat[:, a * T + t0:a * T + t0 + tw],
                    start=(a == 0), stop=(a == NCH - 1)),
                    reads=[f"l0_wmb{wb}", f"l0_ya:{a}"], writes=[f"pb{pbi}"],
                    tag="l0_wo_mm")
            p.v(lambda e, oc=oc, tw=tw, pbi=pbi: e.tensor_scalar(
                out=_ob0[:, 0:tw], in0=pb[pbi][:, 0:tw],
                scalar1=bqk[:, 2 * NCH + oc:2 * NCH + oc + 1], scalar2=None,
                op0=ALU.add),
                reads=[f"pb{pbi}", "l0_bqk"], writes=["l0_ob0", "ln_xs0"], tag="l0_wo_ev")
            p.dma("gpsimd", lambda e, oc=oc, t0=t0, tw=tw: e.dma_start(
                out=xout[oc * 128:(oc + 1) * 128, t0:t0 + tw],
                in_=_ob0[:, 0:tw], accum_op=ALU.add),
                reads=["l0_ob0", "ln_xs0"], writes=[f"xout:{oc}"],
                stream=f"l0_xo{oc % 4}", tag="l0_wo_acc")

    b1c = L.sbuf("l0_b1c", [128, 8 * 32], F32)
    b2c = L.sbuf("l0_b2c", [128, 8 * 8], F32)
    p.dma("sync", lambda e: e.dma_start(out=b1c[:], in_=aps["b1c"]),
          writes=["l0_b1c"], tag="l0_ldb1")
    p.dma("sync", lambda e: e.dma_start(out=b2c[:], in_=aps["b2c"]),
          writes=["l0_b2c"], tag="l0_ldb2")
    hid = yat if NCH * T >= 32 * 512 else L.sbuf("l0_hidS", [128, 32 * 512], BF16)
    gw_sb = L.xn_scratch_a[:, 0:512]
    for ex in range(8):
        p.dma("sync", lambda e, ex=ex: e.dma_start(
            out=ge_row[:], in_=g8_hbm[ex:ex + 1, :]),
            reads=["l0_g8hbm"], writes=["l0_ge", "l0_Sx"], stream="l0_ge",
            tag="l0_gedma")
        for ti, (t0, tw) in enumerate(tts):
            for fc in range(32):
                wb = fc % 2
                p.dma("sync", lambda e, ex=ex, fc=fc, wb=wb: e.dma_start(
                    out=wstg2[wb][:].rearrange("p (a f) -> p a f", a=8),
                    in_=aps[f"w1_{ex}"].rearrange("(a p) f -> p a f", p=128)[:, :, fc * 128:(fc + 1) * 128]),
                    writes=[f"l0_wstg{wb}"], tag="l0_ldw1")
                p.v(lambda e, wb=wb: e.tensor_copy(out=wmb2[wb][:, 0:1024],
                                                   in_=wstg2[wb][:]),
                    reads=[f"l0_wstg{wb}"], writes=[f"l0_wmb{wb}"], tag="l0_cvw1")
                pbi = fc % 2
                for a in range(NCH):
                    p.t(lambda e, a=a, t0=t0, tw=tw, pbi=pbi, wb=wb: e.matmul(
                        out=pb[pbi][:, 0:tw],
                        lhsT=wmb2[wb][:, a * 128:(a + 1) * 128],
                        rhs=xn[:, a * T + t0:a * T + t0 + tw],
                        start=(a == 0), stop=(a == NCH - 1)),
                        reads=[f"l0_wmb{wb}", f"xn:{a}"], writes=[f"pb{pbi}"],
                        tag="l0_e1_mm")
                p.s(lambda e, ex=ex, fc=fc, tw=tw, pbi=pbi: e.activation(
                    out=hid[:, fc * 512:fc * 512 + tw], in_=pb[pbi][:, 0:tw],
                    func=AF.Relu, bias=b1c[:, ex * 32 + fc:ex * 32 + fc + 1],
                    scale=1.0),
                    reads=[f"pb{pbi}", "l0_b1c"],
                    writes=["l0_hid"] + [f"l0_ya:{aa}" for aa in range(NCH)],
                    tag="l0_e1_r")
            p.t(lambda e, t0=t0, tw=tw: e.matmul(
                out=pb[2][:, 0:tw], lhsT=L.ones_row[0:1, 0:128],
                rhs=ge_row[0:1, t0:t0 + tw], start=True, stop=True),
                reads=["l0_ge", "ones_row"], writes=["pb2"], tag="l0_gbc")
            p.v(lambda e, tw=tw: e.tensor_copy(out=gw_sb[:, 0:tw],
                                               in_=pb[2][:, 0:tw]),
                reads=["pb2"], writes=["l0_gwsb", "ln_x2"], tag="l0_gbc_ev")
            for oh in range(2):
                for fc in range(32):
                    wb = fc % 2
                    p.dma("sync", lambda e, ex=ex, fc=fc, oh=oh, wb=wb: e.dma_start(
                        out=wstg2[wb][:, 0:512],
                        in_=aps[f"w2_{ex}"][fc * 128:(fc + 1) * 128,
                                            oh * 512:(oh + 1) * 512]),
                        writes=[f"l0_wstg{wb}"], tag="l0_ldw2")
                    p.v(lambda e, wb=wb: e.tensor_copy(out=wmb2[wb][:, 0:512],
                                                       in_=wstg2[wb][:, 0:512]),
                        reads=[f"l0_wstg{wb}"], writes=[f"l0_wmb{wb}"],
                        tag="l0_cvw2")
                    for o4 in range(4):
                        p.t(lambda e, fc=fc, o4=o4, tw=tw, wb=wb: e.matmul(
                            out=pb[2 + o4][:, 0:tw],
                            lhsT=wmb2[wb][:, o4 * 128:(o4 + 1) * 128],
                            rhs=hid[:, fc * 512:fc * 512 + tw],
                            start=(fc == 0), stop=(fc == 31)),
                            reads=[f"l0_wmb{wb}", "l0_hid"], writes=[f"pb{2 + o4}"],
                            tag="l0_e2_mm")
                for o4 in range(4):
                    oc = oh * 4 + o4
                    p.v(lambda e, ex=ex, oc=oc, o4=o4, tw=tw: e.scalar_tensor_tensor(
                        out=_ob0[:, 0:tw], in0=pb[2 + o4][:, 0:tw],
                        scalar=b2c[:, ex * 8 + oc:ex * 8 + oc + 1],
                        in1=gw_sb[:, 0:tw], op0=ALU.add, op1=ALU.mult),
                        reads=[f"pb{2 + o4}", "l0_b2c", "l0_gwsb"],
                        writes=["l0_ob0", "ln_xs0"], tag="l0_e2_ev")
                    p.dma("gpsimd", lambda e, oc=oc, t0=t0, tw=tw: e.dma_start(
                        out=xout[oc * 128:(oc + 1) * 128, t0:t0 + tw],
                        in_=_ob0[:, 0:tw], accum_op=ALU.add),
                        reads=["l0_ob0", "ln_xs0"], writes=[f"xout:{oc}"],
                        stream=f"l0_xo{oc % 4}", tag="l0_e2_acc")
    L.p.wait_all("gpsimd", [f"xout:{a}" for a in range(NCH)], tag="l0_final")


def prep_l0(inputs):
    c8 = lambda v, n=8: np.ascontiguousarray(np.asarray(v).reshape(n, 128).T)
    d = {
        "wq": np.ascontiguousarray(np.asarray(inputs["L0_attn_Wq"])),
        "wk": np.ascontiguousarray(np.asarray(inputs["L0_attn_Wk"])),
        "wv": np.ascontiguousarray(np.asarray(inputs["L0_attn_Wv"])),
        "wo": np.ascontiguousarray(np.asarray(inputs["L0_attn_Wo"])),
        "bq": c8(inputs["L0_attn_bq"]), "bk": c8(inputs["L0_attn_bk"]),
        "bo": c8(inputs["L0_attn_bo"]),
        "vbrow": np.asarray(inputs["L0_attn_bv"]).reshape(1, 1024),
        "pmrow": np.ascontiguousarray(
            np.asarray(inputs["L0_attn_phase_modulators"]).reshape(1, 256)),
        "iwrow": np.asarray(inputs["L0_attn_interference_weights"]).reshape(1, 16),
        "gwt": np.ascontiguousarray(np.asarray(inputs["L0_ffn_gate_w"]).T),
        "b1c": np.ascontiguousarray(
            np.asarray(inputs["L0_ffn_b1"]).reshape(8, 32, 128)
            .transpose(2, 0, 1).reshape(128, 256)),
        "b2c": np.ascontiguousarray(
            np.asarray(inputs["L0_ffn_b2"]).reshape(8, 8, 128)
            .transpose(2, 0, 1).reshape(128, 64)),
        "idf": np.eye(128, dtype=np.float32),
    }
    w1 = np.asarray(inputs["L0_ffn_w1"])
    w2 = np.asarray(inputs["L0_ffn_w2"])
    for ex in range(8):
        d[f"w1_{ex}"] = np.ascontiguousarray(w1[ex])
        d[f"w2_{ex}"] = np.ascontiguousarray(w2[ex])
    return d


L0_DRAM_SPECS = (
    [("wq", [1024, 1024]), ("wk", [1024, 1024]), ("wv", [1024, 1024]),
     ("wo", [1024, 1024]), ("bq", [128, NCH]), ("bk", [128, NCH]),
     ("bo", [128, NCH]), ("vbrow", [1, 1024]), ("pmrow", [1, 256]),
     ("iwrow", [1, 16]), ("gwt", [1024, 8]), ("b1c", [128, 256]),
     ("b2c", [128, 64]), ("idf", [128, 128])]
    + [(f"w1_{ex}", [1024, 4096]) for ex in range(8)]
    + [(f"w2_{ex}", [4096, 1024]) for ex in range(8)]
)



def build_layer_program(layer, T=2048):
    nc = bass.Bass("TRN2", target_bir_lowering=False)
    xin = nc.dram_tensor("xin", [1024, T], F32, kind="ExternalInput").ap()
    fconst = nc.dram_tensor("fconst", [1, NFCONST], F32, kind="ExternalInput").ap()
    xout = nc.dram_tensor("xout", [1024, T], F32, kind="ExternalOutput").ap()
    specs = LAYER_DRAM_SPECS[layer]
    aps = {}
    for name, shape in specs:
        if shape is None:
            shape = [T, NSEG * 128]
        aps[name] = nc.dram_tensor(f"w_{name}", list(shape), F32,
                                   kind="ExternalInput").ap()
    L = LayerCtx(nc, T)
    with L.ctx:
        L.setup_base(fconst)
        LAYER_BUILDERS[layer](L, aps, xin, xout)
        L.p.emit()
        L._stats = L.p.stats()
    return nc, L._stats, L.fconst_values()


def build_fused_program(T=2048):
    nc = bass.Bass("TRN2", target_bir_lowering=False)
    xin = nc.dram_tensor("xin", [1024, T], F32, kind="ExternalInput").ap()
    fconst = nc.dram_tensor("fconst", [1, NFCONST], F32, kind="ExternalInput").ap()
    xout = nc.dram_tensor("xout", [1024, T], F32, kind="ExternalOutput").ap()
    hs = [nc.dram_tensor(f"h{i}", [1024, T], F32, kind="Internal").ap()
          for i in range(3)]
    chain = [xin, hs[0], hs[1], hs[2], xout]
    L = LayerCtx(nc, T)
    with L.ctx:
        L.setup_base(fconst)
        for k in range(4):
            aps = {}
            for name, shape in LAYER_DRAM_SPECS[k]:
                if shape is None:
                    shape = [T, NSEG * 128]
                aps[name] = nc.dram_tensor(f"L{k}w_{name}", list(shape), F32,
                                           kind="ExternalInput").ap()
            inner = contextlib.ExitStack()
            L.ctx = inner
            L.ns = f"F{k}"
            LAYER_BUILDERS[k](L, aps, chain[k], chain[k + 1])
            inner.close()
            L.ctx = L.root_ctx
            L.p.full_barrier()
        L.p.emit()
        L._stats = L.p.stats()
    return nc, L._stats, L.fconst_values()


_PROG_CACHE = {}


def _get_fused(T=2048):
    if ("fused", T) not in _PROG_CACHE:
        _PROG_CACHE[("fused", T)] = build_fused_program(T)
    return _PROG_CACHE[("fused", T)]


def _get_prog(layer, T=2048):
    key = (layer, T)
    if key not in _PROG_CACHE:
        _PROG_CACHE[key] = build_layer_program(layer, T)
    return _PROG_CACHE[key]


LAYER_BUILDERS.update({0: build_l0, 1: build_l1, 2: build_l2, 3: build_l3})
LAYER_DRAM_SPECS.update({0: L0_DRAM_SPECS, 1: L1_DRAM_SPECS, 2: L2_DRAM_SPECS,
                         3: L3_DRAM_SPECS})
PREPS = {0: prep_l0, 1: prep_l1, 2: prep_l2, 3: prep_l3}


def run_layer(layer, x_fm_shards, inputs, T=2048, cores=None):
    nc, stats, fvals = _get_prog(layer, T)
    w = PREPS[layer](inputs)
    if layer == 2:
        if T not in _L2_MASK_CACHE:
            _L2_MASK_CACHE[T] = _l2_mask_host(T)
        w["mask"] = _L2_MASK_CACHE[T]
    in_maps = []
    for xs in x_fm_shards:
        m = {"xin": np.ascontiguousarray(xs, np.float32), "fconst": fvals}
        for k, v in w.items():
            m[f"w_{k}"] = np.ascontiguousarray(v, np.float32)
        in_maps.append(m)
    cores = cores or list(range(len(x_fm_shards)))
    res = run_bass_kernel_spmd(nc, in_maps, core_ids=cores)
    return [r["xout"] for r in res.results]


def kernel_unfused(**inputs):
    x = np.asarray(inputs["L0_attn_x"], np.float32)
    shards = [np.ascontiguousarray(x[i].T) for i in range(NCORES)]
    for layer in (0, 1, 2, 3):
        shards = run_layer(layer, shards, inputs)
    out = np.stack([np.asarray(s, np.float32).T for s in shards], axis=0)
    return np.ascontiguousarray(out)


def kernel(**inputs):
    T = 2048
    x = np.asarray(inputs["L0_attn_x"], np.float32)
    nc, stats, fvals = _get_fused(T)
    base = {"fconst": fvals}
    for k in range(4):
        w = PREPS[k](inputs)
        if k == 2:
            if T not in _L2_MASK_CACHE:
                _L2_MASK_CACHE[T] = _l2_mask_host(T)
            w["mask"] = _L2_MASK_CACHE[T]
        for name, v in w.items():
            base[f"L{k}w_{name}"] = np.ascontiguousarray(v, np.float32)
    in_maps = []
    for i in range(NCORES):
        m = dict(base)
        m["xin"] = np.ascontiguousarray(x[i].T)
        in_maps.append(m)
    res = run_bass_kernel_spmd(nc, in_maps, core_ids=list(range(NCORES)))
    out = np.stack([np.asarray(r["xout"], np.float32).T for r in res.results],
                   axis=0)
    return np.ascontiguousarray(out)


# version 10
# speedup vs baseline: 1.9355x; 1.1950x over previous
import contextlib
import numpy as np
import concourse.bass as bass
import concourse.mybir as mybir
from concourse.bass_utils import run_bass_kernel_spmd

F32 = mybir.dt.float32
BF16 = mybir.dt.bfloat16
AF = mybir.ActivationFunctionType
ALU = mybir.AluOpType
AX = mybir.AxisListType

NCH = 8
NCORES = 8


ENGINES = ("tensor", "vector", "scalar", "gpsimd", "sync")


class _Op:
    __slots__ = ("idx", "engine", "emit", "is_dma", "stream", "seq", "deps",
                 "waits", "pre_drain", "needed", "ord", "tag", "cclock")

    def __init__(self, idx, engine, emit, is_dma, stream, tag):
        self.idx = idx
        self.engine = engine
        self.emit = emit
        self.is_dma = is_dma
        self.stream = stream
        self.seq = 0
        self.deps = []
        self.waits = []
        self.pre_drain = False
        self.needed = False
        self.ord = None
        self.tag = tag
        self.cclock = None


def _merge(dst, src):
    for k, v in src.items():
        if dst.get(k, -1) < v:
            dst[k] = v


class Prog:
    def __init__(self, nc, ctx):
        self.nc = nc
        self.ctx = ctx
        self.ops = []
        self.last_writer = {}
        self.readers = {}
        self.streams = {}
        self.stream_sems = {}
        self.engine_sems = {}
        self._emitted = False
        self.n_drains = 0
        self.n_waits = 0

    def _op(self, engine, emit, reads, writes, is_dma=False, stream=None, tag=""):
        assert not self._emitted
        idx = len(self.ops)
        op = _Op(idx, engine, emit, is_dma, stream, tag)
        if is_dma:
            if stream is None:
                ks = list(writes) + list(reads)
                assert ks, f"dma op needs reads/writes or stream ({tag})"
                stream = ks[0]
            op.stream = stream
            self.streams[stream] = self.streams.get(stream, 0) + 1
            op.seq = self.streams[stream]
        deps = set()
        for k in reads:
            w = self.last_writer.get(k)
            if w is not None:
                deps.add(w)
        for k in writes:
            w = self.last_writer.get(k)
            if w is not None:
                deps.add(w)
            for r in self.readers.get(k, ()):
                deps.add(r)
        op.deps = sorted(deps)
        for k in reads:
            self.readers.setdefault(k, []).append(idx)
        for k in writes:
            self.last_writer[k] = idx
            self.readers[k] = []
        self.ops.append(op)
        return idx

    def t(self, emit, reads=(), writes=(), tag=""):
        return self._op("tensor", emit, reads, writes, tag=tag)

    def v(self, emit, reads=(), writes=(), tag=""):
        return self._op("vector", emit, reads, writes, tag=tag)

    def s(self, emit, reads=(), writes=(), tag=""):
        return self._op("scalar", emit, reads, writes, tag=tag)

    def g(self, emit, reads=(), writes=(), tag=""):
        return self._op("gpsimd", emit, reads, writes, tag=tag)

    def dma(self, queue, emit, reads=(), writes=(), stream=None, tag=""):
        return self._op(queue, emit, reads, writes, is_dma=True, stream=stream, tag=tag)

    def wait_all(self, engine, keys, tag="final"):
        return self._op(engine, lambda e: e.nop(), keys, (), tag=tag)

    def full_barrier(self, tag="bar"):
        self._barn = getattr(self, "_barn", 0) + 1
        n = self._barn
        keys = list(self.last_writer.keys())
        stage = []
        for e in ENGINES:
            k = f"__bar{n}_{e}"
            self._op(e, lambda e_: e_.nop(), keys, [k], tag=f"{tag}{n}a")
            stage.append(k)
        for e in ENGINES:
            self._op(e, lambda e_: e_.nop(), stage, (), tag=f"{tag}{n}b")

    def _resolve(self):
        know = {e: {} for e in ENGINES}
        drainmark = {e: -1 for e in ENGINES}
        lastop = {e: -1 for e in ENGINES}
        sindex = {}
        created = {}
        for op in self.ops:
            if op.is_dma:
                sindex[(op.stream, op.seq)] = op
        for op in self.ops:
            E = op.engine
            my_know = know[E]
            sem_waits = []
            stream_waits = {}
            need_drain = False
            for d in op.deps:
                p = self.ops[d]
                if p.is_dma:
                    if my_know.get(("s", p.stream), 0) >= p.seq:
                        continue
                    sw = stream_waits.get(p.stream, 0)
                    stream_waits[p.stream] = max(sw, created[p.stream])
                elif p.engine == E:
                    if E == "tensor":
                        continue
                    if p.idx <= drainmark[E]:
                        continue
                    if my_know.get(("e", E), -1) >= p.idx:
                        continue
                    need_drain = True
                else:
                    if my_know.get(("e", p.engine), -1) >= p.idx:
                        continue
                    sem_waits.append(p)
            if need_drain:
                op.pre_drain = True
                self.n_drains += 1
                drainmark[E] = lastop[E]
            waits = []
            for s, seq in stream_waits.items():
                waits.append(("s", s, seq))
                my_know[("s", s)] = max(my_know.get(("s", s), 0), seq)
                dop = sindex.get((s, seq))
                if dop is not None and dop.cclock:
                    _merge(my_know, dop.cclock)
            for pr in sem_waits:
                pr.needed = True
                waits.append(("e", pr.engine, pr.idx))
                my_know[("e", pr.engine)] = max(my_know.get(("e", pr.engine), -1), pr.idx)
                if pr.cclock:
                    _merge(my_know, pr.cclock)
            op.waits = waits
            self.n_waits += len(waits)
            cc = dict(my_know)
            if op.is_dma:
                cc[("s", op.stream)] = max(cc.get(("s", op.stream), 0), op.seq)
                created[op.stream] = created.get(op.stream, 0) + 1
            else:
                cc[("e", E)] = op.idx
            op.cclock = cc
            lastop[E] = op.idx

        counters = {e: 0 for e in ENGINES}
        ordinals = {}
        for op in self.ops:
            if not op.is_dma and op.needed:
                counters[op.engine] += 1
                op.ord = counters[op.engine]
                ordinals[op.idx] = op.ord
        for op in self.ops:
            fixed = []
            for kind, name, val in op.waits:
                if kind == "e":
                    fixed.append(("e", self.ops[val].engine, ordinals[val]))
                else:
                    fixed.append(("s", name, 16 * val))
            op.waits = fixed
        for e in ENGINES:
            if counters[e] > 0:
                self.engine_sems[e] = self.ctx.enter_context(
                    self.nc.semaphore(f"pe_{e}"))
        for s in self.streams:
            self.stream_sems[s] = self.ctx.enter_context(
                self.nc.semaphore(f"st_{s}"[:30]))
        n_sems = len(self.engine_sems) + len(self.stream_sems)
        assert n_sems <= 90, f"too many semaphores: {n_sems}"

    def emit(self):
        self._resolve()
        self._emitted = True

        def section(engine_name):
            def body(eng):
                for op in self.ops:
                    if op.engine != engine_name:
                        continue
                    if op.pre_drain:
                        eng.drain()
                    for kind, name, val in op.waits:
                        if val <= 0:
                            continue
                        sem = (self.engine_sems[name] if kind == "e"
                               else self.stream_sems[name])
                        eng.wait_ge(sem, val)
                    inst = op.emit(eng)
                    if op.is_dma:
                        inst.then_inc(self.stream_sems[op.stream], 16)
                    elif op.needed:
                        inst.then_inc(self.engine_sems[op.engine], 1)
            return body

        with self.nc.Block() as block:
            for name in ENGINES:
                if not any(op.engine == name for op in self.ops):
                    continue
                getattr(block, name)(section(name))

    def stats(self):
        from collections import Counter
        c = Counter(op.engine for op in self.ops)
        d = Counter(op.engine for op in self.ops if op.is_dma)
        return (f"ops={len(self.ops)} per-engine={dict(c)} dma={dict(d)} "
                f"waits={self.n_waits} drains={self.n_drains} "
                f"streams={len(self.streams)}")



NFCONST = 32


class LayerCtx:
    def __init__(self, nc, T):
        self.nc = nc
        self.T = T
        self.ctx = contextlib.ExitStack()
        self.root_ctx = self.ctx
        self.p = Prog(nc, self.root_ctx)
        self._fconsts = {}
        self.ns = ""

    def sbuf(self, name, shape, dtype):
        return self.ctx.enter_context(
            self.nc.sbuf_tensor(self.ns + name, list(shape), dtype))

    def psum(self, name, shape, dtype):
        return self.ctx.enter_context(
            self.nc.psum_tensor(self.ns + name, list(shape), dtype))

    def cap(self, v):
        v = float(v)
        if v not in self._fconsts:
            i = len(self._fconsts)
            assert i < NFCONST
            self._fconsts[v] = i
            self.nc.const_aps.register(F32, v, self.fc[0:1, i:i + 1])
        return self._fconsts[v]

    def fconst_values(self):
        vals = np.zeros((1, NFCONST), np.float32)
        for v, i in self._fconsts.items():
            vals[0, i] = v
        return vals

    def setup_base(self, fconst_ap, n_f32_banks=6, n_bf16_banks=2):
        p, T = self.p, self.T
        self.pb = [self.psum(f"pb{i}", [128, 512], F32) for i in range(n_f32_banks)]
        self.pbt2 = [self.psum(f"pbt{i}", [128, 1024], BF16)
                     for i in range(n_bf16_banks)]
        self.pbt = self.pbt2[0] if self.pbt2 else None
        self.fc = self.sbuf("fc_pool", [1, NFCONST], F32)
        p.dma("sync", lambda e: e.dma_start(out=self.fc[:], in_=fconst_ap),
              writes=["fconst"], tag="ld_fconst")
        self.cst = self.sbuf("cst_sb", [128, 8], F32)
        self.ones_row = self.sbuf("ones_row_sb", [1, 128], F32)
        self.ones_row_bf = self.sbuf("ones_row_bf", [1, T], BF16)
        p.v(lambda e: e.memset(self.cst[:, 0:1], 1.0), writes=["cst"], tag="ms_cst")
        p.v(lambda e: e.memset(self.ones_row[:], 1.0), writes=["ones_row"], tag="ms_ones")
        p.v(lambda e: e.memset(self.ones_row_bf[:], 1.0), writes=["ones_row_bf"],
            tag="ms_onesbf")
        self.ones_col = self.cst[:, 0:1]

    def load_x(self, xin_ap):
        p, T = self.p, self.T
        self.x = self.sbuf("x_fm", [128, NCH * T], F32)
        for a in range(NCH):
            p.dma("sync", lambda e, a=a: e.dma_start(
                out=self.x[:, a * T:(a + 1) * T],
                in_=xin_ap[a * 128:(a + 1) * 128, :]),
                writes=[f"x:{a}"], stream="x_load", tag=f"load_x{a}")

    def store_x(self, xout_ap):
        p, T = self.p, self.T
        for a in range(NCH):
            p.dma("sync", lambda e, a=a: e.dma_start(
                out=xout_ap[a * 128:(a + 1) * 128, :],
                in_=self.x[:, a * T:(a + 1) * T]),
                reads=[f"x:{a}"], writes=[f"xout:{a}"], stream="x_store",
                tag=f"store_x{a}")
        p.wait_all("sync", [f"xout:{a}" for a in range(NCH)], tag="final_store")

    def layernorm(self):
        nc, p, T = self.nc, self.p, self.T
        x, pb = self.x, self.pb
        ntt = (T + 511) // 512
        assert ntt <= 4
        xn = self.sbuf("xn_bf", [128, NCH * T], BF16)
        x2 = self.sbuf("ln_x2", [128, 512], F32)
        stat2 = self.sbuf("ln_stats", [1, 2 * T], F32)
        stat_m = stat2[0:1, 0:T]
        stat_q = stat2[0:1, T:2 * T]
        tts = t_tiles(T)
        for a in range(NCH):
            for ti, (t0, tw) in enumerate(tts):
                p.s(lambda e, a=a, t0=t0, tw=tw: e.activation(
                    out=x2[:, 0:tw], in_=x[:, a * T + t0:a * T + t0 + tw],
                    func=AF.Square, bias=0.0, scale=1.0),
                    reads=[f"x:{a}"], writes=["ln_x2"], tag=f"ln_sq{a}")
                p.t(lambda e, a=a, t0=t0, tw=tw, ti=ti: e.matmul(
                    out=pb[ti][0:1, 0:tw], lhsT=self.ones_col,
                    rhs=x[:, a * T + t0:a * T + t0 + tw],
                    start=(a == 0), stop=(a == NCH - 1)),
                    reads=[f"x:{a}", "cst"], writes=[f"pb{ti}"], tag="ln_mm")
                p.t(lambda e, a=a, tw=tw, ti=ti: e.matmul(
                    out=pb[4 + ti][0:1, 0:tw], lhsT=self.ones_col,
                    rhs=x2[:, 0:tw],
                    start=(a == 0), stop=(a == NCH - 1)),
                    reads=["ln_x2", "cst"], writes=[f"pb{4 + ti}"], tag="ln_mq")
        for ti, (t0, tw) in enumerate(tts):
            p.s(lambda e, ti=ti, t0=t0, tw=tw: e.copy(
                out=stat_m[:, t0:t0 + tw], in_=pb[ti][0:1, 0:tw]),
                reads=[f"pb{ti}"], writes=["ln_m"], tag="ln_evm")
            p.s(lambda e, ti=ti, t0=t0, tw=tw: e.copy(
                out=stat_q[:, t0:t0 + tw], in_=pb[4 + ti][0:1, 0:tw]),
                reads=[f"pb{4 + ti}"], writes=["ln_q"], tag="ln_evq")
        self.cap(1.0 / 1024.0)
        self.cap(1e-5)
        self.cap(-0.5)
        for ti, (t0, tw) in enumerate(tts):
            p.s(lambda e, ti=ti, t0=t0, tw=tw: e.activation(
                out=pb[ti][0:1, 0:tw], in_=stat_m[:, t0:t0 + tw],
                func=AF.Square, bias=0.0, scale=1.0 / 1024.0),
                reads=["ln_m", "fconst"], writes=[f"pb{ti}"], tag="ln_m2")
            p.v(lambda e, ti=ti, t0=t0, tw=tw: e.scalar_tensor_tensor(
                out=stat_q[:, t0:t0 + tw], in0=stat_q[:, t0:t0 + tw],
                scalar=1.0 / 1024.0, in1=pb[ti][0:1, 0:tw],
                op0=ALU.mult, op1=ALU.subtract),
                reads=["ln_q", f"pb{ti}"], writes=["ln_q"], tag="ln_var")
        p.s(lambda e: e.activation(out=stat_q[:], in_=stat_q[:], func=AF.Ln,
                                   bias=1e-5, scale=1.0),
            reads=["ln_q", "fconst"], writes=["ln_q"], tag="ln_ln")
        p.s(lambda e: e.activation(out=stat_q[:], in_=stat_q[:], func=AF.Exp,
                                   bias=0.0, scale=-0.5),
            reads=["ln_q", "fconst"], writes=["ln_q"], tag="ln_exp")
        p.v(lambda e: e.scalar_tensor_tensor(
            out=stat_m[:], in0=stat_m[:], scalar=1.0 / 1024.0, in1=stat_q[:],
            op0=ALU.mult, op1=ALU.mult),
            reads=["ln_m", "ln_q"], writes=["ln_m"], tag="ln_musc")
        for ti, (t0, tw) in enumerate(tts):
            p.t(lambda e, ti=ti, t0=t0, tw=tw: e.matmul(
                out=pb[ti][:, 0:tw], lhsT=self.ones_row[0:1, 0:128],
                rhs=stat_q[:, t0:t0 + tw], start=True, stop=True),
                reads=["ln_q", "ones_row"], writes=[f"pb{ti}"], tag="ln_bcr")
            p.t(lambda e, ti=ti, t0=t0, tw=tw: e.matmul(
                out=pb[4 + ti][:, 0:tw], lhsT=self.ones_row[0:1, 0:128],
                rhs=stat_m[:, t0:t0 + tw], start=True, stop=True),
                reads=["ln_m", "ones_row"], writes=[f"pb{4 + ti}"], tag="ln_bcm")
        tmp = self.sbuf("ln_tmp", [128, 512], F32)
        self.xn_scratch_a = x2
        self.xn_scratch_b = tmp
        for a in range(NCH):
            for ti, (t0, tw) in enumerate(tts):
                p.v(lambda e, a=a, ti=ti, t0=t0, tw=tw: e.tensor_tensor(
                    out=tmp[:, 0:tw], in0=x[:, a * T + t0:a * T + t0 + tw],
                    in1=pb[ti][:, 0:tw], op=ALU.mult),
                    reads=[f"x:{a}", f"pb{ti}"], writes=["ln_tmp"], tag="ln_xr")
                p.v(lambda e, a=a, ti=ti, t0=t0, tw=tw: e.tensor_tensor(
                    out=xn[:, a * T + t0:a * T + t0 + tw], in0=tmp[:, 0:tw],
                    in1=pb[4 + ti][:, 0:tw], op=ALU.subtract),
                    reads=["ln_tmp", f"pb{4 + ti}"], writes=[f"xn:{a}"], tag="ln_xn")
        self.xn = xn
        return xn

    def layernorm_streaming(self, xin_ap):
        nc, p, T = self.nc, self.p, self.T
        pb = self.pb
        tts = t_tiles(T)
        xn = self.sbuf("xn_bf", [128, NCH * T], BF16)
        xs = [self.sbuf(f"ln_xs{i}", [128, 512], F32) for i in range(2)]
        x2 = self.sbuf("ln_x2", [128, 512], F32)
        stat2 = self.sbuf("ln_stats", [1, 2 * T], F32)
        stat_m = stat2[0:1, 0:T]
        stat_q = stat2[0:1, T:2 * T]
        self.cap(1.0 / 1024.0)
        self.cap(1e-5)
        self.cap(-0.5)
        n = 0
        for ti, (t0, tw) in enumerate(tts):
            for a in range(NCH):
                b = n % 2
                n += 1
                p.dma("sync", lambda e, a=a, t0=t0, tw=tw, b=b: e.dma_start(
                    out=xs[b][:, 0:tw],
                    in_=xin_ap[a * 128:(a + 1) * 128, t0:t0 + tw]),
                    writes=[f"ln_xs{b}"], tag="lns_ldx")
                p.s(lambda e, tw=tw, b=b: e.activation(
                    out=x2[:, 0:tw], in_=xs[b][:, 0:tw], func=AF.Square,
                    bias=0.0, scale=1.0),
                    reads=[f"ln_xs{b}"], writes=["ln_x2"], tag="lns_sq")
                p.t(lambda e, a=a, tw=tw, b=b: e.matmul(
                    out=pb[0][0:1, 0:tw], lhsT=self.ones_col,
                    rhs=xs[b][:, 0:tw],
                    start=(a == 0), stop=(a == NCH - 1)),
                    reads=[f"ln_xs{b}", "cst"], writes=["pb0"], tag="lns_mm")
                p.t(lambda e, a=a, tw=tw: e.matmul(
                    out=pb[1][0:1, 0:tw], lhsT=self.ones_col,
                    rhs=x2[:, 0:tw],
                    start=(a == 0), stop=(a == NCH - 1)),
                    reads=["ln_x2", "cst"], writes=["pb1"], tag="lns_mq")
            p.s(lambda e, ti=ti, t0=t0, tw=tw: e.copy(
                out=stat_m[:, t0:t0 + tw], in_=pb[0][0:1, 0:tw]),
                reads=["pb0"], writes=["ln_m"], tag="lns_evm")
            p.s(lambda e, ti=ti, t0=t0, tw=tw: e.copy(
                out=stat_q[:, t0:t0 + tw], in_=pb[1][0:1, 0:tw]),
                reads=["pb1"], writes=["ln_q"], tag="lns_evq")
            p.s(lambda e, t0=t0, tw=tw: e.activation(
                out=pb[0][0:1, 0:tw], in_=stat_m[:, t0:t0 + tw],
                func=AF.Square, bias=0.0, scale=1.0 / 1024.0),
                reads=["ln_m", "fconst"], writes=["pb0"], tag="lns_m2")
            p.v(lambda e, t0=t0, tw=tw: e.scalar_tensor_tensor(
                out=stat_q[:, t0:t0 + tw], in0=stat_q[:, t0:t0 + tw],
                scalar=1.0 / 1024.0, in1=pb[0][0:1, 0:tw],
                op0=ALU.mult, op1=ALU.subtract),
                reads=["ln_q", "pb0"], writes=["ln_q"], tag="lns_var")
        p.s(lambda e: e.activation(out=stat_q[:], in_=stat_q[:], func=AF.Ln,
                                   bias=1e-5, scale=1.0),
            reads=["ln_q", "fconst"], writes=["ln_q"], tag="lns_ln")
        p.s(lambda e: e.activation(out=stat_q[:], in_=stat_q[:], func=AF.Exp,
                                   bias=0.0, scale=-0.5),
            reads=["ln_q", "fconst"], writes=["ln_q"], tag="lns_exp")
        p.v(lambda e: e.scalar_tensor_tensor(
            out=stat_m[:], in0=stat_m[:], scalar=1.0 / 1024.0, in1=stat_q[:],
            op0=ALU.mult, op1=ALU.mult),
            reads=["ln_m", "ln_q"], writes=["ln_m"], tag="lns_musc")
        n = 0
        for ti, (t0, tw) in enumerate(tts):
            p.t(lambda e, t0=t0, tw=tw: e.matmul(
                out=pb[2][:, 0:tw], lhsT=self.ones_row[0:1, 0:128],
                rhs=stat_q[:, t0:t0 + tw], start=True, stop=True),
                reads=["ln_q", "ones_row"], writes=["pb2"], tag="lns_bcr")
            p.t(lambda e, t0=t0, tw=tw: e.matmul(
                out=pb[3][:, 0:tw], lhsT=self.ones_row[0:1, 0:128],
                rhs=stat_m[:, t0:t0 + tw], start=True, stop=True),
                reads=["ln_m", "ones_row"], writes=["pb3"], tag="lns_bcm")
            for a in range(NCH):
                b = n % 2
                n += 1
                p.dma("sync", lambda e, a=a, t0=t0, tw=tw, b=b: e.dma_start(
                    out=xs[b][:, 0:tw],
                    in_=xin_ap[a * 128:(a + 1) * 128, t0:t0 + tw]),
                    writes=[f"ln_xs{b}"], tag="lns_ldx2")
                p.v(lambda e, tw=tw, b=b: e.tensor_tensor(
                    out=x2[:, 0:tw], in0=xs[b][:, 0:tw],
                    in1=pb[2][:, 0:tw], op=ALU.mult),
                    reads=[f"ln_xs{b}", "pb2"], writes=["ln_x2"], tag="lns_xr")
                p.v(lambda e, a=a, t0=t0, tw=tw: e.tensor_tensor(
                    out=xn[:, a * T + t0:a * T + t0 + tw], in0=x2[:, 0:tw],
                    in1=pb[3][:, 0:tw], op=ALU.subtract),
                    reads=["ln_x2", "pb3"], writes=[f"xn:{a}"], tag="lns_xn")
        self.xn = xn
        self.xn_scratch_a = x2
        self.xn_scratch_b = xs[1]
        self.ln_stat2 = stat2
        self.ln_xs = xs
        return xn


def t_tiles(T, w=512):
    return [(i, min(w, T - i)) for i in range(0, T, w)]



def build_l1(L, aps, xin, xout):
    nc, p, T = L.nc, L.p, L.T
    p.dma("sync", lambda e: e.dma_start(out=xout, in_=xin),
          writes=[f"xout:{a}" for a in range(NCH)], stream="xout_init",
          tag="l1_xout_init")
    L.layernorm_streaming(xin)
    xn, pb = L.xn, L.pb
    tts = t_tiles(T)
    ntt = len(tts)

    prm = L.sbuf("l1_prm", [128, 9 * NCH], F32)
    for i, n in enumerate(["td", "tf", "tmk", "tmv", "tmr"]):
        p.dma("sync", lambda e, i=i, n=n: e.dma_start(
            out=prm[:, i * NCH:(i + 1) * NCH], in_=aps[n]),
            writes=[f"l1prm:{n}"], stream="l1_prm", tag=f"l1_prm_{n}")
    pk = ["l1prm:td", "l1prm:tf", "l1prm:tmk", "l1prm:tmv", "l1prm:tmr"]
    p.s(lambda e: e.activation(out=prm[:, 5 * NCH:6 * NCH], in_=prm[:, 0:NCH],
                               func=AF.Exp, bias=0.0, scale=1.0),
        reads=[pk[0]], writes=["l1prm:w0"], tag="l1_w0")
    p.v(lambda e: e.tensor_scalar(out=prm[:, 5 * NCH:6 * NCH],
                                  in0=prm[:, 5 * NCH:6 * NCH],
                                  scalar1=-1.0, scalar2=None, op0=ALU.mult),
        reads=["l1prm:w0"], writes=["l1prm:w"], tag="l1_w")
    for j in range(3):
        p.v(lambda e, j=j: e.tensor_scalar(
            out=prm[:, (6 + j) * NCH:(7 + j) * NCH],
            in0=prm[:, (2 + j) * NCH:(3 + j) * NCH],
            scalar1=-1.0, scalar2=1.0, op0=ALU.mult, op1=ALU.add),
            reads=[pk[2 + j]], writes=[f"l1prm:m1_{j}"], tag=f"l1_1m{j}")
    PK_ALL = ["l1prm:w"] + [f"l1prm:m1_{j}" for j in range(3)] + pk

    def col(i, a):
        return prm[:, i * NCH + a:i * NCH + a + 1]

    wbuf = L.sbuf("l1_wbuf", [128, NCH * 1024], BF16)
    wstg = [L.sbuf(f"l1_wstg{i}", [128, 1024], F32) for i in range(2)]

    def load_weight(name, tag):
        for a in range(NCH):
            b = a % 2
            p.dma("sync", lambda e, a=a, b=b: e.dma_start(
                out=wstg[b][:], in_=aps[name][a * 128:(a + 1) * 128, :]),
                writes=[f"l1_wstg{b}"], tag=f"ld_{tag}{a}")
            p.v(lambda e, a=a, b=b: e.tensor_copy(
                out=wbuf[:, a * 1024:(a + 1) * 1024], in_=wstg[b][:]),
                reads=[f"l1_wstg{b}"], writes=[f"wbuf:{a}"], tag=f"cv_{tag}{a}")

    WBUF_ALL = [f"wbuf:{a}" for a in range(NCH)]

    vr = L.sbuf("l1_vr", [128, NCH * 32], BF16)
    vr32 = L.sbuf("l1_vr32", [128, NCH * 32], F32)
    ct = L.sbuf("l1_ct", [32, 1024], BF16)
    ct32 = L.sbuf("l1_ct32", [32, 1024], F32)
    bcol = L.sbuf("l1_bcol", [128, NCH], F32)
    p.dma("sync", lambda e: e.dma_start(
        out=vr32[:].rearrange("p (a r) -> p a r", a=NCH),
        in_=aps["vr"].rearrange("(a p) r -> p a r", p=128)),
        writes=["l1_vr32"], tag="l1_ldvr")
    p.dma("sync", lambda e: e.dma_start(out=ct32[:], in_=aps["ct"]),
          writes=["l1_ct32"], tag="l1_ldct")
    p.dma("sync", lambda e: e.dma_start(out=bcol[:], in_=aps["b"]),
          writes=["l1_bcol"], tag="l1_ldb")
    p.v(lambda e: e.tensor_copy(out=vr[:], in_=vr32[:]), reads=["l1_vr32"],
        writes=["l1_vr"], tag="l1_cvvr")
    p.v(lambda e: e.tensor_copy(out=ct[:], in_=ct32[:]), reads=["l1_ct32"],
        writes=["l1_ct"], tag="l1_cvct")
    xr = L.sbuf("l1_xr", [32, T], BF16)
    ob1 = L.ln_xs[0]
    for ti, (t0, tw) in enumerate(tts):
        for a in range(NCH):
            p.t(lambda e, a=a, t0=t0, tw=tw: e.matmul(
                out=pb[0][0:32, 0:tw], lhsT=vr[:, a * 32:(a + 1) * 32],
                rhs=xn[:, a * T + t0:a * T + t0 + tw],
                start=(a == 0), stop=(a == NCH - 1)),
                reads=["l1_vr", f"xn:{a}"], writes=["pb0"], tag="l1_xr_mm")
        p.s(lambda e, t0=t0, tw=tw: e.copy(
            out=xr[:, t0:t0 + tw], in_=pb[0][0:32, 0:tw]),
            reads=["pb0"], writes=["l1_xr"], tag="l1_xr_ev")
    load_weight("wt", "ffnw")
    for oc in range(NCH):
        for ti, (t0, tw) in enumerate(tts):
            pbi = 1 + (ti % 2)
            for a in range(NCH):
                p.t(lambda e, a=a, oc=oc, t0=t0, tw=tw, pbi=pbi: e.matmul(
                    out=pb[pbi][:, 0:tw],
                    lhsT=wbuf[:, a * 1024 + oc * 128:a * 1024 + (oc + 1) * 128],
                    rhs=xn[:, a * T + t0:a * T + t0 + tw],
                    start=(a == 0), stop=False),
                    reads=[f"wbuf:{a}", f"xn:{a}"], writes=[f"pb{pbi}"],
                    tag="l1_ffn_mm")
            p.t(lambda e, oc=oc, t0=t0, tw=tw, pbi=pbi: e.matmul(
                out=pb[pbi][:, 0:tw], lhsT=ct[:, oc * 128:(oc + 1) * 128],
                rhs=xr[:, t0:t0 + tw], start=False, stop=True),
                reads=["l1_ct", "l1_xr"], writes=[f"pb{pbi}"], tag="l1_ffn_cr")
            p.v(lambda e, oc=oc, tw=tw, pbi=pbi: e.tensor_scalar(
                out=ob1[:, 0:tw], in0=pb[pbi][:, 0:tw],
                scalar1=bcol[:, oc:oc + 1], scalar2=None, op0=ALU.add),
                reads=[f"pb{pbi}", "l1_bcol"], writes=["l1_ob", "ln_xs0"],
                tag="l1_ffn_ev")
            p.dma("gpsimd", lambda e, oc=oc, t0=t0, tw=tw: e.dma_start(
                out=xout[oc * 128:(oc + 1) * 128, t0:t0 + tw],
                in_=ob1[:, 0:tw], accum_op=ALU.add),
                reads=["l1_ob", "ln_xs0"], writes=[f"xout:{oc}"],
                stream=f"l1_xo{oc % 4}", tag="l1_ffn_acc")

    kf = L.sbuf("l1_k", [128, NCH * 512], F32)
    vf = L.sbuf("l1_v", [128, NCH * 512], BF16)
    srf = L.sbuf("l1_sr", [128, NCH * 512], BF16)
    sm1 = L.sbuf("l1_sm1", [128, 512], BF16)
    sm2 = L.sbuf("l1_sm2", [128, 512], BF16)
    tA = L.sbuf("l1_tA", [128, 512], F32)
    tB = L.sbuf("l1_tB", [128, 512], F32)
    tC = L.sbuf("l1_tC", [128, 512], F32)
    tD = L.sbuf("l1_tD", [128, 512], F32)
    tE = L.sbuf("l1_tE", [128, 512], F32)
    tF = L.sbuf("l1_tF", [128, 512], F32)
    tG = L.xn_scratch_a
    tH = L.xn_scratch_b
    rw = L.sbuf("l1_rw", [128, NCH * 512], BF16)
    carry = L.sbuf("l1_carry", [128, 3 * NCH], F32)
    p.v(lambda e: e.memset(carry[:, 0:NCH], 0.0), writes=["l1_caa"], tag="l1_c0")
    p.v(lambda e: e.memset(carry[:, NCH:2 * NCH], 0.0), writes=["l1_cbb"], tag="l1_c1")
    p.v(lambda e: e.memset(carry[:, 2 * NCH:3 * NCH], -1e38), writes=["l1_cpp"],
        tag="l1_c2")

    projmap = [("k", 2, "wkt"), ("v", 3, "wvt"), ("r", 4, "wrt")]
    for ti, (t0, tw) in enumerate(tts):
        for pname, mi, wname in projmap:
            load_weight(wname, f"{wname}_{ti}")
            for dg in (0, 4):
                for a in range(NCH):
                    p.v(lambda e, a=a, mi=mi, t0=t0, tw=tw: e.tensor_scalar(
                        out=sm1[:, 0:tw], in0=xn[:, a * T + t0:a * T + t0 + tw],
                        scalar1=col(mi, a), scalar2=None, op0=ALU.mult),
                        reads=[f"xn:{a}", pk[mi]],
                        writes=["l1_sm1"], tag="l1_mx")
                    lo = max(t0 - 1, 0)
                    w2 = tw - 1 if t0 == 0 else tw
                    p.v(lambda e, a=a, mi=mi, lo=lo, w2=w2: e.tensor_scalar(
                        out=sm2[:, 0:w2], in0=xn[:, a * T + lo:a * T + lo + w2],
                        scalar1=col(mi + 4, a), scalar2=None, op0=ALU.mult),
                        reads=[f"xn:{a}", f"l1prm:m1_{mi - 2}"],
                        writes=["l1_sm2"], tag="l1_ms")
                    ob = 1 if t0 == 0 else 0
                    for d in range(dg, dg + 4):
                        p.t(lambda e, a=a, d=d, tw=tw: e.matmul(
                            out=pb[d % 4][:, 0:tw],
                            lhsT=wbuf[:, a * 1024 + d * 128:a * 1024 + (d + 1) * 128],
                            rhs=sm1[:, 0:tw], start=(a == 0), stop=False),
                            reads=["l1_sm1", f"wbuf:{a}"], writes=[f"pb{d % 4}"],
                            tag="l1_pj")
                        p.t(lambda e, a=a, d=d, ob=ob, w2=w2: e.matmul(
                            out=pb[d % 4][:, ob:ob + w2],
                            lhsT=wbuf[:, a * 1024 + d * 128:a * 1024 + (d + 1) * 128],
                            rhs=sm2[:, 0:w2], start=False, stop=(a == NCH - 1)),
                            reads=["l1_sm2", f"wbuf:{a}"], writes=[f"pb{d % 4}"],
                            tag="l1_pjs")
                for d in range(dg, dg + 4):
                    if pname == "k":
                        p.s(lambda e, d=d, tw=tw: e.copy(
                            out=kf[:, d * 512:d * 512 + tw], in_=pb[d % 4][:, 0:tw]),
                            reads=[f"pb{d % 4}"], writes=[f"l1_k:{d}"], tag="l1_evk")
                    elif pname == "v":
                        p.s(lambda e, d=d, tw=tw: e.copy(
                            out=vf[:, d * 512:d * 512 + tw], in_=pb[d % 4][:, 0:tw]),
                            reads=[f"pb{d % 4}"], writes=[f"l1_v:{d}"], tag="l1_evv")
                    else:
                        p.s(lambda e, d=d, tw=tw: e.activation(
                            out=srf[:, d * 512:d * 512 + tw], in_=pb[d % 4][:, 0:tw],
                            func=AF.Sigmoid, bias=0.0, scale=1.0),
                            reads=[f"pb{d % 4}"], writes=[f"l1_sr:{d}"], tag="l1_evr")

        for d in range(NCH):
            kc = kf[:, d * 512:d * 512 + tw]
            vc = vf[:, d * 512:d * 512 + tw]
            src = srf[:, d * 512:d * 512 + tw]
            caa = carry[:, d:d + 1]
            cbb = carry[:, NCH + d:NCH + d + 1]
            cpp = carry[:, 2 * NCH + d:2 * NCH + d + 1]
            rk, rv, rs = f"l1_k:{d}", f"l1_v:{d}", f"l1_sr:{d}"
            p.v(lambda e, d=d, tw=tw, kc=kc: e.tensor_scalar(
                out=tA[:, 0:tw], in0=kc, scalar1=0.0, scalar2=col(5, d),
                op0=ALU.mult, op1=ALU.add),
                reads=[rk, "l1prm:w"], writes=["l1_tA"], tag="l1_wbc")
            p.v(lambda e, tw=tw, kc=kc, cpp=cpp: e.tensor_tensor_scan(
                out=tB[:, 0:tw], data0=tA[:, 0:tw], data1=kc,
                initial=cpp, op0=ALU.add, op1=ALU.max),
                reads=["l1_tA", rk, "l1_cpp"], writes=["l1_tB"], tag="l1_scpp")
            p.v(lambda e, cpp=cpp: e.tensor_copy(out=tC[:, 0:1], in_=cpp),
                reads=["l1_cpp"], writes=["l1_tC"], tag="l1_ppp0")
            p.v(lambda e, tw=tw: e.tensor_copy(
                out=tC[:, 1:tw], in_=tB[:, 0:tw - 1]),
                reads=["l1_tB"], writes=["l1_tC"], tag="l1_ppp")
            p.v(lambda e, tw=tw, cpp=cpp: e.tensor_copy(
                out=cpp, in_=tB[:, tw - 1:tw]),
                reads=["l1_tB"], writes=["l1_cpp"], tag="l1_cppu")
            p.v(lambda e, tw=tw: e.tensor_tensor(
                out=tD[:, 0:tw], in0=tC[:, 0:tw], in1=tB[:, 0:tw], op=ALU.subtract),
                reads=["l1_tC", "l1_tB"], writes=["l1_tD"], tag="l1_lam0")
            p.s(lambda e, d=d, tw=tw: e.activation(
                out=tD[:, 0:tw], in_=tD[:, 0:tw], func=AF.Exp, bias=col(5, d),
                scale=1.0),
                reads=["l1_tD", "l1prm:w"], writes=["l1_tD"], tag="l1_lam")
            p.v(lambda e, tw=tw, kc=kc: e.tensor_tensor(
                out=tE[:, 0:tw], in0=kc, in1=tB[:, 0:tw], op=ALU.subtract),
                reads=[rk, "l1_tB"], writes=["l1_tE"], tag="l1_bt0")
            p.s(lambda e, tw=tw: e.activation(
                out=tE[:, 0:tw], in_=tE[:, 0:tw], func=AF.Exp, bias=0.0, scale=1.0),
                reads=["l1_tE"], writes=["l1_tE"], tag="l1_bt")
            p.v(lambda e, tw=tw, vc=vc: e.tensor_tensor(
                out=tF[:, 0:tw], in0=tE[:, 0:tw], in1=vc, op=ALU.mult),
                reads=["l1_tE", rv], writes=["l1_tF"], tag="l1_bv")
            p.v(lambda e, tw=tw, caa=caa: e.tensor_tensor_scan(
                out=tG[:, 0:tw], data0=tD[:, 0:tw], data1=tF[:, 0:tw],
                initial=caa, op0=ALU.mult, op1=ALU.add),
                reads=["l1_tD", "l1_tF", "l1_caa"], writes=["l1_tG", "ln_x2"], tag="l1_saa")
            p.v(lambda e, caa=caa: e.tensor_copy(out=tH[:, 0:1], in_=caa),
                reads=["l1_caa"], writes=["l1_tH", "ln_xs1"], tag="l1_aap0")
            p.v(lambda e, tw=tw: e.tensor_copy(
                out=tH[:, 1:tw], in_=tG[:, 0:tw - 1]),
                reads=["l1_tG"], writes=["l1_tH"], tag="l1_aap")
            p.v(lambda e, tw=tw, caa=caa: e.tensor_copy(
                out=caa, in_=tG[:, tw - 1:tw]),
                reads=["l1_tG"], writes=["l1_caa"], tag="l1_caau")
            p.v(lambda e, tw=tw, cbb=cbb: e.tensor_tensor_scan(
                out=tF[:, 0:tw], data0=tD[:, 0:tw], data1=tE[:, 0:tw],
                initial=cbb, op0=ALU.mult, op1=ALU.add),
                reads=["l1_tD", "l1_tE", "l1_cbb"], writes=["l1_tF"], tag="l1_sbb")
            p.v(lambda e, cbb=cbb: e.tensor_copy(out=tA[:, 0:1], in_=cbb),
                reads=["l1_cbb"], writes=["l1_tA"], tag="l1_bbp0")
            p.v(lambda e, tw=tw: e.tensor_copy(
                out=tA[:, 1:tw], in_=tF[:, 0:tw - 1]),
                reads=["l1_tF"], writes=["l1_tA"], tag="l1_bbp")
            p.v(lambda e, tw=tw, cbb=cbb: e.tensor_copy(
                out=cbb, in_=tF[:, tw - 1:tw]),
                reads=["l1_tF"], writes=["l1_cbb"], tag="l1_cbbu")
            p.v(lambda e, d=d, tw=tw, kc=kc: e.tensor_scalar(
                out=tB[:, 0:tw], in0=kc, scalar1=col(1, d), scalar2=None,
                op0=ALU.add),
                reads=[rk, pk[1]], writes=["l1_tB"], tag="l1_ww")
            p.v(lambda e, tw=tw: e.tensor_tensor(
                out=tD[:, 0:tw], in0=tC[:, 0:tw], in1=tB[:, 0:tw], op=ALU.max),
                reads=["l1_tC", "l1_tB"], writes=["l1_tD"], tag="l1_q")
            p.v(lambda e, tw=tw: e.tensor_tensor(
                out=tC[:, 0:tw], in0=tC[:, 0:tw], in1=tD[:, 0:tw], op=ALU.subtract),
                reads=["l1_tC", "l1_tD"], writes=["l1_tC"], tag="l1_e10")
            p.s(lambda e, tw=tw: e.activation(
                out=tC[:, 0:tw], in_=tC[:, 0:tw], func=AF.Exp, bias=0.0, scale=1.0),
                reads=["l1_tC"], writes=["l1_tC"], tag="l1_e1")
            p.v(lambda e, tw=tw: e.tensor_tensor(
                out=tB[:, 0:tw], in0=tB[:, 0:tw], in1=tD[:, 0:tw], op=ALU.subtract),
                reads=["l1_tB", "l1_tD"], writes=["l1_tB"], tag="l1_e20")
            p.s(lambda e, tw=tw: e.activation(
                out=tB[:, 0:tw], in_=tB[:, 0:tw], func=AF.Exp, bias=0.0, scale=1.0),
                reads=["l1_tB"], writes=["l1_tB"], tag="l1_e2")
            p.v(lambda e, tw=tw: e.tensor_tensor(
                out=tH[:, 0:tw], in0=tC[:, 0:tw], in1=tH[:, 0:tw], op=ALU.mult),
                reads=["l1_tC", "l1_tH"], writes=["l1_tH"], tag="l1_n1")
            p.v(lambda e, tw=tw, vc=vc: e.tensor_tensor(
                out=tD[:, 0:tw], in0=tB[:, 0:tw], in1=vc, op=ALU.mult),
                reads=["l1_tB", rv], writes=["l1_tD"], tag="l1_n2")
            p.v(lambda e, tw=tw: e.tensor_tensor(
                out=tH[:, 0:tw], in0=tH[:, 0:tw], in1=tD[:, 0:tw], op=ALU.add),
                reads=["l1_tH", "l1_tD"], writes=["l1_tH"], tag="l1_num")
            p.v(lambda e, tw=tw: e.tensor_tensor(
                out=tA[:, 0:tw], in0=tC[:, 0:tw], in1=tA[:, 0:tw], op=ALU.mult),
                reads=["l1_tC", "l1_tA"], writes=["l1_tA"], tag="l1_d1")
            p.v(lambda e, tw=tw: e.tensor_tensor(
                out=tA[:, 0:tw], in0=tA[:, 0:tw], in1=tB[:, 0:tw], op=ALU.add),
                reads=["l1_tA", "l1_tB"], writes=["l1_tA"], tag="l1_den")
            p.s(lambda e, tw=tw: e.activation(
                out=tA[:, 0:tw], in_=tA[:, 0:tw], func=AF.Ln, bias=0.0, scale=1.0),
                reads=["l1_tA"], writes=["l1_tA"], tag="l1_rd0")
            p.s(lambda e, tw=tw: e.activation(
                out=tA[:, 0:tw], in_=tA[:, 0:tw], func=AF.Exp, bias=0.0, scale=-1.0),
                reads=["l1_tA"], writes=["l1_tA"], tag="l1_rd1")
            p.v(lambda e, tw=tw: e.tensor_tensor(
                out=tH[:, 0:tw], in0=tH[:, 0:tw], in1=tA[:, 0:tw], op=ALU.mult),
                reads=["l1_tH", "l1_tA"], writes=["l1_tH"], tag="l1_y")
            p.v(lambda e, d=d, tw=tw, src=src: e.tensor_tensor(
                out=rw[:, d * 512:d * 512 + tw], in0=tH[:, 0:tw], in1=src,
                op=ALU.mult),
                reads=["l1_tH", rs], writes=[f"l1_rw:{d}"], tag="l1_rwkv")

        load_weight("wot", f"wo_{ti}")
        for oc in range(NCH):
            pbi = oc % 4
            for a in range(NCH):
                p.t(lambda e, a=a, oc=oc, tw=tw, pbi=pbi: e.matmul(
                    out=pb[pbi][:, 0:tw],
                    lhsT=wbuf[:, a * 1024 + oc * 128:a * 1024 + (oc + 1) * 128],
                    rhs=rw[:, a * 512:a * 512 + tw],
                    start=(a == 0), stop=(a == NCH - 1)),
                    reads=[f"wbuf:{a}", f"l1_rw:{a}"], writes=[f"pb{pbi}"],
                    tag="l1_wo")
            p.v(lambda e, tw=tw, pbi=pbi: e.tensor_copy(
                out=ob1[:, 0:tw], in_=pb[pbi][:, 0:tw]),
                reads=[f"pb{pbi}"], writes=["l1_ob", "ln_xs0"], tag="l1_wo_ev")
            p.dma("gpsimd", lambda e, oc=oc, t0=t0, tw=tw: e.dma_start(
                out=xout[oc * 128:(oc + 1) * 128, t0:t0 + tw],
                in_=ob1[:, 0:tw], accum_op=ALU.add),
                reads=["l1_ob", "ln_xs0"], writes=[f"xout:{oc}"],
                stream=f"l1_xo{oc % 4}", tag="l1_wo_acc")


    L.p.wait_all("gpsimd", [f"xout:{a}" for a in range(NCH)], tag="l1_final")


def prep_l1(inputs):
    c8 = lambda v: np.ascontiguousarray(v.reshape(NCH, 128).T)
    return {
        "td": c8(np.asarray(inputs["L1_attn_time_decay"])),
        "tf": c8(np.asarray(inputs["L1_attn_time_first"])),
        "tmk": c8(np.asarray(inputs["L1_attn_time_mix_k"]).reshape(-1)),
        "tmv": c8(np.asarray(inputs["L1_attn_time_mix_v"]).reshape(-1)),
        "tmr": c8(np.asarray(inputs["L1_attn_time_mix_r"]).reshape(-1)),
        "wkt": np.ascontiguousarray(np.asarray(inputs["L1_attn_Wk"]).T),
        "wvt": np.ascontiguousarray(np.asarray(inputs["L1_attn_Wv"]).T),
        "wrt": np.ascontiguousarray(np.asarray(inputs["L1_attn_Wr"]).T),
        "wot": np.ascontiguousarray(np.asarray(inputs["L1_attn_Wo"]).T),
        "wt": np.ascontiguousarray(np.asarray(inputs["L1_ffn_W"]).T),
        "b": c8(np.asarray(inputs["L1_ffn_b"])),
        "vr": np.ascontiguousarray(np.asarray(inputs["L1_ffn_V_r"])),
        "ct": np.ascontiguousarray(np.asarray(inputs["L1_ffn_C"]).T),
    }


L1_DRAM_SPECS = [
    ("td", [128, NCH]), ("tf", [128, NCH]), ("tmk", [128, NCH]),
    ("tmv", [128, NCH]), ("tmr", [128, NCH]),
    ("wkt", [1024, 1024]), ("wvt", [1024, 1024]), ("wrt", [1024, 1024]),
    ("wot", [1024, 1024]), ("wt", [1024, 1024]), ("b", [128, NCH]),
    ("vr", [1024, 32]), ("ct", [32, 1024]),
]

LAYER_BUILDERS = {}
LAYER_DRAM_SPECS = {}





NSEG = 5


def _l2_urdhva_fm(L, p, prefix, wslot, wkey, src_t, src_key_fmt, dst,
                  dst_key_fmt, bias, bias_idx):
    T = L.T
    pb, wbuf = L.pb, L.wbuf
    tts = t_tiles(T)
    for b in range(NCH):
        for ti, (t0, tw) in enumerate(tts):
            pbi = (b * len(tts) + ti) % 2
            parts = [(b, wslot + b)]
            if b > 0:
                parts.append((b - 1, wslot + 8 + (b - 1)))
            if b < 7:
                parts.append((b + 1, wslot + 15 + b))
            for pi, (src_c, slot) in enumerate(parts):
                p.t(lambda e, src_c=src_c, slot=slot, t0=t0, tw=tw, pbi=pbi,
                    first=(pi == 0), last=(pi == len(parts) - 1): e.matmul(
                    out=pb[pbi][:, 0:tw],
                    lhsT=wbuf[:, slot * 128:(slot + 1) * 128],
                    rhs=src_t[:, src_c * T + t0:src_c * T + t0 + tw],
                    start=first, stop=last),
                    reads=[wkey, src_key_fmt.format(src_c)], writes=[f"pb{pbi}"],
                    tag=f"l2_{prefix}_mm")
            p.v(lambda e, b=b, t0=t0, tw=tw, pbi=pbi: e.tensor_scalar(
                out=dst[:, b * T + t0:b * T + t0 + tw], in0=pb[pbi][:, 0:tw],
                scalar1=bias[:, bias_idx * NCH + b:bias_idx * NCH + b + 1],
                scalar2=None, op0=ALU.add),
                reads=[f"pb{pbi}", "l2_bias"], writes=[dst_key_fmt.format(b)],
                tag=f"l2_{prefix}_ev")


def build_l2(L, aps, xin, xout):
    nc, p, T = L.nc, L.p, L.T
    pb = L.pb
    tts = t_tiles(T)
    p.dma("sync", lambda e: e.dma_start(out=xout, in_=xin),
          writes=[f"xout:{a}" for a in range(NCH)], stream="xout_init",
          tag="l2_xout_init")
    L.layernorm_streaming(xin)
    xn = L.xn

    wbuf = L.sbuf("l2_wbuf", [128, 88 * 128], BF16)
    L.wbuf = wbuf
    wstg2 = [L.sbuf(f"l2_wstg{i}", [128, 1024], F32) for i in range(2)]
    wstg = wstg2[0]
    bias = L.sbuf("l2_bias", [128, 6 * NCH], F32)
    for i, nm in enumerate(["qb", "kb", "vb", "ob", "fsc", "fbi"]):
        p.dma("sync", lambda e, i=i, nm=nm: e.dma_start(
            out=bias[:, i * NCH:(i + 1) * NCH], in_=aps[nm]),
            writes=["l2_bias"], stream="l2_bias", tag=f"l2_ldb_{nm}")

    def bcol(i):
        return lambda b: bias[:, i * NCH + b:i * NCH + b + 1]

    SGNK = tuple(f"l2_sgn:{a}" for a in range(NCH))

    def load_blocks(name, slot0, nblk, wkey, alias=()):
        for b8 in range(0, nblk, 8):
            wbx = (b8 // 8) % 2
            cnt = min(8, nblk - b8)
            p.dma("sync", lambda e, b8=b8, cnt=cnt, name=name, wbx=wbx: e.dma_start(
                out=wstg2[wbx][:, 0:cnt * 128].rearrange("p (n c) -> p n c", c=128),
                in_=aps[name].rearrange("(n p) c -> p n c", p=128)[:, b8:b8 + cnt, :]),
                writes=[f"l2_wstg{wbx}"], tag=f"l2_ldw_{name}{b8}")
            p.v(lambda e, b8=b8, cnt=cnt, slot0=slot0, wbx=wbx: e.tensor_copy(
                out=wbuf[:, (slot0 + b8) * 128:(slot0 + b8 + cnt) * 128],
                in_=wstg2[wbx][:, 0:cnt * 128]),
                reads=[f"l2_wstg{wbx}"], writes=[wkey] + list(alias) + list(SGNK),
                tag=f"l2_cvw_{name}{b8}")

    sgn = wbuf[:, 0:NCH * 1024]
    for a in range(NCH):
        wbx = a % 2
        p.dma("sync", lambda e, a=a, wbx=wbx: e.dma_start(
            out=wstg2[wbx][:], in_=aps["fwt"][a * 128:(a + 1) * 128, :]),
            writes=[f"l2_wstg{wbx}"], tag=f"l2_ldfw{a}")
        p.v(lambda e, a=a, wbx=wbx: e.tensor_scalar(
            out=sgn[:, a * 1024:(a + 1) * 1024], in0=wstg2[wbx][:],
            scalar1=0.0, scalar2=None, op0=ALU.is_ge),
            reads=[f"l2_wstg{wbx}"], writes=[f"l2_sgn:{a}"], tag=f"l2_sg0{a}")
        p.v(lambda e, a=a: e.tensor_scalar(
            out=sgn[:, a * 1024:(a + 1) * 1024],
            in0=sgn[:, a * 1024:(a + 1) * 1024],
            scalar1=2.0, scalar2=-1.0, op0=ALU.mult, op1=ALU.add),
            reads=[f"l2_sgn:{a}"], writes=[f"l2_sgn:{a}"], tag=f"l2_sg1{a}")
    ob = L.ln_xs
    L.cap(100.0)
    L.cap(-100.0)
    for oc in range(NCH):
        for ti, (t0, tw) in enumerate(tts):
            pbi = (oc * len(tts) + ti) % 2
            for a in range(NCH):
                p.t(lambda e, a=a, oc=oc, t0=t0, tw=tw, pbi=pbi: e.matmul(
                    out=pb[pbi][:, 0:tw],
                    lhsT=sgn[:, a * 1024 + oc * 128:a * 1024 + (oc + 1) * 128],
                    rhs=xn[:, a * T + t0:a * T + t0 + tw],
                    start=(a == 0), stop=(a == NCH - 1)),
                    reads=[f"l2_sgn:{a}", f"xn:{a}"], writes=[f"pb{pbi}"],
                    tag="l2_ffn_mm")
            bo = (oc * len(tts) + ti) % 2
            p.v(lambda e, oc=oc, tw=tw, pbi=pbi, bo=bo: e.tensor_scalar(
                out=ob[bo][:, 0:tw], in0=pb[pbi][:, 0:tw],
                scalar1=bias[:, 4 * NCH + oc:4 * NCH + oc + 1],
                scalar2=bias[:, 5 * NCH + oc:5 * NCH + oc + 1],
                op0=ALU.mult, op1=ALU.add),
                reads=[f"pb{pbi}", "l2_bias"], writes=[f"l2_ob{bo}", f"ln_xs{bo}"],
                tag="l2_ffn_sb")
            p.v(lambda e, tw=tw, bo=bo: e.tensor_scalar(
                out=ob[bo][:, 0:tw], in0=ob[bo][:, 0:tw],
                scalar1=100.0, scalar2=-100.0, op0=ALU.min, op1=ALU.max),
                reads=[f"l2_ob{bo}", "fconst"],
                writes=[f"l2_ob{bo}", f"ln_xs{bo}"], tag="l2_ffn_clip")
            p.dma("gpsimd", lambda e, oc=oc, t0=t0, tw=tw, bo=bo: e.dma_start(
                out=xout[oc * 128:(oc + 1) * 128, t0:t0 + tw],
                in_=ob[bo][:, 0:tw], accum_op=ALU.add),
                reads=[f"l2_ob{bo}", f"ln_xs{bo}"], writes=[f"xout:{oc}"],
                stream=f"l2_xo{oc % 4}", tag="l2_ffn_acc")

    kf = L.sbuf("l2_kf", [128, NCH * T], BF16)
    n_tc = T // 128
    vtok = L.sbuf("l2_vt", [128, n_tc * 1024], BF16)
    yat = L.sbuf("l2_ya", [128, NCH * T], BF16)
    qt = L.sbuf("l2_qt", [128, T], BF16)
    for nm, s0 in (("kwv", 0), ("kwr", 8), ("kwl", 15)):
        load_blocks(nm, s0, 8 if nm.endswith("wv") else 7, "wb:k")
    for nm, s0 in (("vwv", 22), ("vwr", 30), ("vwl", 37)):
        load_blocks(nm, s0, 8 if nm.endswith("wv") else 7, "wb:v")
    for nm, s0 in (("qwv", 44), ("qwr", 52), ("qwl", 59)):
        load_blocks(nm, s0, 8 if nm.endswith("wv") else 7, "wb:q")
    vbrow = L.sbuf("l2_vbrow", [1, 1024], BF16)
    vbrow32 = L.sbuf("l2_vbrow32", [1, 1024], F32)
    p.dma("sync", lambda e: e.dma_start(out=vbrow32[:], in_=aps["vbrow"]),
          writes=["l2_vbrow32"], tag="l2_ldvbr")
    p.v(lambda e: e.tensor_copy(out=vbrow[:], in_=vbrow32[:]),
        reads=["l2_vbrow32"], writes=["l2_vbrow"], tag="l2_cvvbr")

    _l2_urdhva_fm(L, p, "k", 0, "wb:k", xn, "xn:{}", kf, "l2_kf:{}", bias, 1)
    for tc in range(n_tc):
        for bg in range(2):
            pbi = (tc * 2 + bg) % 2
            for bi in range(4):
                b = bg * 4 + bi
                parts = [(b, 22 + b)]
                if b > 0:
                    parts.append((b - 1, 22 + 8 + (b - 1)))
                if b < 7:
                    parts.append((b + 1, 22 + 15 + b))
                for pi, (src_c, slot) in enumerate(parts):
                    p.t(lambda e, src_c=src_c, slot=slot, tc=tc, bi=bi, pbi=pbi,
                        first=(pi == 0): e.matmul(
                        out=pb[pbi][:, bi * 128:(bi + 1) * 128],
                        lhsT=xn[:, src_c * T + tc * 128:src_c * T + (tc + 1) * 128],
                        rhs=wbuf[:, slot * 128:(slot + 1) * 128],
                        start=first, stop=False),
                        reads=["wb:v", f"xn:{src_c}"], writes=[f"pb{pbi}"],
                        tag="l2_vt_mm")
                p.t(lambda e, b=b, bi=bi, pbi=pbi: e.matmul(
                    out=pb[pbi][:, bi * 128:(bi + 1) * 128],
                    lhsT=L.ones_row_bf[0:1, 0:128],
                    rhs=vbrow[0:1, b * 128:(b + 1) * 128],
                    start=False, stop=True),
                    reads=["l2_vbrow", "ones_row_bf"], writes=[f"pb{pbi}"],
                    tag="l2_vt_b")
            p.v(lambda e, tc=tc, bg=bg, pbi=pbi: e.tensor_copy(
                out=vtok[:, tc * 1024 + bg * 512:tc * 1024 + (bg + 1) * 512],
                in_=pb[pbi][:]),
                reads=[f"pb{pbi}"], writes=[f"l2_vt:{tc}"], tag="l2_vt_ev")

    msk = [L.sbuf(f"l2_msk{i}", [128, NSEG * 128], F32) for i in range(2)]
    Sx = L.sbuf("l2_S", [128, NSEG * 128], F32)
    Af = Sx
    at_sb = L.sbuf("l2_AT", [128, NSEG * 128], BF16)
    nm_t = L.sbuf("l2_nm", [128, 4], F32)
    idf = L.sbuf("l2_idf", [128, 128], F32)
    p.dma("sync", lambda e: e.dma_start(out=idf[:], in_=aps["idf"]),
          writes=["l2_idf"], tag="l2_ldid")
    L.cap(0.125)

    for ec in range(NCH):
        for ti, (t0, tw) in enumerate(tts):
            pbi = 4
            parts = [(ec, 44 + ec)]
            if ec > 0:
                parts.append((ec - 1, 44 + 8 + (ec - 1)))
            if ec < 7:
                parts.append((ec + 1, 44 + 15 + ec))
            for pi, (src_c, slot) in enumerate(parts):
                p.t(lambda e, src_c=src_c, slot=slot, t0=t0, tw=tw, pbi=pbi,
                    first=(pi == 0), last=(pi == len(parts) - 1): e.matmul(
                    out=pb[pbi][:, 0:tw],
                    lhsT=wbuf[:, slot * 128:(slot + 1) * 128],
                    rhs=xn[:, src_c * T + t0:src_c * T + t0 + tw],
                    start=first, stop=last),
                    reads=["wb:q", f"xn:{src_c}"], writes=[f"pb{pbi}"],
                    tag="l2_q_mm")
            p.v(lambda e, ec=ec, t0=t0, tw=tw, pbi=pbi: e.tensor_scalar(
                out=qt[:, t0:t0 + tw], in0=pb[pbi][:, 0:tw],
                scalar1=bias[:, ec:ec + 1], scalar2=None, op0=ALU.add),
                reads=[f"pb{pbi}", "l2_bias"], writes=["l2_qt"], tag="l2_q_ev")
        nqc = T // 128
        for hh in range(2):
            h = 2 * ec + hh
            r0 = hh * 64
            for i in range(nqc):
                segs = [max(j, 0) for j in (i - 8, i - 4, i - 2, i - 1, i)]
                mb = i % 2
                p.dma("sync", lambda e, i=i, mb=mb: e.dma_start(
                    out=msk[mb][:], in_=aps["mask"][i * 128:(i + 1) * 128, :]),
                    writes=[f"l2_msk{mb}"], tag="l2_ldmsk")
                for s, j in enumerate(segs):
                    tgt = (0, s * 128) if s < 4 else (1, 0)
                    p.t(lambda e, j=j, i=i, r0=r0, ec=ec, tgt=tgt: e.matmul(
                        out=pb[tgt[0]][:, tgt[1]:tgt[1] + 128],
                        lhsT=qt[r0:r0 + 64, i * 128:(i + 1) * 128],
                        rhs=kf[r0:r0 + 64, ec * T + j * 128:ec * T + (j + 1) * 128],
                        start=True, stop=True),
                        reads=["l2_qt", f"l2_kf:{ec}"], writes=[f"pb{tgt[0]}"],
                        tag="l2_sc_mm")
                p.v(lambda e, mb=mb: e.tensor_tensor(
                    out=Sx[:, 0:512], in0=pb[0][:, 0:512], in1=msk[mb][:, 0:512],
                    op=ALU.add),
                    reads=["pb0", f"l2_msk{mb}"], writes=["l2_S"], tag="l2_sc_m0")
                p.v(lambda e, mb=mb: e.tensor_tensor(
                    out=Sx[:, 512:640], in0=pb[1][:, 0:128],
                    in1=msk[mb][:, 512:640], op=ALU.add),
                    reads=["pb1", f"l2_msk{mb}"], writes=["l2_S"], tag="l2_sc_m1")
                p.v(lambda e: e.reduce_max(out=nm_t[:, 0:1], in_=Sx[:],
                                           axis=AX.X, negate=True),
                    reads=["l2_S"], writes=["l2_nm"], tag="l2_rmax")
                p.v(lambda e: e.tensor_scalar(
                    out=nm_t[:, 1:2], in0=nm_t[:, 0:1], scalar1=0.125,
                    scalar2=None, op0=ALU.mult),
                    reads=["l2_nm", "fconst"], writes=["l2_nm"], tag="l2_nm8")
                p.s(lambda e: e.activation(
                    out=Sx[:], in_=Sx[:], func=AF.Exp,
                    bias=nm_t[:, 1:2], scale=0.125, accum_out=nm_t[:, 2:3]),
                    reads=["l2_S", "l2_nm", "fconst"], writes=["l2_S", "l2_nm"],
                    tag="l2_exp")
                p.v(lambda e: e.reciprocal(out=nm_t[:, 3:4], in_=nm_t[:, 2:3]),
                    reads=["l2_nm"], writes=["l2_nm"], tag="l2_rz")
                p.v(lambda e: e.tensor_scalar(
                    out=Af[:], in0=Sx[:], scalar1=nm_t[:, 3:4], scalar2=None,
                    op0=ALU.mult),
                    reads=["l2_S", "l2_nm"], writes=["l2_Af", "l2_S"],
                    tag="l2_an")
                for s in range(NSEG):
                    p.t(lambda e, s=s: e.transpose(
                        out=pb[2 + (s % 2)][:, (s // 2 % 2) * 128:(s // 2 % 2) * 128 + 128],
                        in_=Af[:, s * 128:(s + 1) * 128], identity=idf[:]),
                        reads=["l2_Af", "l2_idf"], writes=[f"pb{2 + (s % 2)}"],
                        tag="l2_tr")
                    p.v(lambda e, s=s: e.tensor_copy(
                        out=at_sb[:, s * 128:(s + 1) * 128],
                        in_=pb[2 + (s % 2)][:, (s // 2 % 2) * 128:(s // 2 % 2) * 128 + 128]),
                        reads=[f"pb{2 + (s % 2)}"], writes=["l2_AT"], tag="l2_trev")
                for s, j in enumerate(segs):
                    p.t(lambda e, s=s, j=j, h=h, first=(s == 0),
                        last=(s == NSEG - 1): e.matmul(
                        out=pb[5][0:64, 0:128],
                        lhsT=vtok[:, j * 1024 + h * 64:j * 1024 + h * 64 + 64],
                        rhs=at_sb[:, s * 128:(s + 1) * 128],
                        start=first, stop=last),
                        reads=[f"l2_vt:{j}", "l2_AT"], writes=["pb5"],
                        tag="l2_av")
                p.s(lambda e, r0=r0, ec=ec, i=i: e.copy(
                    out=yat[r0:r0 + 64, ec * T + i * 128:ec * T + (i + 1) * 128],
                    in_=pb[5][0:64, 0:128]),
                    reads=["pb5"], writes=[f"l2_ya:{ec}"], tag="l2_yev")

    for nm, s0 in (("owv", 0), ("owr", 8), ("owl", 15)):
        load_blocks(nm, s0, 8 if nm.endswith("wv") else 7, "wb:o", alias=("wb:k",))
    for b in range(NCH):
        for ti, (t0, tw) in enumerate(tts):
            pbi = (b * len(tts) + ti) % 2
            parts = [(b, b)]
            if b > 0:
                parts.append((b - 1, 8 + (b - 1)))
            if b < 7:
                parts.append((b + 1, 15 + b))
            for pi, (src_c, slot) in enumerate(parts):
                p.t(lambda e, src_c=src_c, slot=slot, t0=t0, tw=tw, pbi=pbi,
                    first=(pi == 0), last=(pi == len(parts) - 1): e.matmul(
                    out=pb[pbi][:, 0:tw],
                    lhsT=wbuf[:, slot * 128:(slot + 1) * 128],
                    rhs=yat[:, src_c * T + t0:src_c * T + t0 + tw],
                    start=first, stop=last),
                    reads=["wb:o", f"l2_ya:{src_c}"], writes=[f"pb{pbi}"],
                    tag="l2_o_mm")
            bo = (b * len(tts) + ti) % 2
            p.v(lambda e, b=b, tw=tw, pbi=pbi, bo=bo: e.tensor_scalar(
                out=ob[bo][:, 0:tw], in0=pb[pbi][:, 0:tw],
                scalar1=bias[:, 3 * NCH + b:3 * NCH + b + 1], scalar2=None,
                op0=ALU.add),
                reads=[f"pb{pbi}", "l2_bias"],
                writes=[f"l2_ob{bo}", f"ln_xs{bo}"], tag="l2_o_ev")
            p.dma("gpsimd", lambda e, b=b, t0=t0, tw=tw, bo=bo: e.dma_start(
                out=xout[b * 128:(b + 1) * 128, t0:t0 + tw],
                in_=ob[bo][:, 0:tw], accum_op=ALU.add),
                reads=[f"l2_ob{bo}", f"ln_xs{bo}"], writes=[f"xout:{b}"],
                stream=f"l2_xo{b % 4}", tag="l2_o_acc")
    L.p.wait_all("gpsimd", [f"xout:{a}" for a in range(NCH)], tag="l2_final")


def _l2_mask_host(T):
    allowed = set(range(5)) | {8 << m for m in range(20) if (8 << m) < T}
    m = np.full((T, NSEG * 128), -1e30, np.float32)
    for i in range(T // 128):
        js = [i - 8, i - 4, i - 2, i - 1, i]
        for s, j in enumerate(js):
            if j < 0:
                continue
            for r in range(128):
                q = i * 128 + r
                for c in range(128):
                    k = j * 128 + c
                    d = q - k
                    if d >= 0 and d in allowed:
                        m[q, s * 128 + c] = 0.0
    return m


_L2_MASK_CACHE = {}


def prep_l2(inputs):
    c8 = lambda v: np.ascontiguousarray(np.asarray(v).reshape(NCH, 128).T)
    d = {}
    for pnm, tag in (("q", "q"), ("k", "k"), ("v", "v"), ("o", "o")):
        d[f"{tag}wv"] = np.asarray(inputs[f"L2_attn_{pnm}_Wv"]).reshape(8 * 128, 128)
        d[f"{tag}wr"] = np.asarray(inputs[f"L2_attn_{pnm}_Wr"]).reshape(7 * 128, 128)
        d[f"{tag}wl"] = np.asarray(inputs[f"L2_attn_{pnm}_Wl"]).reshape(7 * 128, 128)
        d[f"{tag}b"] = c8(inputs[f"L2_attn_{pnm}_b"])
    d["vbrow"] = np.asarray(inputs["L2_attn_v_b"]).reshape(1, 1024)
    d["fwt"] = np.ascontiguousarray(np.asarray(inputs["L2_ffn_weights"]).T)
    d["fsc"] = c8(inputs["L2_ffn_scale"])
    d["fbi"] = c8(inputs["L2_ffn_bias"])
    d["idf"] = np.eye(128, dtype=np.float32)
    return d


L2_DRAM_SPECS = (
    [(f"{t}wv", [8 * 128, 128]) for t in "qkvo"]
    + [(f"{t}wr", [7 * 128, 128]) for t in "qkvo"]
    + [(f"{t}wl", [7 * 128, 128]) for t in "qkvo"]
    + [(f"{t}b", [128, NCH]) for t in "qkvo"]
    + [("vbrow", [1, 1024]), ("fwt", [1024, 1024]), ("fsc", [128, NCH]),
       ("fbi", [128, NCH]), ("idf", [128, 128]), ("mask", None)]
)



RMS_EPS = 1.1920928955078125e-07


def build_l3(L, aps, xin, xout):
    nc, p, T = L.nc, L.p, L.T
    pb, pbt = L.pb, L.pbt
    tts = t_tiles(T)
    n_tc = T // 128
    p.dma("sync", lambda e: e.dma_start(out=xout, in_=xin),
          writes=[f"xout:{a}" for a in range(NCH)], stream="xout_init",
          tag="l3_xout_init")
    L.layernorm_streaming(xin)
    xn = L.xn

    wstg2 = [L.sbuf(f"l3_wstg{i}", [128, 1024], F32) for i in range(2)]
    wmb2 = [L.sbuf(f"l3_wmb{i}", [128, 1024], BF16) for i in range(2)]
    wstg = wstg2[0]
    wmb = wmb2[0]

    def load_slice(name, c0, cw, tag, b=0):
        p.dma("sync", lambda e, b=b: e.dma_start(
            out=wstg2[b][:, 0:NCH * cw].rearrange("p (a f) -> p a f", a=NCH),
            in_=aps[name].rearrange("(a p) f -> p a f", p=128)[:, :, c0:c0 + cw]),
            writes=[f"l3_wstg{b}"], tag=f"l3_lds_{tag}")
        p.v(lambda e, b=b: e.tensor_copy(out=wmb2[b][:, 0:NCH * cw],
                                         in_=wstg2[b][:, 0:NCH * cw]),
            reads=[f"l3_wstg{b}"], writes=[f"l3_wmb{b}"], tag=f"l3_cvs_{tag}")

    qf = L.sbuf("l3_qf", [128, NCH * T], BF16)
    kf = L.sbuf("l3_kf", [128, NCH * T], BF16)
    vtok = L.sbuf("l3_vt", [128, n_tc * 1024], BF16)
    yat = L.xn

    wvb = (kf[:, 0:NCH * 512] if T >= 512
           else L.sbuf("l3_wvbS", [128, NCH * 512], BF16))
    for eh in range(2):
        for a in range(NCH):
            b = a % 2
            p.dma("sync", lambda e, a=a, eh=eh, b=b: e.dma_start(
                out=wstg2[b][:, 0:512],
                in_=aps["wvt"][a * 128:(a + 1) * 128, eh * 512:(eh + 1) * 512]),
                writes=[f"l3_wstg{b}"], tag="l3_ldwv")
            p.v(lambda e, a=a, b=b: e.tensor_copy(
                out=wvb[:, a * 512:(a + 1) * 512], in_=wstg2[b][:, 0:512]),
                reads=[f"l3_wstg{b}"], writes=["l3_wvb"], tag="l3_cvwv")
        for tc in range(n_tc):
            pbi = tc % 2
            for a in range(NCH):
                p.t(lambda e, a=a, tc=tc, pbi=pbi: e.matmul(
                    out=pb[pbi][:],
                    lhsT=xn[:, a * T + tc * 128:a * T + (tc + 1) * 128],
                    rhs=wvb[:, a * 512:(a + 1) * 512],
                    start=(a == 0), stop=(a == NCH - 1)),
                    reads=["l3_wvb", f"xn:{a}"], writes=[f"pb{pbi}"], tag="l3_vt_mm")
            p.v(lambda e, tc=tc, eh=eh, pbi=pbi: e.tensor_copy(
                out=vtok[:, tc * 1024 + eh * 512:tc * 1024 + (eh + 1) * 512],
                in_=pb[pbi][:]),
                reads=[f"pb{pbi}"], writes=[f"l3_vt:{tc}"], tag="l3_vt_ev")

    for wname, dst, dkey in (("wqt", qf, "l3_qf"), ("wkt", kf, "l3_kf:{}")):
        for oc in range(NCH):
            wb = oc % 2
            load_slice(wname, oc * 128, 128, f"{wname}{oc}", wb)
            for ti, (t0, tw) in enumerate(tts):
                pbi = ti % 2
                for a in range(NCH):
                    p.t(lambda e, a=a, t0=t0, tw=tw, pbi=pbi, wb=wb: e.matmul(
                        out=pb[pbi][:, 0:tw],
                        lhsT=wmb2[wb][:, a * 128:(a + 1) * 128],
                        rhs=xn[:, a * T + t0:a * T + t0 + tw],
                        start=(a == 0), stop=(a == NCH - 1)),
                        reads=[f"l3_wmb{wb}", f"xn:{a}"], writes=[f"pb{pbi}"],
                        tag=f"l3_{wname}_mm")
                p.v(lambda e, oc=oc, t0=t0, tw=tw, pbi=pbi, dst=dst,
                    dkey=dkey: e.tensor_copy(
                    out=dst[:, oc * T + t0:oc * T + t0 + tw], in_=pb[pbi][:, 0:tw]),
                    reads=[f"pb{pbi}"],
                    writes=([dkey.format(oc) if "{" in dkey else dkey]
                            + (["l3_wvb"] if wname == "wkt" and oc < 2 else [])),
                    tag=f"l3_{wname}_ev")
    prm = L.sbuf("l3_prm", [128, 4], F32)
    bm2c = L.sbuf("l3_bm2", [16, 1], F32)
    p.dma("sync", lambda e: e.dma_start(out=prm[:, 0:2], in_=aps["bm1"]),
          writes=["l3_bm1"], tag="l3_ldbm1")
    p.dma("sync", lambda e: e.dma_start(out=bm2c[:], in_=aps["bm2"]),
          writes=["l3_bm2"], tag="l3_ldbm2")
    wm2 = L.sbuf("l3_wm2", [128, 2 * 16], BF16)
    p.dma("sync", lambda e: e.dma_start(
        out=wstg[:, 0:32].rearrange("p (a r) -> p a r", a=2),
        in_=aps["wm2t"].rearrange("(a p) r -> p a r", p=128)),
        writes=["l3_wstg0"], tag="l3_ldm2")
    p.v(lambda e: e.tensor_copy(out=wm2[:], in_=wstg[:, 0:32]),
        reads=["l3_wstg0"], writes=["l3_wm2"], tag="l3_cvm2")
    mht = L.sbuf("l3_mh", [128, 2 * 512], BF16)
    lam = L.sbuf("l3_lamt", [16, 512], F32)
    lam_hbm = nc.dram_tensor("l3_lam_hbm", [16, T], F32, kind="Internal").ap()
    Bm = L.sbuf("l3_B", [16, T], F32)
    mtX = L.sbuf("l3_mt", [128, 512], F32)
    s1X = L.sbuf("l3_s1", [128, 512], F32)
    s2X = L.sbuf("l3_s2", [128, 512], F32)
    mt = mtX[0:16, :]
    s1 = s1X[0:16, :]
    s2 = s2X[0:16, :]
    dcar = L.sbuf("l3_dcar", [16, 1], F32)
    dcar2 = L.sbuf("l3_dcar2", [16, 1], F32)
    p.v(lambda e: e.memset(dcar[:], 0.0), writes=["l3_dcar"], tag="l3_dc0")
    for ti, (t0, tw) in enumerate(tts):
        for hc in range(2):
            load_slice("wm1t", hc * 128, 128, f"wm1_{ti}_{hc}", hc)
            pbi = hc % 2
            for a in range(NCH):
                p.t(lambda e, a=a, t0=t0, tw=tw, pbi=pbi, hc=hc: e.matmul(
                    out=pb[pbi][:, 0:tw],
                    lhsT=wmb2[hc][:, a * 128:(a + 1) * 128],
                    rhs=xn[:, a * T + t0:a * T + t0 + tw],
                    start=(a == 0), stop=(a == NCH - 1)),
                    reads=[f"l3_wmb{hc}", f"xn:{a}"], writes=[f"pb{pbi}"],
                    tag="l3_mh_mm")
            p.s(lambda e, hc=hc, tw=tw, pbi=pbi: e.activation(
                out=mht[:, hc * 512:hc * 512 + tw], in_=pb[pbi][:, 0:tw],
                func=AF.Relu, bias=prm[:, hc:hc + 1], scale=1.0),
                reads=[f"pb{pbi}", "l3_bm1"], writes=["l3_mh"], tag="l3_mh_ev")
        pbi = ti % 2
        for hc in range(2):
            p.t(lambda e, hc=hc, tw=tw, pbi=pbi: e.matmul(
                out=pb[pbi][0:16, 0:tw],
                lhsT=wm2[:, hc * 16:(hc + 1) * 16],
                rhs=mht[:, hc * 512:hc * 512 + tw],
                start=(hc == 0), stop=(hc == 1)),
                reads=["l3_wm2", "l3_mh"], writes=[f"pb{pbi}"], tag="l3_m_mm")
        p.v(lambda e, t0=t0, tw=tw, pbi=pbi: e.tensor_scalar(
            out=mt[:, 0:tw], in0=pb[pbi][0:16, 0:tw], scalar1=bm2c[:],
            scalar2=None, op0=ALU.add),
            reads=[f"pb{pbi}", "l3_bm2"], writes=["l3_mt"], tag="l3_m_ev")
        p.s(lambda e, tw=tw: e.activation(out=s1[:, 0:tw], in_=mt[:, 0:tw],
                                          func=AF.Abs, bias=0.0, scale=1.0),
            reads=["l3_mt"], writes=["l3_s1"], tag="l3_abs")
        p.s(lambda e, tw=tw: e.activation(out=s1[:, 0:tw], in_=s1[:, 0:tw],
                                          func=AF.Exp, bias=0.0, scale=-1.0),
            reads=["l3_s1"], writes=["l3_s1"], tag="l3_expn")
        p.s(lambda e, tw=tw: e.activation(out=s1[:, 0:tw], in_=s1[:, 0:tw],
                                          func=AF.Ln, bias=1.0, scale=1.0),
            reads=["l3_s1"], writes=["l3_s1"], tag="l3_ln1p")
        p.s(lambda e, tw=tw: e.activation(out=s2[:, 0:tw], in_=mt[:, 0:tw],
                                          func=AF.Relu, bias=0.0, scale=1.0),
            reads=["l3_mt"], writes=["l3_s2"], tag="l3_rp")
        p.v(lambda e, tw=tw: e.tensor_tensor(out=s2[:, 0:tw], in0=s2[:, 0:tw],
                                             in1=s1[:, 0:tw], op=ALU.add),
            reads=["l3_s2", "l3_s1"], writes=["l3_s2"], tag="l3_lnd0")
        p.v(lambda e, tw=tw: e.tensor_scalar(out=s2[:, 0:tw], in0=s2[:, 0:tw],
                                             scalar1=-1.0, scalar2=None,
                                             op0=ALU.mult),
            reads=["l3_s2"], writes=["l3_s2"], tag="l3_lnd")
        p.v(lambda e, tw=tw: e.tensor_tensor_scan(
            out=lam[:, 0:tw], data0=s2[:, 0:tw], data1=mt[:, 0:tw],
            initial=dcar[:], op0=ALU.add, op1=ALU.bypass),
            reads=["l3_s2", "l3_mt", "l3_dcar"], writes=["l3_lam"], tag="l3_cums")
        p.s(lambda e, tw=tw: e.activation(out=mt[:, 0:tw], in_=mt[:, 0:tw],
                                          func=AF.Relu, bias=0.0, scale=-1.0),
            reads=["l3_mt"], writes=["l3_mt"], tag="l3_rn")
        p.v(lambda e, tw=tw: e.tensor_tensor(out=s1[:, 0:tw], in0=mt[:, 0:tw],
                                             in1=s1[:, 0:tw], op=ALU.add),
            reads=["l3_mt", "l3_s1"], writes=["l3_s1"], tag="l3_lng0")
        p.v(lambda e, t0=t0, tw=tw: e.scalar_tensor_tensor(
            out=Bm[:, t0:t0 + tw], in0=s1[:, 0:tw], scalar=-1.0,
            in1=lam[:, 0:tw], op0=ALU.mult, op1=ALU.subtract),
            reads=["l3_s1", "l3_lam"], writes=["l3_B"], tag="l3_Bm")
        p.v(lambda e, tw=tw: e.tensor_copy(out=dcar2[:], in_=lam[:, tw - 1:tw]),
            reads=["l3_lam"], writes=["l3_dcar2"], tag="l3_dc2")
        for c in reversed(range(tw // 128)):
            sc = (dcar[:] if c == 0 else lam[:, c * 128 - 1:c * 128])
            sck = "l3_dcar" if c == 0 else "l3_lam"
            p.v(lambda e, c=c, t0=t0, sc=sc: e.tensor_scalar(
                out=Bm[:, t0 + c * 128:t0 + (c + 1) * 128],
                in0=Bm[:, t0 + c * 128:t0 + (c + 1) * 128],
                scalar1=sc, scalar2=None, op0=ALU.add),
                reads=["l3_B", sck], writes=["l3_B"], tag="l3_Brel")
            p.v(lambda e, c=c, sc=sc: e.tensor_scalar(
                out=lam[:, c * 128:(c + 1) * 128],
                in0=lam[:, c * 128:(c + 1) * 128],
                scalar1=sc, scalar2=None, op0=ALU.subtract),
                reads=["l3_lam", sck], writes=["l3_lam"], tag="l3_Lrel")
        p.dma("sync", lambda e, t0=t0, tw=tw: e.dma_start(
            out=lam_hbm[:, t0:t0 + tw], in_=lam[:, 0:tw]),
            reads=["l3_lam"], writes=["l3_lamhbm"], stream="l3_lamhbm",
            tag="l3_lam_st")
        p.v(lambda e: e.tensor_copy(out=dcar[:], in_=dcar2[:]),
            reads=["l3_dcar2"], writes=["l3_dcar"], tag="l3_dcaru")
    btok = L.sbuf("l3_btok", [128, n_tc * 16], F32)
    idf = L.sbuf("l3_idf", [128, 128], F32)
    idb = L.sbuf("l3_idb", [128, 128], BF16)
    trim = L.sbuf("l3_trim", [128, 128], F32)
    p.dma("sync", lambda e: e.dma_start(out=idf[:], in_=aps["idf"]),
          writes=["l3_idf"], tag="l3_ldid")
    p.dma("sync", lambda e: e.dma_start(out=trim[:], in_=aps["trimask"]),
          writes=["l3_trim"], tag="l3_ldtm")
    p.v(lambda e: e.tensor_copy(out=idb[:], in_=idf[:]), reads=["l3_idf"],
        writes=["l3_idb"], tag="l3_cvid")
    for tc in range(n_tc):
        pbi = tc % 2
        p.t(lambda e, tc=tc, pbi=pbi: e.transpose(
            out=pb[pbi][0:128, 0:16], in_=Bm[0:16, tc * 128:(tc + 1) * 128],
            identity=idf[0:16, 0:16]),
            reads=["l3_B", "l3_idf"], writes=[f"pb{pbi}"], tag="l3_btr")
        p.v(lambda e, tc=tc, pbi=pbi: e.tensor_copy(
            out=btok[:, tc * 16:(tc + 1) * 16], in_=pb[pbi][0:128, 0:16]),
            reads=[f"pb{pbi}"], writes=["l3_btok"], tag="l3_btr_ev")

    lrow = L.sbuf("l3_lrow", [1, T], F32)
    dt_sb = L.sbuf("l3_dt", [128, 128], F32)
    dc_sb = L.sbuf("l3_dc", [128, 128], F32)
    atw = L.sbuf("l3_atw", [128, 128], BF16)
    acr = L.sbuf("l3_acr", [64, 128], BF16)
    ktr = L.sbuf("l3_ktr", [128, 64], BF16)
    ks = L.sbuf("l3_ks", [128, 64], BF16)
    stf = L.sbuf("l3_stf", [64, 16 * 64], F32)
    stb = L.sbuf("l3_stb", [64, 16 * 64], BF16)
    p.v(lambda e: e.memset(stf[:], 0.0), writes=["l3_stf"], tag="l3_st0")
    p.v(lambda e: e.memset(stb[:], 0.0), writes=["l3_stb"], tag="l3_stb0")
    for h in range(16):
        ec, hh = h // 2, h % 2
        r0 = hh * 64
        p.dma("sync", lambda e, h=h: e.dma_start(
            out=lrow[:], in_=None), reads=["l3_lam"], writes=["l3_lrow"],
            stream="l3_lrow", tag="noemit") if False else None
        p.dma("sync", lambda e, h=h: e.dma_start(
            out=lrow[0:1, :], in_=lam_hbm[h:h + 1, :]),
            reads=["l3_lamhbm"], writes=["l3_lrow"], stream="l3_lrow",
            tag="l3_lrowdma")
        sth = stf[:, h * 64:(h + 1) * 64]
        stbh = stb[:, h * 64:(h + 1) * 64]
        for tc in range(n_tc):
            p.t(lambda e, tc=tc: e.matmul(
                out=pb[2][:, 0:128], lhsT=L.ones_row[0:1, 0:128],
                rhs=lrow[0:1, tc * 128:(tc + 1) * 128], start=True, stop=True),
                reads=["l3_lrow", "ones_row"], writes=["pb2"], tag="l3_bcA")
            p.t(lambda e, tc=tc: e.matmul(
                out=pb[3][:, 0:128], lhsT=L.ones_row[0:1, 0:128],
                rhs=lrow[0:1, tc * 128:(tc + 1) * 128], start=True, stop=False),
                reads=["l3_lrow", "ones_row"], writes=["pb3"], tag="l3_bcB0")
            p.t(lambda e: e.matmul(
                out=pb[3][:, 0:128], lhsT=idf[:], rhs=trim[:],
                start=False, stop=True),
                reads=["l3_trim", "l3_idf"], writes=["pb3"], tag="l3_bcB1")
            p.s(lambda e, tc=tc, h=h: e.activation(
                out=dt_sb[:], in_=pb[3][:, 0:128], func=AF.Exp,
                bias=btok[:, tc * 16 + h:tc * 16 + h + 1], scale=1.0),
                reads=["pb3", "l3_btok"], writes=["l3_dt"], tag="l3_dt")
            p.s(lambda e: e.activation(
                out=dc_sb[:], in_=pb[2][:, 0:128], func=AF.Exp,
                bias=0.0, scale=1.0),
                reads=["pb2"], writes=["l3_dc"], tag="l3_dcx")
            p.t(lambda e, ec=ec, r0=r0, tc=tc: e.matmul(
                out=pb[4][:, 0:128],
                lhsT=kf[r0:r0 + 64, ec * T + tc * 128:ec * T + (tc + 1) * 128],
                rhs=qf[r0:r0 + 64, ec * T + tc * 128:ec * T + (tc + 1) * 128],
                start=True, stop=True),
                reads=[f"l3_kf:{ec}", "l3_qf"], writes=["pb4"], tag="l3_sc")
            p.v(lambda e: e.tensor_tensor(
                out=atw[:], in0=pb[4][:, 0:128], in1=dt_sb[:], op=ALU.mult),
                reads=["pb4", "l3_dt"], writes=["l3_atw"], tag="l3_atw")
            p.v(lambda e, ec=ec, r0=r0, tc=tc: e.tensor_tensor(
                out=acr[:],
                in0=qf[r0:r0 + 64, ec * T + tc * 128:ec * T + (tc + 1) * 128],
                in1=dc_sb[r0:r0 + 64, :], op=ALU.mult),
                reads=["l3_qf", "l3_dc"], writes=["l3_acr"], tag="l3_acr")
            p.t(lambda e, tc=tc, h=h: e.matmul(
                out=pb[5][0:64, 0:128],
                lhsT=vtok[:, tc * 1024 + h * 64:tc * 1024 + h * 64 + 64],
                rhs=atw[:], start=True, stop=False),
                reads=[f"l3_vt:{tc}", "l3_atw"], writes=["pb5"], tag="l3_av")
            p.t(lambda e, h=h: e.matmul(
                out=pb[5][0:64, 0:128], lhsT=stb[:, h * 64:(h + 1) * 64],
                rhs=acr[:], start=False, stop=True),
                reads=["l3_stb", "l3_acr"], writes=["pb5"], tag="l3_avc")
            p.s(lambda e, r0=r0, ec=ec, tc=tc: e.copy(
                out=yat[r0:r0 + 64, ec * T + tc * 128:ec * T + (tc + 1) * 128],
                in_=pb[5][0:64, 0:128]),
                reads=["pb5"], writes=[f"l3_ya:{ec}", f"xn:{ec}"], tag="l3_yev")
            p.t(lambda e, ec=ec, r0=r0, tc=tc: e.transpose(
                out=pbt[:, 0:64],
                in_=kf[r0:r0 + 64, ec * T + tc * 128:ec * T + (tc + 1) * 128],
                identity=idb[r0:r0 + 64, r0:r0 + 64]),
                reads=[f"l3_kf:{ec}", "l3_idb"], writes=["pbt0"], tag="l3_ktr")
            p.v(lambda e: e.tensor_copy(out=ktr[:], in_=pbt[:, 0:64]),
                reads=["pbt0"], writes=["l3_ktr"], tag="l3_ktrev")
            p.v(lambda e: e.tensor_scalar(
                out=ks[:], in0=ktr[:], scalar1=dt_sb[:, 127:128], scalar2=None,
                op0=ALU.mult),
                reads=["l3_ktr", "l3_dt"], writes=["l3_ks"], tag="l3_ks")
            p.t(lambda e, tc=tc, h=h: e.matmul(
                out=pb[0][0:64, 0:64], lhsT=ks[:],
                rhs=vtok[:, tc * 1024 + h * 64:tc * 1024 + h * 64 + 64],
                start=True, stop=True),
                reads=["l3_ks", f"l3_vt:{tc}"], writes=["pb0"], tag="l3_kv")
            p.v(lambda e, sth=sth: e.scalar_tensor_tensor(
                out=sth, in0=sth, scalar=dc_sb[0:64, 127:128],
                in1=pb[0][0:64, 0:64],
                op0=ALU.mult, op1=ALU.add),
                reads=["l3_stf", "l3_dc", "pb0"], writes=["l3_stf"], tag="l3_stup")
            p.v(lambda e, sth=sth, stbh=stbh: e.tensor_copy(out=stbh, in_=sth),
                reads=["l3_stf"], writes=["l3_stb"], tag="l3_stbcv")

    y2bf = qf
    _ob0 = L.ln_xs[0]
    ob = [_ob0, _ob0]
    sq = L.xn_scratch_a
    rms_row = Bm[0:1, 0:T]
    for oc in range(NCH):
        wb = oc % 2
        load_slice("wprojt", oc * 128, 128, f"pj{oc}", wb)
        for ti, (t0, tw) in enumerate(tts):
            pbi = oc % 2
            for a in range(NCH):
                p.t(lambda e, a=a, oc=oc, t0=t0, tw=tw, pbi=pbi, wb=wb: e.matmul(
                    out=pb[pbi][:, 0:tw],
                    lhsT=wmb2[wb][:, a * 128:(a + 1) * 128],
                    rhs=yat[:, a * T + t0:a * T + t0 + tw],
                    start=(a == 0), stop=(a == NCH - 1)),
                    reads=[f"l3_wmb{wb}", f"l3_ya:{a}"], writes=[f"pb{pbi}"],
                    tag="l3_pj_mm")
            bo = oc % 2
            p.v(lambda e, tw=tw, pbi=pbi, bo=bo: e.tensor_copy(
                out=ob[bo][:, 0:tw], in_=pb[pbi][:, 0:tw]),
                reads=[f"pb{pbi}"], writes=["l3_ob0", "ln_xs0"], tag="l3_pj_ev")
            p.dma("gpsimd", lambda e, oc=oc, t0=t0, tw=tw, bo=bo: e.dma_start(
                out=xout[oc * 128:(oc + 1) * 128, t0:t0 + tw],
                in_=ob[bo][:, 0:tw], accum_op=ALU.add),
                reads=["l3_ob0", "ln_xs0"], writes=[f"xout:{oc}"],
                stream=f"l3_xo{oc % 4}", tag="l3_pj_acc")
            p.v(lambda e, oc=oc, t0=t0, tw=tw, bo=bo: e.tensor_copy(
                out=y2bf[:, oc * T + t0:oc * T + t0 + tw], in_=ob[bo][:, 0:tw]),
                reads=["l3_ob0"], writes=["l3_y2", "l3_qf"], tag="l3_y2cv")
            p.s(lambda e, tw=tw, bo=bo: e.activation(
                out=sq[:, 0:tw], in_=ob[bo][:, 0:tw], func=AF.Square,
                bias=0.0, scale=1.0),
                reads=["l3_ob0"], writes=["l3_sq", "ln_x2"], tag="l3_sqev")
            p.t(lambda e, ti=ti, tw=tw, oc=oc: e.matmul(
                out=pb[2 + ti][0:1, 0:tw],
                lhsT=L.ones_col, rhs=sq[:, 0:tw],
                start=(oc == 0), stop=(oc == NCH - 1)),
                reads=["l3_sq", "cst"], writes=[f"pb{2 + ti}"], tag="l3_rms_mm")
    for ti, (t0, tw) in enumerate(t_tiles(T)):
        p.s(lambda e, ti=ti, t0=t0, tw=tw: e.copy(
            out=rms_row[:, t0:t0 + tw], in_=pb[2 + ti][0:1, 0:tw]),
            reads=[f"pb{2 + ti}"], writes=["l3_rmsr", "l3_B"], tag="l3_rms_ev")
    L.cap(RMS_EPS)
    p.s(lambda e: e.activation(out=rms_row[:], in_=rms_row[:], func=AF.Ln,
                               bias=RMS_EPS, scale=1.0 / 1024.0),
        reads=["l3_rmsr", "fconst"], writes=["l3_rmsr"], tag="l3_rms_ln")
    p.s(lambda e: e.activation(out=rms_row[:], in_=rms_row[:], func=AF.Exp,
                               bias=0.0, scale=-0.5),
        reads=["l3_rmsr", "fconst"], writes=["l3_rmsr"], tag="l3_rms_ex")
    for ti, (t0, tw) in enumerate(tts):
        p.t(lambda e, t0=t0, tw=tw: e.matmul(
            out=pb[2][:, 0:tw], lhsT=L.ones_row[0:1, 0:128],
            rhs=rms_row[:, t0:t0 + tw], start=True, stop=True),
            reads=["l3_rmsr", "ones_row"], writes=["pb2"], tag="l3_rms_bc")
        for a in range(NCH):
            p.v(lambda e, a=a, t0=t0, tw=tw: e.tensor_tensor(
                out=y2bf[:, a * T + t0:a * T + t0 + tw],
                in0=y2bf[:, a * T + t0:a * T + t0 + tw],
                in1=pb[2][:, 0:tw], op=ALU.mult),
                reads=["l3_y2", "pb2"], writes=["l3_y2", "l3_qf"], tag="l3_yn")

    hid = kf if NCH * T >= 32 * 512 else L.sbuf("l3_hidS", [128, 32 * 512], BF16)
    for ti, (t0, tw) in enumerate(tts):
        for fc in range(32):
            wb = fc % 2
            p.dma("sync", lambda e, fc=fc, wb=wb: e.dma_start(
                out=wstg2[wb][:, 0:1024].rearrange("p (a f) -> p a f", a=8),
                in_=aps["wfct"].rearrange("(a p) f -> p a f", p=128)[:, :, fc * 128:(fc + 1) * 128]),
                writes=[f"l3_wstg{wb}"], tag="l3_ldfc")
            p.v(lambda e, wb=wb: e.tensor_copy(out=wmb2[wb][:], in_=wstg2[wb][:]),
                reads=[f"l3_wstg{wb}"], writes=[f"l3_wmb{wb}"], tag="l3_cvfc")
            pbi = fc % 2
            for a in range(NCH):
                p.t(lambda e, a=a, t0=t0, tw=tw, pbi=pbi, wb=wb: e.matmul(
                    out=pb[pbi][:, 0:tw],
                    lhsT=wmb2[wb][:, a * 128:(a + 1) * 128],
                    rhs=y2bf[:, a * T + t0:a * T + t0 + tw],
                    start=(a == 0), stop=(a == NCH - 1)),
                    reads=[f"l3_wmb{wb}", "l3_y2"], writes=[f"pb{pbi}"],
                    tag="l3_fc_mm")
            p.s(lambda e, fc=fc, tw=tw, pbi=pbi: e.activation(
                out=sq[:, 0:tw], in_=pb[pbi][:, 0:tw], func=AF.Relu,
                bias=0.0, scale=1.0),
                reads=[f"pb{pbi}"], writes=["l3_sq"], tag="l3_fc_r")
            p.s(lambda e, fc=fc, tw=tw: e.activation(
                out=hid[:, fc * 512:fc * 512 + tw], in_=sq[:, 0:tw],
                func=AF.Square, bias=0.0, scale=1.0),
                reads=["l3_sq"], writes=["l3_hid", "l3_kf:0"], tag="l3_fc_s")
        for oh in range(2):
            for fc in range(32):
                wb = fc % 2
                p.dma("sync", lambda e, fc=fc, oh=oh, wb=wb: e.dma_start(
                    out=wstg2[wb][:, 0:512],
                    in_=aps["wcpt"][fc * 128:(fc + 1) * 128,
                                    oh * 512:(oh + 1) * 512]),
                    writes=[f"l3_wstg{wb}"], tag="l3_ldcp")
                p.v(lambda e, wb=wb: e.tensor_copy(out=wmb2[wb][:, 0:512],
                                                   in_=wstg2[wb][:, 0:512]),
                    reads=[f"l3_wstg{wb}"], writes=[f"l3_wmb{wb}"], tag="l3_cvcp")
                for o4 in range(4):
                    p.t(lambda e, fc=fc, o4=o4, tw=tw, wb=wb: e.matmul(
                        out=pb[2 + o4][:, 0:tw],
                        lhsT=wmb2[wb][:, o4 * 128:(o4 + 1) * 128],
                        rhs=hid[:, fc * 512:fc * 512 + tw],
                        start=(fc == 0), stop=(fc == 31)),
                        reads=[f"l3_wmb{wb}", "l3_hid"], writes=[f"pb{2 + o4}"],
                        tag="l3_cp_mm")
            for o4 in range(4):
                oc = oh * 4 + o4
                bo = o4 % 2
                p.v(lambda e, o4=o4, tw=tw, bo=bo: e.tensor_copy(
                    out=ob[bo][:, 0:tw], in_=pb[2 + o4][:, 0:tw]),
                    reads=[f"pb{2 + o4}"], writes=["l3_ob0", "ln_xs0"], tag="l3_cp_ev")
                p.dma("gpsimd", lambda e, oc=oc, t0=t0, tw=tw, bo=bo: e.dma_start(
                    out=xout[oc * 128:(oc + 1) * 128, t0:t0 + tw],
                    in_=ob[bo][:, 0:tw], accum_op=ALU.add),
                    reads=["l3_ob0", "ln_xs0"], writes=[f"xout:{oc}"],
                    stream=f"l3_xo{oc % 4}", tag="l3_cp_acc")

    stat2 = L.ln_stat2
    xs = L.ln_xs
    x2r = L.xn_scratch_a
    nld = 0
    for ti, (t0, tw) in enumerate(tts):
        p.t(lambda e, t0=t0, tw=tw: e.matmul(
            out=pb[2][:, 0:tw], lhsT=L.ones_row[0:1, 0:128],
            rhs=stat2[0:1, T + t0:T + t0 + tw], start=True, stop=True),
            reads=["ln_q", "ones_row"], writes=["pb2"], tag="l3_rb_bcr")
        p.t(lambda e, t0=t0, tw=tw: e.matmul(
            out=pb[3][:, 0:tw], lhsT=L.ones_row[0:1, 0:128],
            rhs=stat2[0:1, t0:t0 + tw], start=True, stop=True),
            reads=["ln_m", "ones_row"], writes=["pb3"], tag="l3_rb_bcm")
        for a in range(NCH):
            b = nld % 2
            nld += 1
            p.dma("sync", lambda e, a=a, t0=t0, tw=tw, b=b: e.dma_start(
                out=xs[b][:, 0:tw],
                in_=xin[a * 128:(a + 1) * 128, t0:t0 + tw]),
                writes=[f"ln_xs{b}"], tag="l3_rb_ldx")
            p.v(lambda e, tw=tw, b=b: e.tensor_tensor(
                out=x2r[:, 0:tw], in0=xs[b][:, 0:tw], in1=pb[2][:, 0:tw],
                op=ALU.mult),
                reads=[f"ln_xs{b}", "pb2"], writes=["ln_x2"], tag="l3_rb_xr")
            p.v(lambda e, a=a, t0=t0, tw=tw: e.tensor_tensor(
                out=xn[:, a * T + t0:a * T + t0 + tw], in0=x2r[:, 0:tw],
                in1=pb[3][:, 0:tw], op=ALU.subtract),
                reads=["ln_x2", "pb3"],
                writes=[f"xn:{a}", f"l3_ya:{a}"], tag="l3_rb_xn")

    rt = mtX[:, 0:n_tc * 8]
    wsel = mtX[:, 256:256 + n_tc * 8]
    mxp = s1X[:, 0:4]
    exf = s1X[:, 8:16]
    g8 = s2X[0:8, 0:128]
    g8_hbm = nc.dram_tensor("l3_g8_hbm", [8, n_tc * 128], F32, kind="Internal").ap()
    ge_row = lrow
    gw_sb = L.xn_scratch_a[:, 0:512]
    wr32 = L.sbuf("l3_wr32", [128, NCH * 8], F32)
    br_32 = L.sbuf("l3_br32", [1, 8], F32)
    p.dma("sync", lambda e: e.dma_start(
        out=wr32[:].rearrange("p (a r) -> p a r", a=NCH),
        in_=aps["wr"].rearrange("(a p) r -> p a r", p=128)),
        writes=["l3_wr32"], tag="l3_ldwr")
    p.dma("sync", lambda e: e.dma_start(out=br_32[:], in_=aps["br"]),
          writes=["l3_br32"], tag="l3_ldbr")
    for a in range(NCH):
        p.t(lambda e, a=a: e.matmul(
            out=pb[4][0:1, 0:8], lhsT=L.ones_col, rhs=wr32[:, a * 8:(a + 1) * 8],
            start=(a == 0), stop=(a == NCH - 1)),
            reads=["l3_wr32", "cst"], writes=["pb4"], tag="l3_sr_mm")
    srow = L.sbuf("l3_srow", [1, 8], F32)
    srow_bc = L.sbuf("l3_srbc", [128, 8], F32)
    p.v(lambda e: e.tensor_scalar(out=srow[:], in0=pb[4][0:1, 0:8],
                                  scalar1=-1.0, scalar2=None, op0=ALU.mult),
        reads=["pb4"], writes=["l3_srow"], tag="l3_sr_ev")
    p.t(lambda e: e.matmul(out=pb[4][:, 0:8], lhsT=L.ones_row[0:1, 0:128],
                           rhs=srow[:], start=True, stop=True),
        reads=["l3_srow", "ones_row"], writes=["pb4"], tag="l3_sr_bc")
    p.v(lambda e: e.tensor_copy(out=srow_bc[:], in_=pb[4][:, 0:8]),
        reads=["pb4"], writes=["l3_srbc"], tag="l3_sr_bcev")
    br_bc = L.sbuf("l3_brbc", [128, 8], F32)
    p.t(lambda e: e.matmul(out=pb[4][:, 0:8], lhsT=L.ones_row[0:1, 0:128],
                           rhs=br_32[:], start=True, stop=True),
        reads=["l3_br32", "ones_row"], writes=["pb4"], tag="l3_br_bc")
    p.v(lambda e: e.tensor_copy(out=br_bc[:], in_=pb[4][:, 0:8]),
        reads=["pb4"], writes=["l3_brbc"], tag="l3_br_bcev")
    rmcols = L.sbuf("l3_rmc", [128, 2 * n_tc], F32)
    for tc in range(n_tc):
        p.t(lambda e, tc=tc: e.transpose(
            out=pb[4][:, 0:1], in_=stat2[0:1, T + tc * 128:T + (tc + 1) * 128],
            identity=L.ones_row[0:1, 0:1]),
            reads=["ln_q", "ones_row"], writes=["pb4"], tag="l3_rst_tr")
        p.v(lambda e, tc=tc: e.tensor_copy(out=rmcols[:, tc:tc + 1],
                                           in_=pb[4][:, 0:1]),
            reads=["pb4"], writes=["l3_rmc"], tag="l3_rst_ev")
        p.t(lambda e, tc=tc: e.transpose(
            out=pb[4][:, 0:1], in_=stat2[0:1, tc * 128:(tc + 1) * 128],
            identity=L.ones_row[0:1, 0:1]),
            reads=["ln_m", "ones_row"], writes=["pb4"], tag="l3_msc_tr")
        p.v(lambda e, tc=tc: e.tensor_copy(out=rmcols[:, n_tc + tc:n_tc + tc + 1],
                                           in_=pb[4][:, 0:1]),
            reads=["pb4"], writes=["l3_rmc"], tag="l3_msc_ev")
    xs_r = L.ln_xs
    nld2 = 0
    for ti, (t0, tw) in enumerate(tts):
        nsub = tw // 128
        for a in range(NCH):
            b = nld2 % 2
            nld2 += 1
            p.dma("sync", lambda e, a=a, t0=t0, tw=tw, b=b: e.dma_start(
                out=xs_r[b][:, 0:tw],
                in_=xin[a * 128:(a + 1) * 128, t0:t0 + tw]),
                writes=[f"ln_xs{b}"], tag="l3_rt_ldx")
            for su in range(nsub):
                sb_ = [0, 1, 4, 5][su]
                p.t(lambda e, a=a, su=su, b=b, sb_=sb_: e.matmul(
                    out=pb[sb_][:, 0:8],
                    lhsT=xs_r[b][:, su * 128:(su + 1) * 128],
                    rhs=wr32[:, a * 8:(a + 1) * 8],
                    start=(a == 0), stop=False),
                    reads=[f"ln_xs{b}", "l3_wr32"], writes=[f"pb{sb_}"],
                    tag="l3_rt_mm")
        for su in range(nsub):
            sb_ = [0, 1, 4, 5][su]
            p.t(lambda e, su=su, sb_=sb_: e.matmul(
                out=pb[sb_][:, 0:8], lhsT=L.ones_row[0:1, 0:128],
                rhs=br_32[:], start=False, stop=True),
                reads=["l3_br32", "ones_row"], writes=[f"pb{sb_}"],
                tag="l3_rt_b")
        for su in range(nsub):
            tc = (t0 // 128) + su
            sb_ = [0, 1, 4, 5][su]
            p.v(lambda e, su=su, sb_=sb_: e.tensor_tensor(
                out=exf[:], in0=pb[sb_][:, 0:8], in1=br_bc[:],
                op=ALU.subtract),
                reads=[f"pb{sb_}", "l3_brbc"], writes=["l3_exf", "l3_s1"],
                tag="l3_rt_x")
            p.v(lambda e, tc=tc: e.tensor_scalar(
                out=exf[:], in0=exf[:], scalar1=rmcols[:, tc:tc + 1],
                scalar2=None, op0=ALU.mult),
                reads=["l3_exf", "l3_rmc"], writes=["l3_exf", "l3_s1"],
                tag="l3_rt_a")
            p.v(lambda e, tc=tc: e.scalar_tensor_tensor(
                out=exf[:], in0=srow_bc[:], scalar=rmcols[:, n_tc + tc:n_tc + tc + 1],
                in1=exf[:], op0=ALU.mult, op1=ALU.add),
                reads=["l3_srbc", "l3_rmc", "l3_exf"], writes=["l3_exf", "l3_s1"],
                tag="l3_rt_ms")
            p.v(lambda e: e.tensor_tensor(
                out=exf[:], in0=exf[:], in1=br_bc[:], op=ALU.add),
                reads=["l3_exf", "l3_brbc"], writes=["l3_exf", "l3_s1"],
                tag="l3_rt_br")
            p.s(lambda e, tc=tc: e.activation(
                out=rt[:, tc * 8:(tc + 1) * 8], in_=exf[:], func=AF.Sigmoid,
                bias=0.0, scale=1.0),
                reads=["l3_exf"], writes=["l3_rt", "l3_mt"], tag="l3_rt_sg")
    for tc in range(n_tc):
        pbi = tc % 2
        p.v(lambda e, tc=tc: e.reduce_max(
            out=mxp[:, 0:1], in_=rt[:, tc * 8:(tc + 1) * 8], axis=AX.X),
            reads=["l3_rt"], writes=["l3_mxp", "l3_s1"], tag="l3_mx")
        p.v(lambda e, tc=tc: e.tensor_tensor_scan(
            out=wsel[:, tc * 8:(tc + 1) * 8], data0=rt[:, tc * 8:(tc + 1) * 8],
            data1=rt[:, tc * 8:(tc + 1) * 8], initial=-1e30,
            op0=ALU.max, op1=ALU.bypass),
            reads=["l3_rt"], writes=["l3_wsel", "l3_mt"], tag="l3_pfx")
        p.v(lambda e: e.memset(mxp[:, 1:2], -1e30), writes=["l3_mxp", "l3_s1"],
            tag="l3_pfx0")
        p.v(lambda e, tc=tc: e.tensor_copy(
            out=mxp[:, 2:3], in_=mxp[:, 0:1]),
            reads=["l3_mxp"], writes=["l3_mxp", "l3_s1"], tag="l3_mxcp")
        p.v(lambda e, tc=tc: e.tensor_scalar(
            out=exf[:, 1:8], in0=wsel[:, tc * 8:tc * 8 + 7],
            scalar1=mxp[:, 0:1], scalar2=None, op0=ALU.is_lt),
            reads=["l3_wsel", "l3_mxp"], writes=["l3_exf", "l3_s1"], tag="l3_exlt")
        p.v(lambda e: e.memset(exf[:, 0:1], 1.0), writes=["l3_exf", "l3_s1"], tag="l3_ex0")
        p.v(lambda e, tc=tc: e.tensor_copy(out=wsel[:, tc * 8:(tc + 1) * 8],
                                           in_=exf[:]),
            reads=["l3_exf"], writes=["l3_wsel", "l3_mt"], tag="l3_excp")
        p.v(lambda e, tc=tc: e.tensor_scalar(
            out=rt[:, tc * 8:(tc + 1) * 8], in0=rt[:, tc * 8:(tc + 1) * 8],
            scalar1=mxp[:, 0:1], scalar2=None, op0=ALU.is_ge),
            reads=["l3_rt", "l3_mxp"], writes=["l3_rt", "l3_mt"], tag="l3_isge")
        p.v(lambda e, tc=tc: e.tensor_tensor(
            out=wsel[:, tc * 8:(tc + 1) * 8], in0=wsel[:, tc * 8:(tc + 1) * 8],
            in1=rt[:, tc * 8:(tc + 1) * 8], op=ALU.mult),
            reads=["l3_wsel", "l3_rt"], writes=["l3_wsel", "l3_mt"], tag="l3_msk")
        p.v(lambda e, tc=tc: e.tensor_scalar(
            out=wsel[:, tc * 8:(tc + 1) * 8], in0=wsel[:, tc * 8:(tc + 1) * 8],
            scalar1=mxp[:, 0:1], scalar2=None, op0=ALU.mult),
            reads=["l3_wsel", "l3_mxp"], writes=["l3_wsel", "l3_mt"], tag="l3_wsc")
        p.t(lambda e, tc=tc, pbi=pbi: e.transpose(
            out=pb[2 + pbi][0:8, 0:128], in_=wsel[:, tc * 8:(tc + 1) * 8],
            identity=idf[:]),
            reads=["l3_wsel", "l3_idf"], writes=[f"pb{2 + pbi}"], tag="l3_gtr")
        p.v(lambda e, tc=tc, pbi=pbi: e.tensor_copy(
            out=g8[:], in_=pb[2 + pbi][0:8, 0:128]),
            reads=[f"pb{2 + pbi}"], writes=["l3_g8", "l3_s2"], tag="l3_gtr_ev")
        p.dma("sync", lambda e, tc=tc: e.dma_start(
            out=g8_hbm[:, tc * 128:(tc + 1) * 128], in_=g8[:]),
            reads=["l3_g8"], writes=["l3_g8hbm"], stream="l3_g8hbm",
            tag="l3_g8_st")
    for ex in range(8):
        p.dma("sync", lambda e, ex=ex: e.dma_start(
            out=ge_row[:], in_=g8_hbm[ex:ex + 1, :]),
            reads=["l3_g8hbm"], writes=["l3_ge", "l3_lrow"], stream="l3_ge",
            tag="l3_gedma")
        for ti, (t0, tw) in enumerate(tts):
            for fc in range(32):
                wb = fc % 2
                p.dma("sync", lambda e, ex=ex, fc=fc, wb=wb: e.dma_start(
                    out=wstg2[wb][:].rearrange("p (a f) -> p a f", a=8),
                    in_=aps[f"w1_{ex}"].rearrange("(a p) f -> p a f", p=128)[:, :, fc * 128:(fc + 1) * 128]),
                    writes=[f"l3_wstg{wb}"], tag="l3_ldw1")
                p.v(lambda e, wb=wb: e.tensor_copy(out=wmb2[wb][:], in_=wstg2[wb][:]),
                    reads=[f"l3_wstg{wb}"], writes=[f"l3_wmb{wb}"], tag="l3_cvw1")
                pbi = fc % 2
                for a in range(NCH):
                    p.t(lambda e, a=a, t0=t0, tw=tw, pbi=pbi, wb=wb: e.matmul(
                        out=pb[pbi][:, 0:tw],
                        lhsT=wmb2[wb][:, a * 128:(a + 1) * 128],
                        rhs=xn[:, a * T + t0:a * T + t0 + tw],
                        start=(a == 0), stop=(a == NCH - 1)),
                        reads=[f"l3_wmb{wb}", f"xn:{a}"], writes=[f"pb{pbi}"],
                        tag="l3_e1_mm")
                p.s(lambda e, fc=fc, tw=tw, pbi=pbi: e.activation(
                    out=hid[:, fc * 512:fc * 512 + tw], in_=pb[pbi][:, 0:tw],
                    func=AF.Gelu, bias=0.0, scale=1.0),
                    reads=[f"pb{pbi}"], writes=["l3_hid", "l3_kf:0"],
                    tag="l3_e1_g")
            p.t(lambda e, t0=t0, tw=tw: e.matmul(
                out=pb[2][:, 0:tw], lhsT=L.ones_row[0:1, 0:128],
                rhs=ge_row[0:1, t0:t0 + tw], start=True, stop=True),
                reads=["l3_ge", "ones_row"], writes=["pb2"], tag="l3_gbc")
            p.v(lambda e, tw=tw: e.tensor_copy(out=gw_sb[:, 0:tw],
                                               in_=pb[2][:, 0:tw]),
                reads=["pb2"], writes=["l3_gwsb", "ln_x2"], tag="l3_gbc_ev")
            for oh in range(2):
                for fc in range(32):
                    wb = fc % 2
                    p.dma("sync", lambda e, ex=ex, fc=fc, oh=oh, wb=wb: e.dma_start(
                        out=wstg2[wb][:, 0:512],
                        in_=aps[f"w2_{ex}"][fc * 128:(fc + 1) * 128,
                                            oh * 512:(oh + 1) * 512]),
                        writes=[f"l3_wstg{wb}"], tag="l3_ldw2")
                    p.v(lambda e, wb=wb: e.tensor_copy(out=wmb2[wb][:, 0:512],
                                                       in_=wstg2[wb][:, 0:512]),
                        reads=[f"l3_wstg{wb}"], writes=[f"l3_wmb{wb}"],
                        tag="l3_cvw2")
                    for o4 in range(4):
                        p.t(lambda e, fc=fc, o4=o4, tw=tw, wb=wb: e.matmul(
                            out=pb[2 + o4][:, 0:tw],
                            lhsT=wmb2[wb][:, o4 * 128:(o4 + 1) * 128],
                            rhs=hid[:, fc * 512:fc * 512 + tw],
                            start=(fc == 0), stop=(fc == 31)),
                            reads=[f"l3_wmb{wb}", "l3_hid"], writes=[f"pb{2 + o4}"],
                            tag="l3_e2_mm")
                for o4 in range(4):
                    oc = oh * 4 + o4
                    bo = o4 % 2
                    p.v(lambda e, o4=o4, tw=tw, bo=bo: e.tensor_tensor(
                        out=ob[bo][:, 0:tw], in0=pb[2 + o4][:, 0:tw],
                        in1=gw_sb[:, 0:tw], op=ALU.mult),
                        reads=[f"pb{2 + o4}", "l3_gwsb"], writes=["l3_ob0"],
                        tag="l3_e2_ev")
                    p.dma("gpsimd", lambda e, oc=oc, t0=t0, tw=tw, bo=bo: e.dma_start(
                        out=xout[oc * 128:(oc + 1) * 128, t0:t0 + tw],
                        in_=ob[bo][:, 0:tw], accum_op=ALU.add),
                        reads=["l3_ob0", "ln_xs0"], writes=[f"xout:{oc}"],
                        stream=f"l3_xo{oc % 4}", tag="l3_e2_acc")
    L.p.wait_all("gpsimd", [f"xout:{a}" for a in range(NCH)], tag="l3_final")


def prep_l3(inputs):
    c8 = lambda v, n: np.ascontiguousarray(np.asarray(v).reshape(n, 128).T)
    tr = lambda v: np.ascontiguousarray(np.asarray(v).T)
    d = {
        "wqt": tr(inputs["L3_attn_Wq"]), "wkt": tr(inputs["L3_attn_Wk"]),
        "wvt": tr(inputs["L3_attn_Wv"]),
        "wm1t": tr(inputs["L3_attn_Wm1"]), "bm1": c8(inputs["L3_attn_bm1"], 2),
        "wm2t": tr(inputs["L3_attn_Wm2"]),
        "bm2": np.ascontiguousarray(np.asarray(inputs["L3_attn_bm2"]).reshape(16, 1)),
        "wprojt": tr(inputs["L3_attn_Wproj"]),
        "wfct": tr(inputs["L3_attn_Wfc"]), "wcpt": tr(inputs["L3_attn_Wcp"]),
        "wr": np.ascontiguousarray(np.asarray(inputs["L3_ffn_Wr"])),
        "br": np.asarray(inputs["L3_ffn_br"]).reshape(1, 8),
        "idf": np.eye(128, dtype=np.float32),
    }
    tm = np.full((128, 128), -1e30, np.float32)
    tm[np.tril_indices(128)] = 0.0
    tm = np.full((128, 128), -1e30, np.float32)
    su, tu = np.meshgrid(np.arange(128), np.arange(128), indexing="ij")
    tm[su <= tu] = 0.0
    d["trimask"] = tm
    w1 = np.asarray(inputs["L3_ffn_W1"])
    w2 = np.asarray(inputs["L3_ffn_W2"])
    for ex in range(8):
        d[f"w1_{ex}"] = np.ascontiguousarray(w1[ex])
        d[f"w2_{ex}"] = np.ascontiguousarray(w2[ex])
    return d


L3_DRAM_SPECS = (
    [("wqt", [1024, 1024]), ("wkt", [1024, 1024]), ("wvt", [1024, 1024]),
     ("wm1t", [1024, 256]), ("bm1", [128, 2]), ("wm2t", [256, 16]),
     ("bm2", [16, 1]), ("wprojt", [1024, 1024]), ("wfct", [1024, 4096]),
     ("wcpt", [4096, 1024]), ("wr", [1024, 8]), ("br", [1, 8]),
     ("idf", [128, 128]), ("trimask", [128, 128])]
    + [(f"w1_{ex}", [1024, 4096]) for ex in range(8)]
    + [(f"w2_{ex}", [4096, 1024]) for ex in range(8)]
)



KERR = 2.7e-18 * 1000000.0
PI = 3.141592653589793


def build_l0(L, aps, xin, xout):
    nc, p, T = L.nc, L.p, L.T
    pb = L.pb
    tts = t_tiles(T)
    n_tc = T // 128
    p.dma("sync", lambda e: e.dma_start(out=xout, in_=xin),
          writes=[f"xout:{a}" for a in range(NCH)], stream="xout_init",
          tag="l0_xout_init")
    L.layernorm_streaming(xin)
    xn = L.xn
    stat2 = L.ln_stat2

    wstg2 = [L.sbuf(f"l0_wstg{i}", [128, 1024], F32) for i in range(2)]
    wmb2 = [L.sbuf(f"l0_wmb{i}", [128, 1024], BF16) for i in range(2)]
    wstg = wstg2[0]
    idf = L.sbuf("l0_idf", [128, 128], F32)
    p.dma("sync", lambda e: e.dma_start(out=idf[:], in_=aps["idf"]),
          writes=["l0_idf"], tag="l0_ldid")

    def load_slice(name, c0, cw, tag, b=0):
        p.dma("sync", lambda e, b=b: e.dma_start(
            out=wstg2[b][:, 0:NCH * cw].rearrange("p (a f) -> p a f", a=NCH),
            in_=aps[name].rearrange("(a p) f -> p a f", p=128)[:, :, c0:c0 + cw]),
            writes=[f"l0_wstg{b}"], tag=f"l0_lds_{tag}")
        p.v(lambda e, b=b: e.tensor_copy(out=wmb2[b][:, 0:NCH * cw],
                                         in_=wstg2[b][:, 0:NCH * cw]),
            reads=[f"l0_wstg{b}"], writes=[f"l0_wmb{b}"], tag=f"l0_cvs_{tag}")

    bqk = L.sbuf("l0_bqk", [128, 3 * NCH], F32)
    for i, nm in enumerate(["bq", "bk", "bo"]):
        p.dma("sync", lambda e, i=i, nm=nm: e.dma_start(
            out=bqk[:, i * NCH:(i + 1) * NCH], in_=aps[nm]),
            writes=["l0_bqk"], stream="l0_bqk", tag=f"l0_ld{nm}")
    vbrow32 = wstg[0:1, :]
    vbrow = L.sbuf("l0_vbrow", [1, 1024], BF16)
    p.dma("sync", lambda e: e.dma_start(out=vbrow32, in_=aps["vbrow"]),
          writes=["l0_wstg0"], tag="l0_ldvb")
    p.v(lambda e: e.tensor_copy(out=vbrow[:], in_=vbrow32),
        reads=["l0_wstg0"], writes=["l0_vbrow"], tag="l0_cvvb")

    gw32 = L.sbuf("l0_gw32", [128, NCH * 8], F32)
    p.dma("sync", lambda e: e.dma_start(
        out=gw32[:].rearrange("p (a r) -> p a r", a=NCH),
        in_=aps["gwt"].rearrange("(a p) r -> p a r", p=128)),
        writes=["l0_gw32"], tag="l0_ldgw")
    for a in range(NCH):
        p.t(lambda e, a=a: e.matmul(
            out=pb[4][0:1, 0:8], lhsT=L.ones_col, rhs=gw32[:, a * 8:(a + 1) * 8],
            start=(a == 0), stop=(a == NCH - 1)),
            reads=["l0_gw32", "cst"], writes=["pb4"], tag="l0_sr_mm")
    srow = L.sbuf("l0_srow", [1, 8], F32)
    srow_bc = L.sbuf("l0_srbc", [128, 8], F32)
    p.v(lambda e: e.tensor_scalar(out=srow[:], in0=pb[4][0:1, 0:8],
                                  scalar1=-1.0, scalar2=None, op0=ALU.mult),
        reads=["pb4"], writes=["l0_srow"], tag="l0_sr_ev")
    p.t(lambda e: e.matmul(out=pb[4][:, 0:8], lhsT=L.ones_row[0:1, 0:128],
                           rhs=srow[:], start=True, stop=True),
        reads=["l0_srow", "ones_row"], writes=["pb4"], tag="l0_sr_bc")
    p.v(lambda e: e.tensor_copy(out=srow_bc[:], in_=pb[4][:, 0:8]),
        reads=["pb4"], writes=["l0_srbc"], tag="l0_sr_bcev")
    rmcols = L.sbuf("l0_rmc", [128, 2 * n_tc], F32)
    for tc in range(n_tc):
        p.t(lambda e, tc=tc: e.transpose(
            out=pb[4][:, 0:1], in_=stat2[0:1, T + tc * 128:T + (tc + 1) * 128],
            identity=L.ones_row[0:1, 0:1]),
            reads=["ln_q", "ones_row"], writes=["pb4"], tag="l0_rst_tr")
        p.v(lambda e, tc=tc: e.tensor_copy(out=rmcols[:, tc:tc + 1],
                                           in_=pb[4][:, 0:1]),
            reads=["pb4"], writes=["l0_rmc"], tag="l0_rst_ev")
        p.t(lambda e, tc=tc: e.transpose(
            out=pb[4][:, 0:1], in_=stat2[0:1, tc * 128:(tc + 1) * 128],
            identity=L.ones_row[0:1, 0:1]),
            reads=["ln_m", "ones_row"], writes=["pb4"], tag="l0_msc_tr")
        p.v(lambda e, tc=tc: e.tensor_copy(out=rmcols[:, n_tc + tc:n_tc + tc + 1],
                                           in_=pb[4][:, 0:1]),
            reads=["pb4"], writes=["l0_rmc"], tag="l0_msc_ev")
    rsc = L.sbuf("l0_rsc", [128, 64], F32)
    wsel = L.sbuf("l0_wsel", [128, n_tc * 8], F32)
    g8 = L.sbuf("l0_g8", [8, 128], F32)
    g8_hbm = nc.dram_tensor("l0_g8_hbm", [8, n_tc * 128], F32, kind="Internal").ap()
    xs_r = L.ln_xs
    nld2 = 0
    for ti, (t0, tw) in enumerate(tts):
        nsub = tw // 128
        for a in range(NCH):
            b = nld2 % 2
            nld2 += 1
            p.dma("sync", lambda e, a=a, t0=t0, tw=tw, b=b: e.dma_start(
                out=xs_r[b][:, 0:tw],
                in_=xin[a * 128:(a + 1) * 128, t0:t0 + tw]),
                writes=[f"ln_xs{b}"], tag="l0_rt_ldx")
            for su in range(nsub):
                sb_ = [0, 1, 4, 5][su]
                p.t(lambda e, a=a, su=su, b=b, sb_=sb_: e.matmul(
                    out=pb[sb_][:, 0:8],
                    lhsT=xs_r[b][:, su * 128:(su + 1) * 128],
                    rhs=gw32[:, a * 8:(a + 1) * 8],
                    start=(a == 0), stop=(a == NCH - 1)),
                    reads=[f"ln_xs{b}", "l0_gw32"], writes=[f"pb{sb_}"],
                    tag="l0_rt_mm")
        for su in range(nsub):
            tc = (t0 // 128) + su
            sb_ = [0, 1, 4, 5][su]
            p.v(lambda e, su=su, tc=tc, sb_=sb_: e.tensor_scalar(
                out=rsc[:, 0:8], in0=pb[sb_][:, 0:8],
                scalar1=rmcols[:, tc:tc + 1], scalar2=None, op0=ALU.mult),
                reads=[f"pb{sb_}", "l0_rmc"], writes=["l0_rsc"], tag="l0_rt_a")
            p.v(lambda e, tc=tc: e.scalar_tensor_tensor(
                out=rsc[:, 0:8], in0=srow_bc[:],
                scalar=rmcols[:, n_tc + tc:n_tc + tc + 1],
                in1=rsc[:, 0:8], op0=ALU.mult, op1=ALU.add),
                reads=["l0_srbc", "l0_rmc", "l0_rsc"], writes=["l0_rsc"],
                tag="l0_rt_ms")
            p.v(lambda e: e.reduce_max(out=rsc[:, 16:17], in_=rsc[:, 0:8],
                                       axis=AX.X, negate=True),
                reads=["l0_rsc"], writes=["l0_rsc"], tag="l0_rt_mx")
            p.s(lambda e: e.activation(out=rsc[:, 8:16], in_=rsc[:, 0:8],
                                       func=AF.Exp, bias=rsc[:, 16:17],
                                       scale=1.0, accum_out=rsc[:, 17:18]),
                reads=["l0_rsc"], writes=["l0_rsc"], tag="l0_rt_ex")
            p.v(lambda e: e.reciprocal(out=rsc[:, 18:19], in_=rsc[:, 17:18]),
                reads=["l0_rsc"], writes=["l0_rsc"], tag="l0_rt_rz")
            p.v(lambda e: e.tensor_scalar(
                out=rsc[:, 8:16], in0=rsc[:, 8:16], scalar1=rsc[:, 18:19],
                scalar2=None, op0=ALU.mult),
                reads=["l0_rsc"], writes=["l0_rsc"], tag="l0_rt_p")
            p.v(lambda e: e.reduce_max(out=rsc[:, 16:17], in_=rsc[:, 8:16],
                                       axis=AX.X),
                reads=["l0_rsc"], writes=["l0_rsc"], tag="l0_t2_m1")
            p.v(lambda e: e.tensor_tensor_scan(
                out=rsc[:, 24:32], data0=rsc[:, 8:16], data1=rsc[:, 8:16],
                initial=-1e30, op0=ALU.max, op1=ALU.bypass),
                reads=["l0_rsc"], writes=["l0_rsc"], tag="l0_t2_pfx")
            p.v(lambda e: e.tensor_scalar(
                out=rsc[:, 33:40], in0=rsc[:, 24:31], scalar1=rsc[:, 16:17],
                scalar2=None, op0=ALU.is_lt),
                reads=["l0_rsc"], writes=["l0_rsc"], tag="l0_t2_ex")
            p.v(lambda e: e.memset(rsc[:, 32:33], 1.0),
                writes=["l0_rsc"], tag="l0_t2_e0")
            p.v(lambda e: e.tensor_scalar(
                out=rsc[:, 40:48], in0=rsc[:, 8:16], scalar1=rsc[:, 16:17],
                scalar2=None, op0=ALU.is_ge),
                reads=["l0_rsc"], writes=["l0_rsc"], tag="l0_t2_ge")
            p.v(lambda e: e.tensor_tensor(
                out=rsc[:, 40:48], in0=rsc[:, 40:48], in1=rsc[:, 32:40],
                op=ALU.mult),
                reads=["l0_rsc"], writes=["l0_rsc"], tag="l0_t2_m1f")
            p.v(lambda e: e.scalar_tensor_tensor(
                out=rsc[:, 48:56], in0=rsc[:, 40:48], scalar=-2.0,
                in1=rsc[:, 8:16], op0=ALU.mult, op1=ALU.add),
                reads=["l0_rsc"], writes=["l0_rsc"], tag="l0_t2_p2")
            p.v(lambda e: e.reduce_max(out=rsc[:, 16:17], in_=rsc[:, 48:56],
                                       axis=AX.X),
                reads=["l0_rsc"], writes=["l0_rsc"], tag="l0_t2_m2")
            p.v(lambda e: e.tensor_tensor_scan(
                out=rsc[:, 24:32], data0=rsc[:, 48:56], data1=rsc[:, 48:56],
                initial=-1e30, op0=ALU.max, op1=ALU.bypass),
                reads=["l0_rsc"], writes=["l0_rsc"], tag="l0_t2_pfx2")
            p.v(lambda e: e.tensor_scalar(
                out=rsc[:, 33:40], in0=rsc[:, 24:31], scalar1=rsc[:, 16:17],
                scalar2=None, op0=ALU.is_lt),
                reads=["l0_rsc"], writes=["l0_rsc"], tag="l0_t2_ex2")
            p.v(lambda e: e.memset(rsc[:, 32:33], 1.0),
                writes=["l0_rsc"], tag="l0_t2_e02")
            p.v(lambda e: e.tensor_scalar(
                out=rsc[:, 56:64], in0=rsc[:, 48:56], scalar1=rsc[:, 16:17],
                scalar2=None, op0=ALU.is_ge),
                reads=["l0_rsc"], writes=["l0_rsc"], tag="l0_t2_ge2")
            p.v(lambda e: e.tensor_tensor(
                out=rsc[:, 56:64], in0=rsc[:, 56:64], in1=rsc[:, 32:40],
                op=ALU.mult),
                reads=["l0_rsc"], writes=["l0_rsc"], tag="l0_t2_m2f")
            p.v(lambda e: e.tensor_tensor(
                out=rsc[:, 40:48], in0=rsc[:, 40:48], in1=rsc[:, 56:64],
                op=ALU.add),
                reads=["l0_rsc"], writes=["l0_rsc"], tag="l0_t2_mm")
            p.v(lambda e, tc=tc: e.tensor_tensor(
                out=wsel[:, tc * 8:(tc + 1) * 8], in0=rsc[:, 8:16],
                in1=rsc[:, 40:48], op=ALU.mult),
                reads=["l0_rsc"], writes=["l0_wsel"], tag="l0_t2_w")
            p.t(lambda e, tc=tc: e.transpose(
                out=pb[2][0:8, 0:128], in_=wsel[:, tc * 8:(tc + 1) * 8],
                identity=idf[:]),
                reads=["l0_wsel", "l0_idf"], writes=["pb2"], tag="l0_gtr")
            p.v(lambda e: e.tensor_copy(out=g8[:], in_=pb[2][0:8, 0:128]),
                reads=["pb2"], writes=["l0_g8"], tag="l0_gtr_ev")
            p.dma("sync", lambda e, tc=tc: e.dma_start(
                out=g8_hbm[:, tc * 128:(tc + 1) * 128], in_=g8[:]),
                reads=["l0_g8"], writes=["l0_g8hbm"], stream="l0_g8hbm",
                tag="l0_g8_st")

    vtok = L.sbuf("l0_vt", [128, n_tc * 1024], BF16)
    yat = L.sbuf("l0_ya", [128, NCH * T], BF16)
    ktc = L.sbuf("l0_ktc", [128, T], BF16)
    for eo in range(NCH):
        wb = eo % 2
        load_slice("wv", eo * 128, 128, f"wv{eo}", wb)
        for tc in range(n_tc):
            pbi = tc % 2
            for a in range(NCH):
                p.t(lambda e, a=a, tc=tc, pbi=pbi, wb=wb: e.matmul(
                    out=pb[pbi][:, 0:128],
                    lhsT=xn[:, a * T + tc * 128:a * T + (tc + 1) * 128],
                    rhs=wmb2[wb][:, a * 128:(a + 1) * 128],
                    start=(a == 0), stop=False),
                    reads=[f"l0_wmb{wb}", f"xn:{a}"], writes=[f"pb{pbi}"],
                    tag="l0_v_mm")
            p.t(lambda e, eo=eo, pbi=pbi: e.matmul(
                out=pb[pbi][:, 0:128], lhsT=L.ones_row_bf[0:1, 0:128],
                rhs=vbrow[0:1, eo * 128:(eo + 1) * 128],
                start=False, stop=True),
                reads=["l0_vbrow", "ones_row_bf"], writes=[f"pb{pbi}"],
                tag="l0_v_b")
            p.v(lambda e, tc=tc, eo=eo, pbi=pbi: e.tensor_copy(
                out=vtok[:, tc * 1024 + eo * 128:tc * 1024 + (eo + 1) * 128],
                in_=pb[pbi][:, 0:128]),
                reads=[f"pb{pbi}"], writes=[f"l0_vt:{tc}"], tag="l0_v_ev")

    pmr = L.sbuf("l0_pmr", [1, 3 * 256], F32)
    iwr = L.sbuf("l0_iwr", [1, 16], F32)
    p.dma("sync", lambda e: e.dma_start(out=pmr[0:1, 0:256], in_=aps["pmrow"]),
          writes=["l0_pmr"], tag="l0_ldpm")
    p.dma("sync", lambda e: e.dma_start(out=iwr[:], in_=aps["iwrow"]),
          writes=["l0_iwr"], tag="l0_ldiw")
    L.cap(PI / 2)
    p.v(lambda e: e.memset(pmr[0:1, 512:768], 0.0), writes=["l0_pmr"],
        tag="l0_pm_k0")
    for i in range(4):
        thr = (2 * i + 1) * PI
        p.v(lambda e, thr=thr: e.tensor_scalar(
            out=pmr[0:1, 256:512], in0=pmr[0:1, 0:256], scalar1=thr,
            scalar2=None, op0=ALU.is_ge),
            reads=["l0_pmr"], writes=["l0_pmr"], tag="l0_pm_ge")
        p.v(lambda e: e.tensor_tensor(
            out=pmr[0:1, 512:768], in0=pmr[0:1, 512:768],
            in1=pmr[0:1, 256:512], op=ALU.add),
            reads=["l0_pmr"], writes=["l0_pmr"], tag="l0_pm_ka")
        p.v(lambda e, thr=thr: e.tensor_scalar(
            out=pmr[0:1, 256:512], in0=pmr[0:1, 0:256], scalar1=-thr,
            scalar2=None, op0=ALU.is_le),
            reads=["l0_pmr"], writes=["l0_pmr"], tag="l0_pm_le")
        p.v(lambda e: e.tensor_tensor(
            out=pmr[0:1, 512:768], in0=pmr[0:1, 512:768],
            in1=pmr[0:1, 256:512], op=ALU.subtract),
            reads=["l0_pmr"], writes=["l0_pmr"], tag="l0_pm_ks")
    p.v(lambda e: e.scalar_tensor_tensor(
        out=pmr[0:1, 256:512], in0=pmr[0:1, 512:768], scalar=-2 * PI,
        in1=pmr[0:1, 0:256], op0=ALU.mult, op1=ALU.add),
        reads=["l0_pmr"], writes=["l0_pmr"], tag="l0_pm_rr")
    p.s(lambda e: e.activation(out=pmr[0:1, 256:512], in_=pmr[0:1, 256:512],
                               func=AF.Abs, bias=0.0, scale=1.0),
        reads=["l0_pmr"], writes=["l0_pmr"], tag="l0_pm_abs")
    p.s(lambda e: e.activation(out=pmr[0:1, 256:512], in_=pmr[0:1, 256:512],
                               func=AF.Sin, bias=PI / 2, scale=-1.0),
        reads=["l0_pmr", "fconst"], writes=["l0_pmr"], tag="l0_pm_cos")
    p.s(lambda e: e.activation(out=pmr[0:1, 256:512], in_=pmr[0:1, 256:512],
                               func=AF.Square, bias=0.0, scale=1.0),
        reads=["l0_pmr"], writes=["l0_pmr"], tag="l0_pm_sq")
    p.v(lambda e: e.tensor_scalar(
        out=pmr[0:1, 256:512], in0=pmr[0:1, 256:512], scalar1=0.125,
        scalar2=None, op0=ALU.mult),
        reads=["l0_pmr", "fconst"], writes=["l0_pmr"], tag="l0_pm_c8")
    p.s(lambda e: e.activation(out=pmr[0:1, 512:768], in_=pmr[0:1, 256:512],
                               func=AF.Square, bias=0.0, scale=1.0),
        reads=["l0_pmr"], writes=["l0_pmr"], tag="l0_pm_k0")
    p.v(lambda e: e.tensor_scalar(
        out=pmr[0:1, 512:768], in0=pmr[0:1, 512:768], scalar1=-KERR,
        scalar2=None, op0=ALU.mult),
        reads=["l0_pmr"], writes=["l0_pmr"], tag="l0_pm_k1")
    p.v(lambda e: e.tensor_scalar(
        out=iwr[:], in0=iwr[:], scalar1=1.0 / 16.0, scalar2=None, op0=ALU.mult),
        reads=["l0_iwr", "fconst"], writes=["l0_iwr"], tag="l0_iw16")
    c8bc = L.sbuf("l0_c8bc", [128, 256], F32)
    nkbc = L.sbuf("l0_nkbc", [128, 256], F32)
    iwbc = L.sbuf("l0_iwbc", [128, 16], F32)
    for nm, srcsl, dst, dk in (("c8", (256, 512), c8bc, "l0_c8bc"),
                               ("nk", (512, 768), nkbc, "l0_nkbc")):
        p.t(lambda e, srcsl=srcsl: e.matmul(
            out=pb[2][:, 0:256], lhsT=L.ones_row[0:1, 0:128],
            rhs=pmr[0:1, srcsl[0]:srcsl[1]], start=True, stop=True),
            reads=["l0_pmr", "ones_row"], writes=["pb2"], tag=f"l0_bc_{nm}")
        p.v(lambda e, dst=dst, dk=dk: e.tensor_copy(out=dst[:], in_=pb[2][:, 0:256]),
            reads=["pb2"], writes=[dk], tag=f"l0_bc_{nm}ev")
    p.t(lambda e: e.matmul(
        out=pb[2][:, 0:16], lhsT=L.ones_row[0:1, 0:128], rhs=iwr[:],
        start=True, stop=True),
        reads=["l0_iwr", "ones_row"], writes=["pb2"], tag="l0_bc_iw")
    p.v(lambda e: e.tensor_copy(out=iwbc[:], in_=pb[2][:, 0:16]),
        reads=["pb2"], writes=["l0_iwbc"], tag="l0_bc_iwev")

    qt = L.sbuf("l0_qt", [128, T], BF16)
    Sx2t = L.sbuf("l0_Sx", [128, 2 * T], BF16)
    t22t = L.sbuf("l0_t2", [128, T], BF16)
    ck4 = L.sbuf("l0_ck4", [128, 4 * T], BF16)
    sk4 = L.sbuf("l0_sk4", [128, 4 * T], BF16)
    abT = L.sbuf("l0_AbT", [128, 2 * T], BF16)
    atb = abT[:, T:2 * T]
    db2 = L.sbuf("l0_db", [128, 128], BF16)
    idb = L.sbuf("l0_idb", [128, 128], BF16)
    mc = L.sbuf("l0_mc", [128, 48], F32)
    p.v(lambda e: e.tensor_copy(out=idb[:], in_=idf[:]), reads=["l0_idf"],
        writes=["l0_idb"], tag="l0_cvid")
    pihalf = L.sbuf("l0_pih", [128, 1], F32)
    p.v(lambda e: e.memset(pihalf[:], PI / 2), writes=["l0_pih"], tag="l0_pih")
    ge_row = stat2[0:1, 0:T]
    for ec in range(NCH):
        wb = ec % 2
        load_slice("wq", ec * 128, 128, f"wq{ec}", wb)
        for ti, (t0, tw) in enumerate(tts):
            pbi = ti % 2
            for a in range(NCH):
                p.t(lambda e, a=a, t0=t0, tw=tw, pbi=pbi, wb=wb: e.matmul(
                    out=pb[pbi][:, 0:tw],
                    lhsT=wmb2[wb][:, a * 128:(a + 1) * 128],
                    rhs=xn[:, a * T + t0:a * T + t0 + tw],
                    start=(a == 0), stop=(a == NCH - 1)),
                    reads=[f"l0_wmb{wb}", f"xn:{a}"], writes=[f"pb{pbi}"],
                    tag="l0_q_mm")
            p.v(lambda e, ec=ec, t0=t0, tw=tw, pbi=pbi: e.tensor_scalar(
                out=qt[:, t0:t0 + tw], in0=pb[pbi][:, 0:tw],
                scalar1=bqk[:, ec:ec + 1], scalar2=None, op0=ALU.add),
                reads=[f"pb{pbi}", "l0_bqk"], writes=["l0_qt"], tag="l0_q_ev")
        wb2 = (ec + 1) % 2
        load_slice("wk", ec * 128, 128, f"wk{ec}", wb2)
        for ti, (t0, tw) in enumerate(tts):
            pbi = 2 + ti % 2
            for a in range(NCH):
                p.t(lambda e, a=a, t0=t0, tw=tw, pbi=pbi, wb2=wb2: e.matmul(
                    out=pb[pbi][:, 0:tw],
                    lhsT=wmb2[wb2][:, a * 128:(a + 1) * 128],
                    rhs=xn[:, a * T + t0:a * T + t0 + tw],
                    start=(a == 0), stop=(a == NCH - 1)),
                    reads=[f"l0_wmb{wb2}", f"xn:{a}"], writes=[f"pb{pbi}"],
                    tag="l0_kc_mm")
            p.v(lambda e, ec=ec, t0=t0, tw=tw, pbi=pbi: e.tensor_scalar(
                out=ktc[:, t0:t0 + tw], in0=pb[pbi][:, 0:tw],
                scalar1=bqk[:, NCH + ec:NCH + ec + 1], scalar2=None, op0=ALU.add),
                reads=[f"pb{pbi}", "l0_bqk"], writes=["l0_ktc"], tag="l0_kc_ev")
        def emit_scores(hh, qc):
            pr = (hh * n_tc + qc) % 2
            Sx = Sx2t[:, pr * T:(pr + 1) * T]
            kS = f"l0_Sx:{pr}"
            mb0 = 24 * pr
            kM0 = f"l0_mc0:{pr}"
            r0 = hh * 64
            for kt4, (k0, kw) in enumerate(tts):
                pbs = 4 + kt4 % 2
                p.t(lambda e, pbs=pbs, k0=k0, kw=kw, qc=qc, r0=r0: e.matmul(
                    out=pb[pbs][:, 0:kw],
                    lhsT=qt[r0:r0 + 64, qc * 128:(qc + 1) * 128],
                    rhs=ktc[r0:r0 + 64, k0:k0 + kw],
                    start=True, stop=True),
                    reads=["l0_qt", "l0_ktc"], writes=[f"pb{pbs}"],
                    tag="l0_sc_mm")
                p.v(lambda e, pbs=pbs, k0=k0, kw=kw, Sx=Sx: e.tensor_copy(
                    out=Sx[:, k0:k0 + kw], in_=pb[pbs][:, 0:kw]),
                    reads=[f"pb{pbs}"], writes=[kS], tag="l0_sc_ev")
            p.v(lambda e, Sx=Sx, mb0=mb0: e.reduce_max(
                out=mc[:, mb0:mb0 + 1], in_=Sx[:], axis=AX.X, negate=True),
                reads=[kS], writes=[kM0], tag="l0_rmax")

        def emit_t2(hh, qc):
            pr = (hh * n_tc + qc) % 2
            Sx = Sx2t[:, pr * T:(pr + 1) * T]
            p.g(lambda e, Sx=Sx: e.tensor_tensor(
                out=t22t[:, 0:T], in0=Sx[:], in1=Sx[:], op=ALU.mult),
                reads=[f"l0_Sx:{pr}"], writes=["l0_t2"], tag="l0_t2")

        iters = [(hh, qc) for hh in range(2) for qc in range(n_tc)]
        emit_scores(*iters[0])
        emit_t2(*iters[0])
        for jj, (hh, qc) in enumerate(iters):
            h = 2 * ec + hh
            r0 = hh * 64
            if True:
                pr = (hh * n_tc + qc) % 2
                Sx = Sx2t[:, pr * T:(pr + 1) * T]
                t2 = t22t[:, 0:T]
                Ab = abT[:, 0:T]
                kAb = "l0_Ab"
                kS, kT2 = f"l0_Sx:{pr}", "l0_t2"
                mb0 = 24 * pr
                kM0 = f"l0_mc0:{pr}"
                for g in range(4):
                    for i in range(4):
                        wl = 4 * g + i
                        j = wl * 16 + h
                        p.s(lambda e, j=j, i=i, t2=t2: e.activation(
                            out=ck4[:, i * T:(i + 1) * T], in_=t2[:],
                            func=AF.Sin, bias=pihalf[:],
                            scale=nkbc[:, j:j + 1]),
                            reads=[kT2, "l0_nkbc", "l0_pih"],
                            writes=[f"l0_ck4:{i}"], tag="l0_ck")
                    for i in range(4):
                        skv = sk4[:, i * T:(i + 1) * T]
                        p.v(lambda e, i=i, skv=skv, Sx=Sx: e.tensor_tensor(
                            out=skv, in0=Sx[:], in1=ck4[:, i * T:(i + 1) * T],
                            op=ALU.mult),
                            reads=[kS, f"l0_ck4:{i}"],
                            writes=[f"l0_sk:{i}"], tag="l0_sk")
                    for i in range(4):
                        wl = 4 * g + i
                        j = wl * 16 + h
                        cb = mb0 + 4 + 4 * i
                        kMi = f"l0_mc:{pr}:{i}"
                        p.v(lambda e, j=j, cb=cb, mb0=mb0: e.tensor_scalar(
                            out=mc[:, cb:cb + 1], in0=mc[:, mb0:mb0 + 1],
                            scalar1=c8bc[:, j:j + 1], scalar2=None, op0=ALU.mult),
                            reads=[kM0, "l0_c8bc"], writes=[kMi],
                            tag="l0_nmj")
                    for i in range(4):
                        wl = 4 * g + i
                        j = wl * 16 + h
                        cb = mb0 + 4 + 4 * i
                        kMi = f"l0_mc:{pr}:{i}"
                        skv = sk4[:, i * T:(i + 1) * T]
                        p.s(lambda e, j=j, cb=cb, skv=skv, i=i: e.activation(
                            out=skv, in_=skv, func=AF.Exp,
                            bias=mc[:, cb:cb + 1], scale=c8bc[:, j:j + 1],
                            accum_out=mc[:, cb + 1:cb + 2]),
                            reads=[f"l0_sk:{i}", kMi, "l0_c8bc"],
                            writes=[f"l0_sk:{i}", kMi], tag="l0_exp")
                        p.v(lambda e, cb=cb: e.reciprocal(
                            out=mc[:, cb + 2:cb + 3], in_=mc[:, cb + 1:cb + 2]),
                            reads=[kMi], writes=[kMi], tag="l0_rz")
                        p.v(lambda e, wl=wl, cb=cb: e.tensor_scalar(
                            out=mc[:, cb + 3:cb + 4], in0=mc[:, cb + 2:cb + 3],
                            scalar1=iwbc[:, wl:wl + 1], scalar2=None,
                            op0=ALU.mult),
                            reads=[kMi, "l0_iwbc"], writes=[kMi], tag="l0_al")
                        dbv = db2[:, 0:128]
                        p.v(lambda e, cb=cb, dbv=dbv: e.tensor_scalar(
                            out=dbv, in0=idb[:], scalar1=mc[:, cb + 3:cb + 4],
                            scalar2=None, op0=ALU.mult),
                            reads=["l0_idb", kMi], writes=["l0_db"],
                            tag="l0_db")
                        for kt4, (k0, kw) in enumerate(tts):
                            p.t(lambda e, kt4=kt4, k0=k0, kw=kw, skv=skv,
                                dbv=dbv, first=(wl == 0), last=(wl == 15): e.matmul(
                                out=pb[kt4][:, 0:kw], lhsT=dbv,
                                rhs=skv[:, k0:k0 + kw],
                                start=first, stop=last),
                                reads=["l0_db", f"l0_sk:{i}"],
                                writes=[f"pb{kt4}"], tag="l0_aacc")
                    if g == 1 and jj + 1 < len(iters):
                        emit_scores(*iters[jj + 1])
                if jj + 1 < len(iters):
                    emit_t2(*iters[jj + 1])
                for kt4, (k0, kw) in enumerate(tts):
                    p.v(lambda e, kt4=kt4, k0=k0, kw=kw: e.tensor_copy(
                        out=Ab[:, k0:k0 + kw], in_=pb[kt4][:, 0:kw]),
                        reads=[f"pb{kt4}"], writes=[kAb],
                        tag="l0_abev")
                for kc in range(n_tc):
                    bk = kc % 2
                    sl = (kc // 2) % 2
                    p.t(lambda e, kc=kc, bk=bk, sl=sl: e.transpose(
                        out=L.pbt2[bk][:, sl * 128:(sl + 1) * 128],
                        in_=Ab[:, kc * 128:(kc + 1) * 128], identity=idb[:]),
                        reads=[kAb, "l0_idb"], writes=[f"pbt{bk}"],
                        tag="l0_atr")
                    p.v(lambda e, kc=kc, bk=bk, sl=sl: e.tensor_copy(
                        out=atb[:, kc * 128:(kc + 1) * 128],
                        in_=L.pbt2[bk][:, sl * 128:(sl + 1) * 128]),
                        reads=[f"pbt{bk}"], writes=["l0_atb"],
                        tag="l0_atr_ev")
                for kc in range(n_tc):
                    p.t(lambda e, kc=kc, h=h, first=(kc == 0),
                        last=(kc == n_tc - 1): e.matmul(
                        out=pb[0][0:64, 0:128],
                        lhsT=vtok[:, kc * 1024 + h * 64:kc * 1024 + h * 64 + 64],
                        rhs=atb[:, kc * 128:(kc + 1) * 128],
                        start=first, stop=last),
                        reads=[f"l0_vt:{kc}", "l0_atb", kAb], writes=["pb0"],
                        tag="l0_av")
                p.v(lambda e, r0=r0, ec=ec, qc=qc: e.tensor_copy(
                    out=yat[r0:r0 + 64, ec * T + qc * 128:ec * T + (qc + 1) * 128],
                    in_=pb[0][0:64, 0:128]),
                    reads=["pb0"], writes=[f"l0_ya:{ec}"], tag="l0_yev")

    _ob0 = L.ln_xs[0]
    for oc in range(NCH):
        wb = oc % 2
        load_slice("wo", oc * 128, 128, f"wo{oc}", wb)
        for ti, (t0, tw) in enumerate(tts):
            pbi = ti % 2
            for a in range(NCH):
                p.t(lambda e, a=a, t0=t0, tw=tw, pbi=pbi, wb=wb: e.matmul(
                    out=pb[pbi][:, 0:tw],
                    lhsT=wmb2[wb][:, a * 128:(a + 1) * 128],
                    rhs=yat[:, a * T + t0:a * T + t0 + tw],
                    start=(a == 0), stop=(a == NCH - 1)),
                    reads=[f"l0_wmb{wb}", f"l0_ya:{a}"], writes=[f"pb{pbi}"],
                    tag="l0_wo_mm")
            p.v(lambda e, oc=oc, tw=tw, pbi=pbi: e.tensor_scalar(
                out=_ob0[:, 0:tw], in0=pb[pbi][:, 0:tw],
                scalar1=bqk[:, 2 * NCH + oc:2 * NCH + oc + 1], scalar2=None,
                op0=ALU.add),
                reads=[f"pb{pbi}", "l0_bqk"], writes=["l0_ob0", "ln_xs0"], tag="l0_wo_ev")
            p.dma("gpsimd", lambda e, oc=oc, t0=t0, tw=tw: e.dma_start(
                out=xout[oc * 128:(oc + 1) * 128, t0:t0 + tw],
                in_=_ob0[:, 0:tw], accum_op=ALU.add),
                reads=["l0_ob0", "ln_xs0"], writes=[f"xout:{oc}"],
                stream=f"l0_xo{oc % 4}", tag="l0_wo_acc")

    b1c = L.sbuf("l0_b1c", [128, 8 * 32], F32)
    b2c = L.sbuf("l0_b2c", [128, 8 * 8], F32)
    p.dma("sync", lambda e: e.dma_start(out=b1c[:], in_=aps["b1c"]),
          writes=["l0_b1c"], tag="l0_ldb1")
    p.dma("sync", lambda e: e.dma_start(out=b2c[:], in_=aps["b2c"]),
          writes=["l0_b2c"], tag="l0_ldb2")
    hid = yat if NCH * T >= 32 * 512 else L.sbuf("l0_hidS", [128, 32 * 512], BF16)
    gw_sb = L.xn_scratch_a[:, 0:512]
    for ex in range(8):
        p.dma("sync", lambda e, ex=ex: e.dma_start(
            out=ge_row[:], in_=g8_hbm[ex:ex + 1, :]),
            reads=["l0_g8hbm"], writes=["l0_ge", "l0_Sx"], stream="l0_ge",
            tag="l0_gedma")
        for ti, (t0, tw) in enumerate(tts):
            for fc in range(32):
                wb = fc % 2
                p.dma("sync", lambda e, ex=ex, fc=fc, wb=wb: e.dma_start(
                    out=wstg2[wb][:].rearrange("p (a f) -> p a f", a=8),
                    in_=aps[f"w1_{ex}"].rearrange("(a p) f -> p a f", p=128)[:, :, fc * 128:(fc + 1) * 128]),
                    writes=[f"l0_wstg{wb}"], tag="l0_ldw1")
                p.v(lambda e, wb=wb: e.tensor_copy(out=wmb2[wb][:, 0:1024],
                                                   in_=wstg2[wb][:]),
                    reads=[f"l0_wstg{wb}"], writes=[f"l0_wmb{wb}"], tag="l0_cvw1")
                pbi = fc % 2
                for a in range(NCH):
                    p.t(lambda e, a=a, t0=t0, tw=tw, pbi=pbi, wb=wb: e.matmul(
                        out=pb[pbi][:, 0:tw],
                        lhsT=wmb2[wb][:, a * 128:(a + 1) * 128],
                        rhs=xn[:, a * T + t0:a * T + t0 + tw],
                        start=(a == 0), stop=(a == NCH - 1)),
                        reads=[f"l0_wmb{wb}", f"xn:{a}"], writes=[f"pb{pbi}"],
                        tag="l0_e1_mm")
                p.s(lambda e, ex=ex, fc=fc, tw=tw, pbi=pbi: e.activation(
                    out=hid[:, fc * 512:fc * 512 + tw], in_=pb[pbi][:, 0:tw],
                    func=AF.Relu, bias=b1c[:, ex * 32 + fc:ex * 32 + fc + 1],
                    scale=1.0),
                    reads=[f"pb{pbi}", "l0_b1c"],
                    writes=["l0_hid"] + [f"l0_ya:{aa}" for aa in range(NCH)],
                    tag="l0_e1_r")
            p.t(lambda e, t0=t0, tw=tw: e.matmul(
                out=pb[2][:, 0:tw], lhsT=L.ones_row[0:1, 0:128],
                rhs=ge_row[0:1, t0:t0 + tw], start=True, stop=True),
                reads=["l0_ge", "ones_row"], writes=["pb2"], tag="l0_gbc")
            p.v(lambda e, tw=tw: e.tensor_copy(out=gw_sb[:, 0:tw],
                                               in_=pb[2][:, 0:tw]),
                reads=["pb2"], writes=["l0_gwsb", "ln_x2"], tag="l0_gbc_ev")
            for oh in range(2):
                for fc in range(32):
                    wb = fc % 2
                    p.dma("sync", lambda e, ex=ex, fc=fc, oh=oh, wb=wb: e.dma_start(
                        out=wstg2[wb][:, 0:512],
                        in_=aps[f"w2_{ex}"][fc * 128:(fc + 1) * 128,
                                            oh * 512:(oh + 1) * 512]),
                        writes=[f"l0_wstg{wb}"], tag="l0_ldw2")
                    p.v(lambda e, wb=wb: e.tensor_copy(out=wmb2[wb][:, 0:512],
                                                       in_=wstg2[wb][:, 0:512]),
                        reads=[f"l0_wstg{wb}"], writes=[f"l0_wmb{wb}"],
                        tag="l0_cvw2")
                    for o4 in range(4):
                        p.t(lambda e, fc=fc, o4=o4, tw=tw, wb=wb: e.matmul(
                            out=pb[2 + o4][:, 0:tw],
                            lhsT=wmb2[wb][:, o4 * 128:(o4 + 1) * 128],
                            rhs=hid[:, fc * 512:fc * 512 + tw],
                            start=(fc == 0), stop=(fc == 31)),
                            reads=[f"l0_wmb{wb}", "l0_hid"], writes=[f"pb{2 + o4}"],
                            tag="l0_e2_mm")
                for o4 in range(4):
                    oc = oh * 4 + o4
                    p.v(lambda e, ex=ex, oc=oc, o4=o4, tw=tw: e.scalar_tensor_tensor(
                        out=_ob0[:, 0:tw], in0=pb[2 + o4][:, 0:tw],
                        scalar=b2c[:, ex * 8 + oc:ex * 8 + oc + 1],
                        in1=gw_sb[:, 0:tw], op0=ALU.add, op1=ALU.mult),
                        reads=[f"pb{2 + o4}", "l0_b2c", "l0_gwsb"],
                        writes=["l0_ob0", "ln_xs0"], tag="l0_e2_ev")
                    p.dma("gpsimd", lambda e, oc=oc, t0=t0, tw=tw: e.dma_start(
                        out=xout[oc * 128:(oc + 1) * 128, t0:t0 + tw],
                        in_=_ob0[:, 0:tw], accum_op=ALU.add),
                        reads=["l0_ob0", "ln_xs0"], writes=[f"xout:{oc}"],
                        stream=f"l0_xo{oc % 4}", tag="l0_e2_acc")
    L.p.wait_all("gpsimd", [f"xout:{a}" for a in range(NCH)], tag="l0_final")


def prep_l0(inputs):
    c8 = lambda v, n=8: np.ascontiguousarray(np.asarray(v).reshape(n, 128).T)
    d = {
        "wq": np.ascontiguousarray(np.asarray(inputs["L0_attn_Wq"])),
        "wk": np.ascontiguousarray(np.asarray(inputs["L0_attn_Wk"])),
        "wv": np.ascontiguousarray(np.asarray(inputs["L0_attn_Wv"])),
        "wo": np.ascontiguousarray(np.asarray(inputs["L0_attn_Wo"])),
        "bq": c8(inputs["L0_attn_bq"]), "bk": c8(inputs["L0_attn_bk"]),
        "bo": c8(inputs["L0_attn_bo"]),
        "vbrow": np.asarray(inputs["L0_attn_bv"]).reshape(1, 1024),
        "pmrow": np.ascontiguousarray(
            np.asarray(inputs["L0_attn_phase_modulators"]).reshape(1, 256)),
        "iwrow": np.asarray(inputs["L0_attn_interference_weights"]).reshape(1, 16),
        "gwt": np.ascontiguousarray(np.asarray(inputs["L0_ffn_gate_w"]).T),
        "b1c": np.ascontiguousarray(
            np.asarray(inputs["L0_ffn_b1"]).reshape(8, 32, 128)
            .transpose(2, 0, 1).reshape(128, 256)),
        "b2c": np.ascontiguousarray(
            np.asarray(inputs["L0_ffn_b2"]).reshape(8, 8, 128)
            .transpose(2, 0, 1).reshape(128, 64)),
        "idf": np.eye(128, dtype=np.float32),
    }
    w1 = np.asarray(inputs["L0_ffn_w1"])
    w2 = np.asarray(inputs["L0_ffn_w2"])
    for ex in range(8):
        d[f"w1_{ex}"] = np.ascontiguousarray(w1[ex])
        d[f"w2_{ex}"] = np.ascontiguousarray(w2[ex])
    return d


L0_DRAM_SPECS = (
    [("wq", [1024, 1024]), ("wk", [1024, 1024]), ("wv", [1024, 1024]),
     ("wo", [1024, 1024]), ("bq", [128, NCH]), ("bk", [128, NCH]),
     ("bo", [128, NCH]), ("vbrow", [1, 1024]), ("pmrow", [1, 256]),
     ("iwrow", [1, 16]), ("gwt", [1024, 8]), ("b1c", [128, 256]),
     ("b2c", [128, 64]), ("idf", [128, 128])]
    + [(f"w1_{ex}", [1024, 4096]) for ex in range(8)]
    + [(f"w2_{ex}", [4096, 1024]) for ex in range(8)]
)



def build_layer_program(layer, T=2048):
    nc = bass.Bass("TRN2", target_bir_lowering=False)
    xin = nc.dram_tensor("xin", [1024, T], F32, kind="ExternalInput").ap()
    fconst = nc.dram_tensor("fconst", [1, NFCONST], F32, kind="ExternalInput").ap()
    xout = nc.dram_tensor("xout", [1024, T], F32, kind="ExternalOutput").ap()
    specs = LAYER_DRAM_SPECS[layer]
    aps = {}
    for name, shape in specs:
        if shape is None:
            shape = [T, NSEG * 128]
        aps[name] = nc.dram_tensor(f"w_{name}", list(shape), F32,
                                   kind="ExternalInput").ap()
    L = LayerCtx(nc, T)
    with L.ctx:
        L.setup_base(fconst)
        LAYER_BUILDERS[layer](L, aps, xin, xout)
        L.p.emit()
        L._stats = L.p.stats()
    return nc, L._stats, L.fconst_values()


def build_fused_program(T=2048):
    nc = bass.Bass("TRN2", target_bir_lowering=False)
    xin = nc.dram_tensor("xin", [1024, T], F32, kind="ExternalInput").ap()
    fconst = nc.dram_tensor("fconst", [1, NFCONST], F32, kind="ExternalInput").ap()
    xout = nc.dram_tensor("xout", [1024, T], F32, kind="ExternalOutput").ap()
    hs = [nc.dram_tensor(f"h{i}", [1024, T], F32, kind="Internal").ap()
          for i in range(3)]
    chain = [xin, hs[0], hs[1], hs[2], xout]
    L = LayerCtx(nc, T)
    with L.ctx:
        L.setup_base(fconst)
        for k in range(4):
            aps = {}
            for name, shape in LAYER_DRAM_SPECS[k]:
                if shape is None:
                    shape = [T, NSEG * 128]
                aps[name] = nc.dram_tensor(f"L{k}w_{name}", list(shape), F32,
                                           kind="ExternalInput").ap()
            inner = contextlib.ExitStack()
            L.ctx = inner
            L.ns = f"F{k}"
            LAYER_BUILDERS[k](L, aps, chain[k], chain[k + 1])
            inner.close()
            L.ctx = L.root_ctx
            L.p.full_barrier()
        L.p.emit()
        L._stats = L.p.stats()
    return nc, L._stats, L.fconst_values()


_PROG_CACHE = {}


def _get_fused(T=2048):
    if ("fused", T) not in _PROG_CACHE:
        _PROG_CACHE[("fused", T)] = build_fused_program(T)
    return _PROG_CACHE[("fused", T)]


def _get_prog(layer, T=2048):
    key = (layer, T)
    if key not in _PROG_CACHE:
        _PROG_CACHE[key] = build_layer_program(layer, T)
    return _PROG_CACHE[key]


LAYER_BUILDERS.update({0: build_l0, 1: build_l1, 2: build_l2, 3: build_l3})
LAYER_DRAM_SPECS.update({0: L0_DRAM_SPECS, 1: L1_DRAM_SPECS, 2: L2_DRAM_SPECS,
                         3: L3_DRAM_SPECS})
PREPS = {0: prep_l0, 1: prep_l1, 2: prep_l2, 3: prep_l3}


def run_layer(layer, x_fm_shards, inputs, T=2048, cores=None):
    nc, stats, fvals = _get_prog(layer, T)
    w = PREPS[layer](inputs)
    if layer == 2:
        if T not in _L2_MASK_CACHE:
            _L2_MASK_CACHE[T] = _l2_mask_host(T)
        w["mask"] = _L2_MASK_CACHE[T]
    in_maps = []
    for xs in x_fm_shards:
        m = {"xin": np.ascontiguousarray(xs, np.float32), "fconst": fvals}
        for k, v in w.items():
            m[f"w_{k}"] = np.ascontiguousarray(v, np.float32)
        in_maps.append(m)
    cores = cores or list(range(len(x_fm_shards)))
    res = run_bass_kernel_spmd(nc, in_maps, core_ids=cores)
    return [r["xout"] for r in res.results]


def kernel_unfused(**inputs):
    x = np.asarray(inputs["L0_attn_x"], np.float32)
    shards = [np.ascontiguousarray(x[i].T) for i in range(NCORES)]
    for layer in (0, 1, 2, 3):
        shards = run_layer(layer, shards, inputs)
    out = np.stack([np.asarray(s, np.float32).T for s in shards], axis=0)
    return np.ascontiguousarray(out)


def kernel(**inputs):
    T = 2048
    x = np.asarray(inputs["L0_attn_x"], np.float32)
    nc, stats, fvals = _get_fused(T)
    base = {"fconst": fvals}
    for k in range(4):
        w = PREPS[k](inputs)
        if k == 2:
            if T not in _L2_MASK_CACHE:
                _L2_MASK_CACHE[T] = _l2_mask_host(T)
            w["mask"] = _L2_MASK_CACHE[T]
        for name, v in w.items():
            base[f"L{k}w_{name}"] = np.ascontiguousarray(v, np.float32)
    in_maps = []
    for i in range(NCORES):
        m = dict(base)
        m["xin"] = np.ascontiguousarray(x[i].T)
        in_maps.append(m)
    res = run_bass_kernel_spmd(nc, in_maps, core_ids=list(range(NCORES)))
    out = np.stack([np.asarray(r["xout"], np.float32).T for r in res.results],
                   axis=0)
    return np.ascontiguousarray(out)
